# version 14
# baseline (speedup 1.0000x reference)
import math
from collections import Counter
from contextlib import ExitStack

import numpy as np
import ml_dtypes
import concourse.bass as bass
import concourse.mybir as mybir
from concourse.bass_utils import run_bass_kernel_spmd

F32 = mybir.dt.float32
BF16 = mybir.dt.bfloat16
U8 = mybir.dt.uint8
AF = mybir.ActivationFunctionType
ALU = mybir.AluOpType
AX = mybir.AxisListType

B, T, D, HID, H, DH, J = 8, 2048, 1024, 512, 8, 64, 4096
NT, KD, NHC, NJ = 16, 8, 4, 32

C23 = float(np.float32(1.5 * 2.0 ** 23))
IL2_HI = float(np.float32(1.0 / math.log(2.0)))
IL2_LO = float(np.float32(1.0 / math.log(2.0) - np.float32(1.0 / math.log(2.0))))
L2_HI = float(np.float32(math.log(2.0)))
L2_LO = float(np.float32(math.log(2.0) - np.float32(math.log(2.0))))
C52 = 1.5 * 2.0 ** 52
LN2_64 = float(math.log(2.0))
EPS40 = float(np.float32(2.0 ** -40))
T40 = float(np.float32(2.0 ** 40))
LN2 = float(np.float32(math.log(2.0)))
INVLN2 = float(np.float32(1.0 / math.log(2.0)))
FLOOR_DELTA = 4e-8
ATTN_SCALE = 0.125
LN_EPS = 1e-5
KB = 1024
F32SZ = 44 * KB
BFSZ = 160 * KB

ENGINES = ("sync", "vector", "scalar", "tensor", "gpsimd")
ESEM = {"sync": "d2", "vector": "v", "scalar": "s", "tensor": "t",
        "gpsimd": "g"}


class Plan:
    def __init__(self):
        self.steps = {e: [] for e in ENGINES}
        self.cnt = Counter()
        self.sems = {}
        self.dma_sems = set()

    def op(self, eng, f, inc=None):
        if inc is not None:
            self.cnt[inc[0]] += inc[1]

        def run(E, f=f, inc=inc, sems=self.sems):
            r = f(E)
            if inc is not None:
                r.then_inc(sems[inc[0]], inc[1])
        self.steps[eng].append(run)

    def wait(self, eng, sem, val=None):
        if val is None:
            val = self.cnt[sem]
        if val <= 0:
            return
        self.steps[eng].append(
            lambda E, s=sem, v=val, sems=self.sems: E.wait_ge(sems[s], v))

    def tick(self, eng, sem):
        self.cnt[sem] += 1

        def run(E, s=sem, sems=self.sems):
            E.drain()
            E.sem_inc(sems[s], 1)
        self.steps[eng].append(run)
        return self.cnt[sem]

    def drain(self, eng):
        self.steps[eng].append(lambda E: E.drain())

    def barrier(self):
        vals = {}
        for e in ("vector", "scalar", "tensor", "gpsimd"):
            vals[ESEM[e]] = self.tick(e, ESEM[e])
        for ds in sorted(self.dma_sems):
            self.wait("sync", ds, self.cnt[ds])
        vals["d2"] = self.tick("sync", "d2")
        for e in ENGINES:
            for s, v in vals.items():
                if s != ESEM[e]:
                    self.wait(e, s, v)


def build_nc():
    nc = bass.Bass("TRN2", target_bir_lowering=False, debug=False)
    p = Plan()

    x_h = nc.dram_tensor("x", [T, D], F32, kind="ExternalInput").ap()
    wq_h = nc.dram_tensor("wq", [D, HID], F32, kind="ExternalInput").ap()
    wk_h = nc.dram_tensor("wk", [D, HID], F32, kind="ExternalInput").ap()
    wv_h = nc.dram_tensor("wv", [D, HID], F32, kind="ExternalInput").ap()
    wo8_h = nc.dram_tensor("wo8", [DH, H * D], F32, kind="ExternalInput").ap()
    bo_h = nc.dram_tensor("bo", [1, D], F32, kind="ExternalInput").ap()
    wfc_h = nc.dram_tensor("wfc", [J, D], F32, kind="ExternalInput").ap()
    bfc_h = nc.dram_tensor("bfc", [1, J], F32, kind="ExternalInput").ap()
    wpj_h = nc.dram_tensor("wpj", [D, J], F32, kind="ExternalInput").ap()
    bpj_h = nc.dram_tensor("bpj", [1, D], F32, kind="ExternalInput").ap()
    ident_h = nc.dram_tensor("ident", [128, 128], F32, kind="ExternalInput").ap()
    consts_h = nc.dram_tensor("consts", [128, 8], F32, kind="ExternalInput").ap()
    out_h = nc.dram_tensor("out", [T, D], F32, kind="ExternalOutput").ap()
    x2s_h = nc.dram_tensor("x2s", [T, D], F32, kind="Internal").ap()

    es = ExitStack()
    arF = es.enter_context(nc.sbuf_tensor("arF", [128, F32SZ // 4], F32))
    arB = es.enter_context(nc.sbuf_tensor("arB", [128, BFSZ // 2], BF16))

    def V(off, parts, n, dtype=F32):
        isz = mybir.dt.size(dtype)
        nb = n * isz
        assert off % 64 == 0, off
        if dtype == F32:
            assert off + nb <= F32SZ, (off, nb)
            return arF[0:parts, off // 4:off // 4 + n]
        assert off + nb <= BFSZ, (off, nb)
        return arB[0:parts, off // 2:off // 2 + n]

    consts = V(0, 128, 8)
    det = V(64, 128, 1)
    det1m = V(128, 128, 1)
    K1 = V(192, 128, 1)
    ident = V(256, 128, 128)
    q_mx = V(768, 128, 8)
    q_sq = V(832, 128, 8)
    q_si = V(896, 128, 8)
    k_mx = V(960, 128, 1)
    k_nmx = V(1024, 128, 1)
    k_sk = V(1088, 128, 1)
    k_ski = V(1152, 128, 1)
    mv6 = V(1216, 128, 12)
    mv = V(1280, 128, 2)
    rstd = V(1344, 128, 1)
    negmr = V(1408, 128, 1)
    am = V(1472, 128, 128)
    am2 = V(1984, 128, 128)
    am3 = V(2496, 128, 128)
    sc_t = V(3008, 128, 128)
    iv_t = V(3520, 128, 128)
    K1S = V(4032, 128, 1)
    bo_bf = V(0, 1, D, BF16)
    bpj_bf = V(2048, 1, D, BF16)
    bfc_bf = V(4096, 1, J, BF16)
    ones1 = V(12288, 1, 128, BF16)
    c_sb = V(12608, 64, 512, BF16)

    pA = es.enter_context(nc.psum_tensor("pA", [128, 2048], F32))
    pB = es.enter_context(nc.psum_tensor("pB", [128, 1024], F32))
    pC = es.enter_context(nc.psum_tensor("pC", [128, 1024], F32))

    DMA_SEMS = ("c0", "bst", "st0", "st1", "st2", "xa0", "xa1",
                "xb0", "xb1", "ox0", "ox1", "sw0", "sw1", "sw2", "sw3",
                "x20", "x21", "ou0", "ou1")
    for s in ("d2", "v", "s", "t", "g") + DMA_SEMS:
        p.sems[s] = es.enter_context(nc.semaphore(s))
    p.dma_sems.update(DMA_SEMS)

    def dma(dst, src, sem):
        p.op("sync", lambda E, d=dst, s=src: E.dma_start(out=d, in_=s),
             inc=(sem, 16))
        return p.cnt[sem]

    def gdma(dst, src, sem):
        p.op("gpsimd", lambda E, d=dst, s=src: E.dma_start(out=d, in_=s),
             inc=(sem, 16))
        return p.cnt[sem]

    def vop(f, drain=True):
        p.op("vector", f)
        if drain:
            p.drain("vector")

    smalls_free = [0]

    def ln_stats(src_a, src_b):
        vop(lambda E, a=src_a: E.bn_stats(out=mv6[:, 0:6], in_=a), drain=False)
        vop(lambda E, b2=src_b: E.bn_stats(out=mv6[:, 6:12], in_=b2))
        vop(lambda E: E.bn_aggr(out=mv, in_=mv6), drain=False)
        vt = p.tick("vector", "v")
        p.wait("scalar", "v", vt)
        p.op("scalar", lambda E: E.activation(out=rstd, in_=mv[:, 1:2],
                                              func=AF.Ln, bias=consts[:, 4:5]))
        p.drain("scalar")
        p.op("scalar", lambda E: E.activation(out=rstd, in_=rstd, func=AF.Exp,
                                              scale=consts[:, 5:6]),
             inc=("s", 1))
        p.wait("vector", "s", p.cnt["s"])
        if smalls_free[0]:
            p.wait("vector", "s", smalls_free[0])
        vop(lambda E: E.tensor_scalar(out=negmr, in0=mv[:, 0:1], scalar1=rstd,
                                      scalar2=-1.0, op0=ALU.mult,
                                      op1=ALU.mult), drain=False)
        return p.tick("vector", "v")

    def normalize(dst, src):
        p.op("scalar", lambda E, d=dst, s=src: E.activation(
            out=d, in_=s, func=AF.Identity, bias=negmr, scale=rstd),
            inc=("s", 1))
        smalls_free[0] = p.cnt["s"]
        return p.cnt["s"]

    def qs_a(nb, lo):
        vop(lambda E, nb=nb, lo=lo: E.tensor_scalar(
            out=am[:, lo:lo + nb], in0=am[:, lo:lo + nb], scalar1=2.0 ** -16,
            scalar2=None, op0=ALU.max), drain=False)
        vt = p.tick("vector", "v")
        p.wait("scalar", "v", vt)
        p.drain("scalar")
        p.op("scalar", lambda E, nb=nb, lo=lo: E.activation(
            out=am2[:, 0:nb], in_=am[:, lo:lo + nb], func=AF.Ln),
            inc=("s", 1))
        return p.cnt["s"]

    def qs_b(nb, lo, s_ln):
        p.wait("vector", "s", s_ln)
        vop(lambda E, nb=nb: E.tensor_scalar(out=am3[:, 0:nb],
                                             in0=am2[:, 0:nb],
                                             scalar1=consts[:, 6:7],
                                             scalar2=consts[:, 7:8],
                                             op0=ALU.mult, op1=ALU.add))
        vop(lambda E, nb=nb: E.tensor_scalar(out=am2[:, 0:nb],
                                             in0=am2[:, 0:nb],
                                             scalar1=IL2_HI, scalar2=3.5,
                                             op0=ALU.mult, op1=ALU.subtract))
        vop(lambda E, nb=nb: E.tensor_tensor(out=am2[:, 0:nb],
                                             in0=am2[:, 0:nb],
                                             in1=am3[:, 0:nb], op=ALU.add))
        vop(lambda E, nb=nb: E.tensor_scalar(out=am2[:, 0:nb],
                                             in0=am2[:, 0:nb], scalar1=K1,
                                             scalar2=K1, op0=ALU.add,
                                             op1=ALU.subtract))
        vop(lambda E, nb=nb: E.tensor_scalar(out=am2[:, 0:nb],
                                             in0=am2[:, 0:nb],
                                             scalar1=LN2_64, scalar2=None,
                                             op0=ALU.mult), drain=False)
        vt = p.tick("vector", "v")
        p.wait("scalar", "v", vt)
        p.op("scalar", lambda E, nb=nb, lo=lo: E.activation(
            out=sc_t[:, lo:lo + nb], in_=am2[:, 0:nb], func=AF.Exp),
            inc=("s", 1))
        return p.cnt["s"]

    def qs_c(nb, lo, s_exp):
        p.wait("vector", "s", s_exp)
        vop(lambda E, nb=nb, lo=lo: E.tensor_scalar(
            out=am2[:, 0:nb], in0=sc_t[:, lo:lo + nb], scalar1=K1S,
            scalar2=None, op0=ALU.mult))
        vop(lambda E, nb=nb, lo=lo: E.tensor_tensor(
            out=am3[:, 0:nb], in0=sc_t[:, lo:lo + nb], in1=am2[:, 0:nb],
            op=ALU.add))
        vop(lambda E, nb=nb, lo=lo: E.tensor_tensor(
            out=sc_t[:, lo:lo + nb], in0=am3[:, 0:nb], in1=am2[:, 0:nb],
            op=ALU.subtract))
        vop(lambda E, nb=nb, lo=lo: E.reciprocal(
            out=iv_t[:, lo:lo + nb], in_=sc_t[:, lo:lo + nb]), drain=False)
        return p.tick("vector", "v")

    def quant_scales(nb, lo=0, syncv=True):
        s_ln = qs_a(nb, lo)
        s_exp = qs_b(nb, lo, s_ln)
        vt2 = qs_c(nb, lo, s_exp)
        return ("v", vt2)

    def quant_chain(src3, nb, sc_lo, wbuf):
        w3 = wbuf.rearrange("p (b k) -> p b k", k=32)
        iv_b = iv_t[:, sc_lo:sc_lo + nb].unsqueeze(2).broadcast_to([128, nb, 32])
        sc_b = sc_t[:, sc_lo:sc_lo + nb].unsqueeze(2).broadcast_to([128, nb, 32])
        vop(lambda E, w3=w3, src3=src3, iv_b=iv_b: E.tensor_tensor(
            out=w3, in0=src3, in1=iv_b, op=ALU.mult))
        vop(lambda E, wbuf=wbuf: E.tensor_scalar(
            out=wbuf, in0=wbuf, scalar1=K1, scalar2=K1, op0=ALU.add,
            op1=ALU.subtract))
        vop(lambda E, w3=w3, sc_b=sc_b: E.tensor_tensor(
            out=w3, in0=w3, in1=sc_b, op=ALU.mult), drain=False)

    wq_sb = V(14 * KB, 128, KD * HID, BF16)
    wk_sb = V(22 * KB, 128, KD * HID, BF16)
    wv_sb = V(30 * KB, 128, KD * HID, BF16)
    eqtbA = V(38 * KB, 64, 4 * T, BF16)
    uT = V(54 * KB, 128, KD * T, BF16)
    eqtbB = V(86 * KB, 64, 4 * T, BF16)
    vb = V(102 * KB, 128, NT * HID, BF16)
    ektb = V(118 * KB, 128, NT * HID, BF16)
    wo_sb = V(144 * KB, 64, H * D, BF16)
    aob = V(102 * KB, 64, H * T, BF16)
    xa = [V(4 * KB + i * 4 * KB, 128, 1024) for i in range(2)]
    ut = [V(12 * KB + i * 4 * KB, 128, 1024) for i in range(2)]
    eqs = [V(20 * KB + i * 2 * KB, 128, 512) for i in range(2)]
    bias_stg = V(24 * KB, 1, J)
    ekw = V(4 * KB, 128, 2048)
    xb = [V(12 * KB + i * 4 * KB, 128, 1024) for i in range(2)]
    ox = [V(20 * KB + i * 4 * KB, 128, 1024) for i in range(2)]
    stg4 = [V(28 * KB + i * 4 * KB, 128, 1024) for i in range(2)]

    dma(consts, consts_h, "c0")
    d0 = dma(ident, ident_h, "c0")
    p.wait("vector", "c0", d0)
    vop(lambda E: E.memset(det, 1.0))
    vop(lambda E: E.tensor_scalar(out=det, in0=det, scalar1=EPS40, scalar2=1.0,
                                  op0=ALU.add, op1=ALU.subtract))
    vop(lambda E: E.tensor_scalar(out=det, in0=det, scalar1=T40, scalar2=None,
                                  op0=ALU.mult))
    vop(lambda E: E.tensor_scalar(out=det1m, in0=det, scalar1=-1.0, scalar2=1.0,
                                  op0=ALU.mult, op1=ALU.add))
    vop(lambda E: E.tensor_scalar(out=K1, in0=det1m, scalar1=C23, scalar2=None,
                                  op0=ALU.mult))
    vop(lambda E: E.scalar_tensor_tensor(out=K1, in0=det, scalar=C52, in1=K1,
                                         op0=ALU.mult, op1=ALU.add))
    vop(lambda E: E.tensor_scalar(out=K1S, in0=det1m, scalar1=C23,
                                  scalar2=None, op0=ALU.mult))
    vop(lambda E: E.memset(ones1, 1.0), drain=False)
    p.tick("vector", "v")

    def load_cast(dst_bf, src_h, parts, n):
        dv = dma(bias_stg[0:parts, 0:n], src_h, "bst")
        p.wait("scalar", "bst", dv)
        p.op("scalar", lambda E, d=dst_bf, s=bias_stg[0:parts, 0:n]:
             E.activation(out=d, in_=s, func=AF.Copy), inc=("s", 1))
        p.wait("sync", "s", p.cnt["s"])

    load_cast(bo_bf, bo_h, 1, D)
    load_cast(bpj_bf, bpj_h, 1, D)
    load_cast(bfc_bf, bfc_h, 1, J)

    si = [0]
    stg_free = [0, 0]
    bias_done = p.cnt["s"]
    p.wait("gpsimd", "s", bias_done)

    def stage_cast(src_ap, dst_bf, parts, n):
        b = si[0] % 2
        si[0] += 1
        if stg_free[b]:
            p.wait("gpsimd", "s", stg_free[b])
        dv = gdma(stg4[b][0:parts, 0:n], src_ap, "st%d" % b)
        p.wait("scalar", "st%d" % b, dv)
        p.op("scalar", lambda E, d=dst_bf, s=stg4[b][0:parts, 0:n]:
             E.activation(out=d, in_=s, func=AF.Copy), inc=("s", 1))
        stg_free[b] = p.cnt["s"]

    CAST_ITEMS = []
    for k in range(KD):
        CAST_ITEMS.append((wq_h[128 * k:128 * k + 128, :],
                           wq_sb[:, HID * k:HID * (k + 1)], 128, HID))
        CAST_ITEMS.append((wv_h[128 * k:128 * k + 128, :],
                           wv_sb[:, HID * k:HID * (k + 1)], 128, HID))
    for k in range(KD):
        CAST_ITEMS.append((wk_h[128 * k:128 * k + 128, :],
                           wk_sb[:, HID * k:HID * (k + 1)], 128, HID))
    for hh in range(H):
        CAST_ITEMS.append((wo8_h[:, D * hh:D * (hh + 1)],
                           wo_sb[0:64, D * hh:D * (hh + 1)], 64, 1024))
    qv_cast_s = [0]

    SH = 8
    xa_free = [0, 0]
    ut_free = [0, 0]
    pA_lo_free_s = [0]
    pA_free_list = []
    pq_free_s = [0]
    eqs_free_t = [0, 0]
    eqt_evac_s = [0]

    def ln_part(c):
        b = c % 2
        if xa_free[b]:
            p.wait("sync", "s", xa_free[b])
        dv = dma(xa[b], x_h[128 * c:128 * c + 128, :], "xa%d" % b)
        p.wait("vector", "xa%d" % b, dv)
        sv = ln_stats(xa[b][:, 0:512], xa[b][:, 512:1024])
        p.wait("scalar", "v", sv)
        if ut_free[b]:
            p.wait("scalar", "t", ut_free[b])
        xa_free[b] = normalize(ut[b], xa[b])
        p.wait("tensor", "s", p.cnt["s"])
        if pA_lo_free_s[0]:
            p.wait("tensor", "s", pA_lo_free_s[0])
        for k in range(KD):
            p.op("tensor", lambda E, k=k, b=b:
                 E.transpose(pA[:, 128 * k:128 * k + 128],
                             ut[b][:, 128 * k:128 * k + 128], ident),
                 inc=("t", 1) if k == KD - 1 else None)
        ut_free[b] = p.cnt["t"]
        p.wait("scalar", "t", p.cnt["t"])
        dst = uT[:, :].rearrange("p (k n) -> p k n", n=T)[:, :,
                                                          128 * c:128 * c + 128]
        p.op("scalar", lambda E, d=dst: E.activation(
            out=d, in_=pA[:, 0:1024].rearrange("p (k n) -> p k n", n=128),
            func=AF.Copy), inc=("s", 1))
        pA_lo_free_s[0] = p.cnt["s"]
        pA_free_list.append(p.cnt["s"])

    def qv_part(j):
        b = j % 2
        p.wait("tensor", "s", qv_cast_s[0])
        p.wait("tensor", "s", pA_free_list[j])
        if pq_free_s[0]:
            p.wait("tensor", "s", pq_free_s[0])
        for k in range(KD):
            lhs = uT[:, :].rearrange("p (k n) -> p k n", n=T)[
                :, k, 128 * j:128 * j + 128]
            p.op("tensor", lambda E, l=lhs, k=k:
                 E.matmul(pC[:, 0:512], l, wq_sb[:, HID * k:HID * (k + 1)],
                          start=(k == 0), stop=(k == KD - 1)))
            p.op("tensor", lambda E, l=lhs, k=k:
                 E.matmul(pC[:, 512:1024], l, wv_sb[:, HID * k:HID * (k + 1)],
                          start=(k == 0), stop=(k == KD - 1)),
                 inc=("t", 1) if k == KD - 1 else None)
        tq = p.cnt["t"]
        p.wait("scalar", "t", tq)
        p.op("scalar", lambda E, j=j:
             E.activation(out=vb[:, HID * j:HID * (j + 1)],
                          in_=pC[:, 512:1024], func=AF.Copy), inc=("s", 1))
        if eqs_free_t[b]:
            p.wait("scalar", "t", eqs_free_t[b])
        p.op("scalar", lambda E, b=b: E.activation(
            out=eqs[b], in_=pC[:, 0:512], func=AF.Exp), inc=("s", 1))
        pq_free_s[0] = p.cnt["s"]
        p.wait("vector", "s", pq_free_s[0])
        e3 = eqs[b].rearrange("p (h e) -> p h e", e=64)
        vop(lambda E, e3=e3: E.tensor_reduce(out=q_sq, in_=e3, axis=AX.X,
                                             op=ALU.add))
        vop(lambda E: E.reciprocal(out=q_si, in_=q_sq))
        vop(lambda E: E.tensor_scalar(out=q_si, in0=q_si, scalar1=ATTN_SCALE,
                                      scalar2=None, op0=ALU.mult))
        sib = q_si.unsqueeze(2).broadcast_to([128, 8, 64])
        vop(lambda E, e3=e3, sib=sib: E.tensor_tensor(
            out=e3, in0=e3, in1=sib, op=ALU.mult), drain=False)
        vq_t = p.tick("vector", "v")
        p.wait("tensor", "v", vq_t)
        if eqt_evac_s[0]:
            p.wait("tensor", "s", eqt_evac_s[0])
        for hh in range(H):
            src = eqs[b][:, 64 * hh:64 * (hh + 1)]
            p.op("tensor", lambda E, s=src, hh=hh:
                 E.transpose(pA[0:64, 1024 + 128 * hh:1024 + 128 * (hh + 1)],
                             s, ident),
                 inc=("t", 1) if hh == H - 1 else None)
        eqs_free_t[b] = p.cnt["t"]
        p.wait("scalar", "t", p.cnt["t"])
        dstA = eqtbA.rearrange("p (h n) -> p h n", n=T)[:, :,
                                                        128 * j:128 * j + 128]
        dstB = eqtbB.rearrange("p (h n) -> p h n", n=T)[:, :,
                                                        128 * j:128 * j + 128]
        p.op("scalar", lambda E, d=dstA: E.activation(
            out=d, in_=pA[0:64, 1024:1536].rearrange("p (h n) -> p h n",
                                                     n=128),
            func=AF.Copy), inc=("s", 1))
        p.op("scalar", lambda E, d=dstB: E.activation(
            out=d, in_=pA[0:64, 1536:2048].rearrange("p (h n) -> p h n",
                                                     n=128),
            func=AF.Copy), inc=("s", 1))
        eqt_evac_s[0] = p.cnt["s"]

    ci = [0]

    def emit_casts(n_):
        for _ in range(n_):
            if ci[0] < len(CAST_ITEMS):
                stage_cast(*CAST_ITEMS[ci[0]])
                ci[0] += 1
                if ci[0] == 16:
                    qv_cast_s[0] = p.cnt["s"]

    for c in range(NT):
        emit_casts(2)
        ln_part(c)
        if c >= SH:
            qv_part(c - SH)
    for j in range(NT - SH, NT):
        qv_part(j)

    pk_free_s = 0
    ekw_free_t = [0]
    ek_pbc_free_s = [0]
    for hc in range(NHC):
        if hc == 0:
            p.wait("tensor", "s", eqt_evac_s[0])
            p.wait("tensor", "s", pq_free_s[0])
        if pk_free_s:
            p.wait("tensor", "s", pk_free_s)
        for k in range(KD):
            lhs = wk_sb[:, HID * k + 128 * hc:HID * k + 128 * (hc + 1)]
            for ns in range(4):
                mov = uT[:, :].rearrange("p (k n) -> p k n", n=T)[
                    :, k, 512 * ns:512 * (ns + 1)]
                p.op("tensor", lambda E, l=lhs, m=mov, ns=ns, k=k:
                     E.matmul(pA[:, 512 * ns:512 * (ns + 1)], l, m,
                              start=(k == 0), stop=(k == KD - 1)),
                     inc=("t", 1) if (k == KD - 1 and ns == 3) else None)
        tk = p.cnt["t"]
        p.wait("scalar", "t", tk)
        if ekw_free_t[0]:
            p.wait("scalar", "t", ekw_free_t[0])
        p.op("scalar", lambda E: E.activation(
            out=ekw[:, 0:1024], in_=pA[:, 0:1024], func=AF.Exp,
            accum_out=k_sk))
        p.drain("scalar")
        p.op("scalar", lambda E: E.activation(
            out=ekw[:, 1024:2048], in_=pA[:, 1024:2048],
            func=AF.Exp, accum_out=k_ski), inc=("s", 1))
        pk_free_s = p.cnt["s"]
        p.wait("vector", "s", p.cnt["s"])
        vop(lambda E: E.tensor_tensor(out=k_sk, in0=k_sk, in1=k_ski,
                                      op=ALU.add))
        vop(lambda E: E.reciprocal(out=k_ski, in_=k_sk))
        vop(lambda E: E.tensor_scalar(
            out=ekw[:, 0:2048], in0=ekw[:, 0:2048],
            scalar1=k_ski, scalar2=None, op0=ALU.mult), drain=False)
        ekvt = p.tick("vector", "v")
        p.wait("tensor", "v", ekvt)
        if ek_pbc_free_s[0]:
            p.wait("tensor", "s", ek_pbc_free_s[0])
        for i in range(8):
            p.op("tensor", lambda E, i=i:
                 E.transpose(pB[:, 128 * i:128 * (i + 1)],
                             ekw[:, 128 * i:128 * (i + 1)], ident))
        for i in range(8):
            p.op("tensor", lambda E, i=i:
                 E.transpose(pC[:, 128 * i:128 * (i + 1)],
                             ekw[:, 1024 + 128 * i:1024 + 128 * (i + 1)],
                             ident),
                 inc=("t", 1) if i == 7 else None)
        ekw_free_t[0] = p.cnt["t"]
        p.wait("scalar", "t", p.cnt["t"])
        d1 = ektb.rearrange("p (c x) -> p c x", x=HID)[
            :, 0:8, 128 * hc:128 * (hc + 1)]
        d2 = ektb.rearrange("p (c x) -> p c x", x=HID)[
            :, 8:16, 128 * hc:128 * (hc + 1)]
        p.op("scalar", lambda E, d=d1: E.activation(
            out=d, in_=pB[:, 0:1024].rearrange("p (i n) -> p i n", n=128),
            func=AF.Copy), inc=("s", 1))
        p.op("scalar", lambda E, d=d2: E.activation(
            out=d, in_=pC[:, 0:1024].rearrange("p (i n) -> p i n", n=128),
            func=AF.Copy), inc=("s", 1))
        ek_pbc_free_s[0] = p.cnt["s"]

    p.barrier()

    for hh in range(H):
        for c in range(NT):
            lhs = ektb[:, HID * c + 64 * hh:HID * c + 64 * (hh + 1)]
            mov = vb[:, HID * c + 64 * hh:HID * c + 64 * (hh + 1)]
            p.op("tensor", lambda E, l=lhs, m=mov, c=c, hh=hh:
                 E.matmul(pA[0:64, 64 * hh:64 * (hh + 1)], l, m,
                          start=(c == 0), stop=(c == NT - 1)),
                 inc=("t", 1) if (c == NT - 1 and hh == H - 1) else None)
    p.wait("scalar", "t", p.cnt["t"])
    p.op("scalar", lambda E: E.activation(out=c_sb, in_=pA[0:64, 0:512],
                                          func=AF.Copy), inc=("s", 1))
    p.wait("tensor", "s", p.cnt["s"])
    po_free = [0, 0]
    pi = 0
    for hh in range(H):
        for ns in range(4):
            if po_free[pi % 2]:
                p.wait("tensor", "s", po_free[pi % 2])
            lhs = c_sb[0:64, 64 * hh:64 * (hh + 1)]
            arr = eqtbA if hh < 4 else eqtbB
            mov = arr[0:64, T * (hh % 4) + 512 * ns:
                      T * (hh % 4) + 512 * (ns + 1)]
            reg = pB[0:64, 512 * (pi % 2):512 * (pi % 2) + 512]
            p.op("tensor", lambda E, l=lhs, m=mov, r=reg:
                 E.matmul(r, l, m, start=True, stop=True), inc=("t", 1))
            p.wait("scalar", "t", p.cnt["t"])
            p.op("scalar", lambda E, hh=hh, ns=ns, r=reg: E.activation(
                out=aob[0:64, T * hh + 512 * ns:T * hh + 512 * (ns + 1)],
                in_=r, func=AF.Copy), inc=("s", 1))
            po_free[pi % 2] = p.cnt["s"]
            pi += 1

    p.wait("tensor", "s", p.cnt["s"])
    p.wait("sync", "t", p.cnt["t"])
    ox_free = [0, 0]
    xb_free = [0, 0]
    pwo_free = 0
    for c in range(NT):
        b = c % 2
        if xb_free[b]:
            p.wait("sync", "v", xb_free[b])
        dv = dma(xb[b], x_h[128 * c:128 * c + 128, :], "xb%d" % b)
        if pwo_free:
            p.wait("tensor", "s", pwo_free)
        for hh in range(H):
            lhs = aob[0:64, T * hh + 128 * c:T * hh + 128 * (c + 1)]
            for kk in range(2):
                p.op("tensor", lambda E, l=lhs, hh=hh, kk=kk:
                     E.matmul(pC[:, 512 * kk:512 * (kk + 1)], l,
                              wo_sb[0:64, D * hh + 512 * kk:
                                    D * hh + 512 * (kk + 1)],
                              start=(hh == 0), stop=False))
        for kk in range(2):
            p.op("tensor", lambda E, kk=kk:
                 E.matmul(pC[:, 512 * kk:512 * (kk + 1)], ones1,
                          bo_bf[0:1, 512 * kk:512 * (kk + 1)],
                          start=False, stop=True),
                 inc=("t", 1) if kk == 1 else None)
        two = p.cnt["t"]
        p.wait("vector", "t", two)
        sv = ln_stats(pC[:, 0:512], pC[:, 512:1024])
        p.wait("scalar", "v", sv)
        if ox_free[b]:
            p.wait("scalar", "ox%d" % b, ox_free[b])
        pwo_free = normalize(ox[b], pC[:, 0:1024])
        p.wait("vector", "s", p.cnt["s"])
        p.wait("vector", "xb%d" % b, dv)
        vop(lambda E, b=b: E.tensor_tensor(out=ox[b], in0=ox[b], in1=xb[b],
                                           op=ALU.add), drain=False)
        vt = p.tick("vector", "v")
        xb_free[b] = vt
        p.wait("sync", "v", vt)
        ox_free[b] = dma(x2s_h[128 * c:128 * c + 128, :], ox[b], "ox%d" % b)

    p.barrier()

    stgW = [V(4 * KB + 8 * KB * i, 128, 2048) for i in range(4)]
    wfcq = V(14 * KB, 128, KD * J, BF16)
    wpjq = V(78 * KB, 128, NJ * D, BF16)

    stgW_free = [0, 0, 0, 0]
    pB_free2 = [None]

    WORK = [(wfc_h[128 * jc:128 * (jc + 1), :], 1024, 32,
             [wfcq[:, J * k + 128 * jc:J * k + 128 * (jc + 1)]
              for k in range(KD)]) for jc in range(NJ)]
    WORK += [(wpj_h[128 * (idx // 2):128 * (idx // 2) + 128,
                    2048 * (idx % 2):2048 * (idx % 2 + 1)], 2048, 64,
              [wpjq[:, D * (16 * (idx % 2) + t) + 128 * (idx // 2):
                    D * (16 * (idx % 2) + t) + 128 * (idx // 2) + 128]
               for t in range(16)]) for idx in range(16)]

    def w_load(i):
        src_h_ap, n, nb, _ = WORK[i]
        b = i % 4
        if stgW_free[b]:
            p.wait("gpsimd", "t", stgW_free[b])
        gdma(stgW[b][:, 0:n], src_h_ap, "sw%d" % b)

    def w_amax_pair(pair):
        tot = 0
        for j in (2 * pair, 2 * pair + 1):
            _, n, nb, _ = WORK[j]
            b = j % 4
            p.wait("vector", "sw%d" % b, p.cnt["sw%d" % b])
            s3 = stgW[b][:, 0:n].rearrange("p (b k) -> p b k", k=32)
            vop(lambda E, s3=s3, nb=nb, lo=tot: E.tensor_reduce(
                out=am[:, lo:lo + nb], in_=s3, axis=AX.X, op=ALU.max,
                apply_absolute_value=True), drain=(j % 2 == 1))
            tot += nb
        return tot, qs_a(tot, 0)

    def w_chain(i, lo):
        _, n, nb, dst_tiles = WORK[i]
        b = i % 4
        s3 = stgW[b][:, 0:n].rearrange("p (b k) -> p b k", k=32)
        quant_chain(s3, nb, lo, stgW[b][:, 0:n])
        return p.tick("vector", "v")

    def w_te(i, vq):
        _, n, nb, dst_tiles = WORK[i]
        b = i % 4
        ntile = n // 128
        for base in range(0, ntile, 8):
            cnt = min(8, ntile - base)
            p.wait("tensor", "v", vq)
            if pB_free2[0]:
                p.wait("tensor", pB_free2[0][0], pB_free2[0][1])
            for i2 in range(cnt):
                src2 = stgW[b][:, 128 * (base + i2):128 * (base + i2 + 1)]
                p.op("tensor", lambda E, s=src2, i=i2:
                     E.transpose(pB[:, 128 * i:128 * (i + 1)], s, ident),
                     inc=("t", 1) if i2 == cnt - 1 else None)
            stgW_free[b] = p.cnt["t"]
            p.wait("scalar", "t", p.cnt["t"])
            for i2 in range(cnt):
                p.op("scalar", lambda E, d=dst_tiles[base + i2], i=i2:
                     E.activation(out=d, in_=pB[:, 128 * i:128 * (i + 1)],
                                  func=AF.Copy),
                     inc=("s", 1) if i2 == cnt - 1 else None)
            pB_free2[0] = ("s", p.cnt["s"])

    NP = len(WORK) // 2
    for j in range(4):
        w_load(j)
    tot0, sln0 = w_amax_pair(0)
    sexp0 = qs_b(tot0, 0, sln0)
    qs_c(tot0, 0, sexp0)
    for P in range(NP):
        i0, i1 = 2 * P, 2 * P + 1
        nxt = P + 1 < NP
        if nxt:
            tot1, sln1 = w_amax_pair(P + 1)
        vq0 = w_chain(i0, 0)
        vq1 = w_chain(i1, WORK[i0][2])
        if nxt:
            sexp1 = qs_b(tot1, 0, sln1)
        w_te(i0, vq0)
        if 2 * P + 4 < len(WORK):
            w_load(2 * P + 4)
        if nxt:
            qs_c(tot1, 0, sexp1)
        w_te(i1, vq1)
        if 2 * P + 5 < len(WORK):
            w_load(2 * P + 5)

    p.barrier()

    x2t = [V(4 * KB, 128, 1024), V(8 * KB, 128, 1024)]
    w1 = V(12 * KB, 128, 2048)
    w2 = V(20 * KB, 128, 2048)
    u2f = V(28 * KB, 128, 1024)
    u2qt = V(142 * KB, 128, 1024, BF16)
    tmpn = V(12 * KB, 128, 1024)
    gqt = V(144 * KB, 128, NJ * 128, BF16)

    x2t_free = [0, 0]
    u2_t_free = [0]
    u2qt_sfree = [0]
    w1_t_free = [0]
    w2_t_free = [0]
    gqt_read_t = [0]
    gqt_s_evac = [0]
    fcA_ok_v = [0]
    fcB_ok_s = [0]
    pCfree_v = [0]

    def u2path(c):
        b = c % 2
        if x2t_free[b]:
            p.wait("sync", "ou%d" % b, x2t_free[b])
        dv = dma(x2t[b], x2s_h[128 * c:128 * c + 128, :], "x2%d" % b)
        p.wait("vector", "x2%d" % b, dv)
        sv = ln_stats(x2t[b][:, 0:512], x2t[b][:, 512:1024])
        p.wait("scalar", "v", sv)
        if u2_t_free[0]:
            p.wait("scalar", "t", u2_t_free[0])
        normalize(u2f, x2t[b])
        p.wait("vector", "s", p.cnt["s"])
        u3 = u2f[:, :].rearrange("p (b k) -> p b k", k=32)
        vop(lambda E, u3=u3: E.tensor_reduce(out=am[:, 0:32], in_=u3,
                                             axis=AX.X, op=ALU.max,
                                             apply_absolute_value=True))
        quant_scales(32)
        quant_chain(u3, 32, 0, u2f)
        return p.tick("vector", "v")

    def transp_u2(c, u2v):
        p.wait("tensor", "v", u2v)
        if fcA_ok_v[0]:
            p.wait("tensor", "v", fcA_ok_v[0])
        if gqt_s_evac[0]:
            p.wait("tensor", "s", gqt_s_evac[0])
        for k in range(KD):
            p.op("tensor", lambda E, k=k:
                 E.transpose(pA[:, 128 * k:128 * k + 128],
                             u2f[:, 128 * k:128 * k + 128], ident),
                 inc=("t", 1) if k == KD - 1 else None)
        u2_t_free[0] = p.cnt["t"]
        p.wait("scalar", "t", p.cnt["t"])
        p.op("scalar", lambda E: E.activation(
            out=u2qt, in_=pA[:, 0:1024], func=AF.Copy), inc=("s", 1))
        u2qt_sfree[0] = p.cnt["s"]

    uv0 = u2path(0)
    transp_u2(0, uv0)

    for c in range(NT):
        b = c % 2
        p.wait("tensor", "s", u2qt_sfree[0])
        for g in range(2):
            pz = pA[:, 1024 * g:1024 * (g + 1)]
            for k in range(KD):
                lhs = u2qt[:, 128 * k:128 * (k + 1)]
                for js in range(2):
                    mov = wfcq[:, J * k + 1024 * g + 512 * js:
                               J * k + 1024 * g + 512 * (js + 1)]
                    p.op("tensor", lambda E, l=lhs, m=mov, pz=pz, js=js, k=k:
                         E.matmul(pz[:, 512 * js:512 * (js + 1)], l, m,
                                  start=(k == 0), stop=False))
            for js in range(2):
                mov = bfc_bf[0:1, 1024 * g + 512 * js:1024 * g + 512 * (js + 1)]
                p.op("tensor", lambda E, m=mov, pz=pz, js=js:
                     E.matmul(pz[:, 512 * js:512 * (js + 1)], ones1, m,
                              start=False, stop=True),
                     inc=("t", 1) if js == 1 else None)
        tfca = p.cnt["t"]
        if fcB_ok_s[0]:
            p.wait("tensor", "s", fcB_ok_s[0])
        if pCfree_v[0]:
            p.wait("tensor", "v", pCfree_v[0])
        for g in range(2):
            pz = (pB if g == 0 else pC)[:, 0:1024]
            for k in range(KD):
                lhs = u2qt[:, 128 * k:128 * (k + 1)]
                for js in range(2):
                    mov = wfcq[:, J * k + 2048 + 1024 * g + 512 * js:
                               J * k + 2048 + 1024 * g + 512 * (js + 1)]
                    p.op("tensor", lambda E, l=lhs, m=mov, pz=pz, js=js, k=k:
                         E.matmul(pz[:, 512 * js:512 * (js + 1)], l, m,
                                  start=(k == 0), stop=False))
            for js in range(2):
                mov = bfc_bf[0:1, 2048 + 1024 * g + 512 * js:
                             2048 + 1024 * g + 512 * (js + 1)]
                p.op("tensor", lambda E, m=mov, pz=pz, js=js:
                     E.matmul(pz[:, 512 * js:512 * (js + 1)], ones1, m,
                              start=False, stop=True),
                     inc=("t", 1) if js == 1 else None)
        tfcb = p.cnt["t"]
        p.wait("scalar", "t", tfca)
        p.op("scalar", lambda E: E.activation(out=pA[:, 0:2048],
                                              in_=pA[:, 0:2048], func=AF.Gelu),
             inc=("s", 1))
        sga = p.cnt["s"]
        p.wait("scalar", "t", tfcb)
        p.op("scalar", lambda E: E.activation(out=pB[:, 0:1024],
                                              in_=pB[:, 0:1024], func=AF.Gelu),
             inc=("s", 1))
        sgb = p.cnt["s"]
        p.op("scalar", lambda E: E.activation(out=pC[:, 0:1024],
                                              in_=pC[:, 0:1024], func=AF.Gelu),
             inc=("s", 1))
        sgc = p.cnt["s"]
        p.wait("vector", "s", sga)
        vop(lambda E: E.tensor_reduce(
            out=am[:, 0:64],
            in_=pA[:, 0:2048].rearrange("p (b k) -> p b k", k=32),
            axis=AX.X, op=ALU.max, apply_absolute_value=True), drain=False)
        p.wait("vector", "s", sgb)
        vop(lambda E: E.tensor_reduce(
            out=am[:, 64:96],
            in_=pB[:, 0:1024].rearrange("p (b k) -> p b k", k=32),
            axis=AX.X, op=ALU.max, apply_absolute_value=True), drain=False)
        p.wait("vector", "s", sgc)
        vop(lambda E: E.tensor_reduce(
            out=am[:, 96:128],
            in_=pC[:, 0:1024].rearrange("p (b k) -> p b k", k=32),
            axis=AX.X, op=ALU.max, apply_absolute_value=True))
        quant_scales(128)
        if w1_t_free[0]:
            p.wait("vector", "t", w1_t_free[0])
        vop(lambda E: E.tensor_tensor(
            out=w1[:, 0:2048].rearrange("p (b k) -> p b k", k=32),
            in0=pA[:, 0:2048].rearrange("p (b k) -> p b k", k=32),
            in1=iv_t[:, 0:64].unsqueeze(2).broadcast_to([128, 64, 32]),
            op=ALU.mult), drain=False)
        vra = p.tick("vector", "v")
        fcA_ok_v[0] = vra
        vop(lambda E: E.tensor_scalar(out=w1[:, 0:2048], in0=w1[:, 0:2048],
                                      scalar1=K1, scalar2=K1, op0=ALU.add,
                                      op1=ALU.subtract))
        vop(lambda E: E.tensor_tensor(
            out=w1[:, 0:2048].rearrange("p (b k) -> p b k", k=32),
            in0=w1[:, 0:2048].rearrange("p (b k) -> p b k", k=32),
            in1=sc_t[:, 0:64].unsqueeze(2).broadcast_to([128, 64, 32]),
            op=ALU.mult), drain=False)
        vqa = p.tick("vector", "v")
        p.wait("tensor", "v", vqa)
        for i in range(16):
            p.op("tensor", lambda E, i=i:
                 E.transpose(pA[:, 128 * i:128 * (i + 1)],
                             w1[:, 128 * i:128 * (i + 1)], ident),
                 inc=("t", 1) if i == 15 else None)
        w1_t_free[0] = p.cnt["t"]
        p.wait("scalar", "t", p.cnt["t"])
        if gqt_read_t[0]:
            p.wait("scalar", "t", gqt_read_t[0])
        p.op("scalar", lambda E: E.activation(out=gqt[:, 0:2048],
                                              in_=pA[:, 0:2048],
                                              func=AF.Copy), inc=("s", 1))
        evacA_s = p.cnt["s"]
        if w2_t_free[0]:
            p.wait("vector", "t", w2_t_free[0])
        vop(lambda E: E.tensor_tensor(
            out=w2[:, 0:1024].rearrange("p (b k) -> p b k", k=32),
            in0=pB[:, 0:1024].rearrange("p (b k) -> p b k", k=32),
            in1=iv_t[:, 64:96].unsqueeze(2).broadcast_to([128, 32, 32]),
            op=ALU.mult), drain=False)
        vop(lambda E: E.tensor_tensor(
            out=w2[:, 1024:2048].rearrange("p (b k) -> p b k", k=32),
            in0=pC[:, 0:1024].rearrange("p (b k) -> p b k", k=32),
            in1=iv_t[:, 96:128].unsqueeze(2).broadcast_to([128, 32, 32]),
            op=ALU.mult), drain=False)
        vrb = p.tick("vector", "v")
        pCfree_v[0] = vrb
        vop(lambda E: E.tensor_scalar(out=w2[:, 0:2048], in0=w2[:, 0:2048],
                                      scalar1=K1, scalar2=K1, op0=ALU.add,
                                      op1=ALU.subtract))
        vop(lambda E: E.tensor_tensor(
            out=w2[:, 0:2048].rearrange("p (b k) -> p b k", k=32),
            in0=w2[:, 0:2048].rearrange("p (b k) -> p b k", k=32),
            in1=sc_t[:, 64:128].unsqueeze(2).broadcast_to([128, 64, 32]),
            op=ALU.mult), drain=False)
        vqb = p.tick("vector", "v")
        p.wait("tensor", "s", evacA_s)
        p.wait("tensor", "v", pCfree_v[0])
        for jc in range(NJ // 2):
            lhs = gqt[:, 128 * jc:128 * (jc + 1)]
            for kk in range(2):
                mov = wpjq[:, D * jc + 512 * kk:D * jc + 512 * (kk + 1)]
                p.op("tensor", lambda E, l=lhs, m=mov, kk=kk, jc=jc:
                     E.matmul(pC[:, 512 * kk:512 * (kk + 1)], l, m,
                              start=(jc == 0), stop=False))
        p.wait("tensor", "v", vqb)
        p.wait("tensor", "s", evacA_s)
        for i in range(16):
            p.op("tensor", lambda E, i=i:
                 E.transpose(pA[:, 128 * i:128 * (i + 1)],
                             w2[:, 128 * i:128 * (i + 1)], ident),
                 inc=("t", 1) if i == 15 else None)
        w2_t_free[0] = p.cnt["t"]
        p.wait("scalar", "t", p.cnt["t"])
        p.op("scalar", lambda E: E.activation(out=gqt[:, 2048:4096],
                                              in_=pA[:, 0:2048],
                                              func=AF.Copy), inc=("s", 1))
        gqt_s_evac[0] = p.cnt["s"]
        if c + 1 < NT:
            uv = u2path(c + 1)
            transp_u2(c + 1, uv)
        p.wait("tensor", "s", gqt_s_evac[0])
        for jc in range(NJ // 2, NJ):
            lhs = gqt[:, 128 * jc:128 * (jc + 1)]
            for kk in range(2):
                mov = wpjq[:, D * jc + 512 * kk:D * jc + 512 * (kk + 1)]
                p.op("tensor", lambda E, l=lhs, m=mov, kk=kk, jc=jc:
                     E.matmul(pC[:, 512 * kk:512 * (kk + 1)], l, m,
                              start=False, stop=False))
        for kk in range(2):
            p.op("tensor", lambda E, kk=kk:
                 E.matmul(pC[:, 512 * kk:512 * (kk + 1)], ones1,
                          bpj_bf[0:1, 512 * kk:512 * (kk + 1)],
                          start=False, stop=True),
                 inc=("t", 1) if kk == 1 else None)
        gqt_read_t[0] = p.cnt["t"]
        tpj = p.cnt["t"]
        p.wait("vector", "t", tpj)
        sv = ln_stats(pC[:, 0:512], pC[:, 512:1024])
        p.wait("scalar", "v", sv)
        p.wait("scalar", "t", w1_t_free[0])
        fcB_ok_s[0] = normalize(tmpn, pC[:, 0:1024])
        p.wait("vector", "s", p.cnt["s"])
        vop(lambda E, b=b: E.tensor_tensor(out=x2t[b], in0=tmpn, in1=x2t[b],
                                           op=ALU.add), drain=False)
        vo = p.tick("vector", "v")
        p.wait("sync", "v", vo)
        x2t_free[b] = dma(out_h[128 * c:128 * c + 128, :], x2t[b],
                          "ou%d" % b)

    p.wait("sync", "ou0", p.cnt["ou0"])
    p.wait("sync", "ou1", p.cnt["ou1"])

    with nc.Block() as blk:
        @blk.sync
        def _(E):
            for f in p.steps["sync"]:
                f(E)

        @blk.vector
        def _(E):
            for f in p.steps["vector"]:
                f(E)

        @blk.scalar
        def _(E):
            for f in p.steps["scalar"]:
                f(E)

        @blk.tensor
        def _(E):
            for f in p.steps["tensor"]:
                f(E)

        @blk.gpsimd
        def _(E):
            for f in p.steps["gpsimd"]:
                f(E)

    es.close()
    return nc


_NC_CACHE = None


def kernel(L0_attn_x, L0_attn_Wq, L0_attn_Wk, L0_attn_Wv, L0_attn_Wo,
           L0_attn_bo, L0_ffn_w_fc, L0_ffn_b_fc, L0_ffn_w_proj, L0_ffn_b_proj):
    global _NC_CACHE
    x = np.ascontiguousarray(np.asarray(L0_attn_x, dtype=np.float32))
    wq = np.ascontiguousarray(np.asarray(L0_attn_Wq, dtype=np.float32))
    wk = np.ascontiguousarray(np.asarray(L0_attn_Wk, dtype=np.float32))
    wv = np.ascontiguousarray(np.asarray(L0_attn_Wv, dtype=np.float32))
    wo = np.asarray(L0_attn_Wo, dtype=np.float32)
    bo = np.ascontiguousarray(
        np.asarray(L0_attn_bo, dtype=np.float32).reshape(1, D))
    wfc = np.ascontiguousarray(np.asarray(L0_ffn_w_fc, dtype=np.float32))
    bfc = np.ascontiguousarray(
        np.asarray(L0_ffn_b_fc, dtype=np.float32).reshape(1, J))
    wpj = np.ascontiguousarray(np.asarray(L0_ffn_w_proj, dtype=np.float32))
    bpj = np.ascontiguousarray(
        np.asarray(L0_ffn_b_proj, dtype=np.float32).reshape(1, D))
    wo8 = np.ascontiguousarray(
        wo.reshape(H, DH, D).transpose(1, 0, 2).reshape(DH, H * D))
    ident = np.eye(128, dtype=np.float32)
    consts = np.ascontiguousarray(np.tile(np.array(
        [LN2, -3.0 * LN2, -LN2, 3.0 * LN2, LN_EPS, -0.5, IL2_LO, FLOOR_DELTA],
        dtype=np.float32), (128, 1)))

    if _NC_CACHE is None:
        _NC_CACHE = build_nc()
    nc = _NC_CACHE

    in_maps = []
    for i in range(B):
        in_maps.append(dict(
            x=np.ascontiguousarray(x[i]), wq=wq, wk=wk, wv=wv, wo8=wo8, bo=bo,
            wfc=wfc, bfc=bfc, wpj=wpj, bpj=bpj, ident=ident, consts=consts))
    res = run_bass_kernel_spmd(nc, in_maps, core_ids=list(range(B)))
    out = np.stack([np.asarray(res.results[i]["out"]) for i in range(B)],
                   axis=0)
    return out


# version 15
# speedup vs baseline: 1.0626x; 1.0626x over previous
import math
from collections import Counter
from contextlib import ExitStack

import numpy as np
import ml_dtypes
import concourse.bass as bass
import concourse.mybir as mybir
from concourse.bass_utils import run_bass_kernel_spmd

F32 = mybir.dt.float32
BF16 = mybir.dt.bfloat16
U8 = mybir.dt.uint8
AF = mybir.ActivationFunctionType
ALU = mybir.AluOpType
AX = mybir.AxisListType

B, T, D, HID, H, DH, J = 8, 2048, 1024, 512, 8, 64, 4096
NT, KD, NHC, NJ = 16, 8, 4, 32

C23 = float(np.float32(1.5 * 2.0 ** 23))
IL2_HI = float(np.float32(1.0 / math.log(2.0)))
IL2_LO = float(np.float32(1.0 / math.log(2.0) - np.float32(1.0 / math.log(2.0))))
L2_HI = float(np.float32(math.log(2.0)))
L2_LO = float(np.float32(math.log(2.0) - np.float32(math.log(2.0))))
C52 = 1.5 * 2.0 ** 52
LN2_64 = float(math.log(2.0))
EPS40 = float(np.float32(2.0 ** -40))
T40 = float(np.float32(2.0 ** 40))
LN2 = float(np.float32(math.log(2.0)))
INVLN2 = float(np.float32(1.0 / math.log(2.0)))
FLOOR_DELTA = 4e-8
ATTN_SCALE = 0.125
LN_EPS = 1e-5
KB = 1024
F32SZ = 44 * KB
BFSZ = 160 * KB

ENGINES = ("sync", "vector", "scalar", "tensor", "gpsimd")
ESEM = {"sync": "d2", "vector": "v", "scalar": "s", "tensor": "t",
        "gpsimd": "g"}


class Plan:
    def __init__(self):
        self.steps = {e: [] for e in ENGINES}
        self.cnt = Counter()
        self.sems = {}
        self.dma_sems = set()

    def op(self, eng, f, inc=None):
        if inc is not None:
            self.cnt[inc[0]] += inc[1]

        def run(E, f=f, inc=inc, sems=self.sems):
            r = f(E)
            if inc is not None:
                r.then_inc(sems[inc[0]], inc[1])
        self.steps[eng].append(run)

    def wait(self, eng, sem, val=None):
        if val is None:
            val = self.cnt[sem]
        if val <= 0:
            return
        self.steps[eng].append(
            lambda E, s=sem, v=val, sems=self.sems: E.wait_ge(sems[s], v))

    def tick(self, eng, sem):
        self.cnt[sem] += 1

        def run(E, s=sem, sems=self.sems):
            E.drain()
            E.sem_inc(sems[s], 1)
        self.steps[eng].append(run)
        return self.cnt[sem]

    def drain(self, eng):
        self.steps[eng].append(lambda E: E.drain())

    def barrier(self):
        vals = {}
        for e in ("vector", "scalar", "tensor", "gpsimd"):
            vals[ESEM[e]] = self.tick(e, ESEM[e])
        for ds in sorted(self.dma_sems):
            self.wait("sync", ds, self.cnt[ds])
        vals["d2"] = self.tick("sync", "d2")
        for e in ENGINES:
            for s, v in vals.items():
                if s != ESEM[e]:
                    self.wait(e, s, v)


def build_nc():
    nc = bass.Bass("TRN2", target_bir_lowering=False, debug=False)
    p = Plan()

    x_h = nc.dram_tensor("x", [T, D], F32, kind="ExternalInput").ap()
    wq_h = nc.dram_tensor("wq", [D, HID], F32, kind="ExternalInput").ap()
    wk_h = nc.dram_tensor("wk", [D, HID], F32, kind="ExternalInput").ap()
    wv_h = nc.dram_tensor("wv", [D, HID], F32, kind="ExternalInput").ap()
    wo8_h = nc.dram_tensor("wo8", [DH, H * D], F32, kind="ExternalInput").ap()
    bo_h = nc.dram_tensor("bo", [1, D], F32, kind="ExternalInput").ap()
    wfc_h = nc.dram_tensor("wfc", [J, D], F32, kind="ExternalInput").ap()
    bfc_h = nc.dram_tensor("bfc", [1, J], F32, kind="ExternalInput").ap()
    wpj_h = nc.dram_tensor("wpj", [D, J], F32, kind="ExternalInput").ap()
    bpj_h = nc.dram_tensor("bpj", [1, D], F32, kind="ExternalInput").ap()
    ident_h = nc.dram_tensor("ident", [128, 128], F32, kind="ExternalInput").ap()
    consts_h = nc.dram_tensor("consts", [128, 8], F32, kind="ExternalInput").ap()
    out_h = nc.dram_tensor("out", [T, D], F32, kind="ExternalOutput").ap()
    x2s_h = nc.dram_tensor("x2s", [T, D], F32, kind="Internal").ap()

    es = ExitStack()
    arF = es.enter_context(nc.sbuf_tensor("arF", [128, F32SZ // 4], F32))
    arB = es.enter_context(nc.sbuf_tensor("arB", [128, BFSZ // 2], BF16))

    def V(off, parts, n, dtype=F32):
        isz = mybir.dt.size(dtype)
        nb = n * isz
        assert off % 64 == 0, off
        if dtype == F32:
            assert off + nb <= F32SZ, (off, nb)
            return arF[0:parts, off // 4:off // 4 + n]
        assert off + nb <= BFSZ, (off, nb)
        return arB[0:parts, off // 2:off // 2 + n]

    consts = V(0, 128, 8)
    det = V(64, 128, 1)
    det1m = V(128, 128, 1)
    K1 = V(192, 128, 1)
    ident = V(256, 128, 128)
    q_mx = V(768, 128, 8)
    q_sq = V(832, 128, 8)
    q_si = V(896, 128, 8)
    k_mx = V(960, 128, 1)
    k_nmx = V(1024, 128, 1)
    k_sk = V(1088, 128, 1)
    k_ski = V(1152, 128, 1)
    mv6 = V(1216, 128, 12)
    mv = V(1280, 128, 2)
    rstd = V(1344, 128, 1)
    negmr = V(1408, 128, 1)
    am = V(1472, 128, 128)
    am2 = V(1984, 128, 128)
    am3 = V(2496, 128, 128)
    sc_t = V(3008, 128, 128)
    iv_t = V(3520, 128, 128)
    K1S = V(4032, 128, 1)
    bo_bf = V(0, 1, D, BF16)
    bpj_bf = V(2048, 1, D, BF16)
    bfc_bf = V(4096, 1, J, BF16)
    ones1 = V(12288, 1, 128, BF16)
    c_sb = V(12608, 64, 512, BF16)

    pA = es.enter_context(nc.psum_tensor("pA", [128, 2048], F32))
    pB = es.enter_context(nc.psum_tensor("pB", [128, 1024], F32))
    pC = es.enter_context(nc.psum_tensor("pC", [128, 1024], F32))

    DMA_SEMS = ("c0", "bst", "st0", "st1", "st2", "xa0", "xa1",
                "xb0", "xb1", "ox0", "ox1", "sw0", "sw1", "sw2", "sw3", "sw4",
                "x20", "x21", "ou0", "ou1")
    for s in ("d2", "v", "s", "t", "g") + DMA_SEMS:
        p.sems[s] = es.enter_context(nc.semaphore(s))
    p.dma_sems.update(DMA_SEMS)

    def dma(dst, src, sem):
        p.op("sync", lambda E, d=dst, s=src: E.dma_start(out=d, in_=s),
             inc=(sem, 16))
        return p.cnt[sem]

    def gdma(dst, src, sem):
        p.op("gpsimd", lambda E, d=dst, s=src: E.dma_start(out=d, in_=s),
             inc=(sem, 16))
        return p.cnt[sem]

    def vop(f, drain=True):
        p.op("vector", f)
        if drain:
            p.drain("vector")

    smalls_free = [0]

    def ln_stats(src_a, src_b):
        vop(lambda E, a=src_a: E.bn_stats(out=mv6[:, 0:6], in_=a), drain=False)
        vop(lambda E, b2=src_b: E.bn_stats(out=mv6[:, 6:12], in_=b2))
        vop(lambda E: E.bn_aggr(out=mv, in_=mv6), drain=False)
        vt = p.tick("vector", "v")
        p.wait("scalar", "v", vt)
        p.op("scalar", lambda E: E.activation(out=rstd, in_=mv[:, 1:2],
                                              func=AF.Ln, bias=consts[:, 4:5]))
        p.drain("scalar")
        p.op("scalar", lambda E: E.activation(out=rstd, in_=rstd, func=AF.Exp,
                                              scale=consts[:, 5:6]),
             inc=("s", 1))
        p.wait("vector", "s", p.cnt["s"])
        if smalls_free[0]:
            p.wait("vector", "s", smalls_free[0])
        vop(lambda E: E.tensor_scalar(out=negmr, in0=mv[:, 0:1], scalar1=rstd,
                                      scalar2=-1.0, op0=ALU.mult,
                                      op1=ALU.mult), drain=False)
        return p.tick("vector", "v")

    def normalize(dst, src):
        p.op("scalar", lambda E, d=dst, s=src: E.activation(
            out=d, in_=s, func=AF.Identity, bias=negmr, scale=rstd),
            inc=("s", 1))
        smalls_free[0] = p.cnt["s"]
        return p.cnt["s"]

    def qs_a(nb, lo):
        vop(lambda E, nb=nb, lo=lo: E.tensor_scalar(
            out=am[:, lo:lo + nb], in0=am[:, lo:lo + nb], scalar1=2.0 ** -16,
            scalar2=None, op0=ALU.max), drain=False)
        vt = p.tick("vector", "v")
        p.wait("scalar", "v", vt)
        p.drain("scalar")
        p.op("scalar", lambda E, nb=nb, lo=lo: E.activation(
            out=am2[:, 0:nb], in_=am[:, lo:lo + nb], func=AF.Ln),
            inc=("s", 1))
        return p.cnt["s"]

    def qs_b(nb, lo, s_ln):
        p.wait("vector", "s", s_ln)
        vop(lambda E, nb=nb: E.tensor_scalar(out=am3[:, 0:nb],
                                             in0=am2[:, 0:nb],
                                             scalar1=consts[:, 6:7],
                                             scalar2=consts[:, 7:8],
                                             op0=ALU.mult, op1=ALU.add))
        vop(lambda E, nb=nb: E.tensor_scalar(out=am2[:, 0:nb],
                                             in0=am2[:, 0:nb],
                                             scalar1=IL2_HI, scalar2=3.5,
                                             op0=ALU.mult, op1=ALU.subtract))
        vop(lambda E, nb=nb: E.tensor_tensor(out=am2[:, 0:nb],
                                             in0=am2[:, 0:nb],
                                             in1=am3[:, 0:nb], op=ALU.add))
        vop(lambda E, nb=nb: E.tensor_scalar(out=am2[:, 0:nb],
                                             in0=am2[:, 0:nb], scalar1=K1,
                                             scalar2=K1, op0=ALU.add,
                                             op1=ALU.subtract))
        vop(lambda E, nb=nb: E.tensor_scalar(out=am2[:, 0:nb],
                                             in0=am2[:, 0:nb],
                                             scalar1=LN2_64, scalar2=None,
                                             op0=ALU.mult), drain=False)
        vt = p.tick("vector", "v")
        p.wait("scalar", "v", vt)
        p.op("scalar", lambda E, nb=nb, lo=lo: E.activation(
            out=sc_t[:, lo:lo + nb], in_=am2[:, 0:nb], func=AF.Exp),
            inc=("s", 1))
        return p.cnt["s"]

    def qs_c(nb, lo, s_exp):
        p.wait("vector", "s", s_exp)
        vop(lambda E, nb=nb, lo=lo: E.tensor_scalar(
            out=am2[:, 0:nb], in0=sc_t[:, lo:lo + nb], scalar1=K1S,
            scalar2=None, op0=ALU.mult))
        vop(lambda E, nb=nb, lo=lo: E.tensor_tensor(
            out=am3[:, 0:nb], in0=sc_t[:, lo:lo + nb], in1=am2[:, 0:nb],
            op=ALU.add))
        vop(lambda E, nb=nb, lo=lo: E.tensor_tensor(
            out=sc_t[:, lo:lo + nb], in0=am3[:, 0:nb], in1=am2[:, 0:nb],
            op=ALU.subtract))
        vop(lambda E, nb=nb, lo=lo: E.reciprocal(
            out=iv_t[:, lo:lo + nb], in_=sc_t[:, lo:lo + nb]), drain=False)
        return p.tick("vector", "v")

    def quant_scales(nb, lo=0, syncv=True):
        s_ln = qs_a(nb, lo)
        s_exp = qs_b(nb, lo, s_ln)
        vt2 = qs_c(nb, lo, s_exp)
        return ("v", vt2)

    def quant_chain(src3, nb, sc_lo, wbuf):
        w3 = wbuf.rearrange("p (b k) -> p b k", k=32)
        iv_b = iv_t[:, sc_lo:sc_lo + nb].unsqueeze(2).broadcast_to([128, nb, 32])
        sc_b = sc_t[:, sc_lo:sc_lo + nb].unsqueeze(2).broadcast_to([128, nb, 32])
        vop(lambda E, w3=w3, src3=src3, iv_b=iv_b: E.tensor_tensor(
            out=w3, in0=src3, in1=iv_b, op=ALU.mult))
        vop(lambda E, wbuf=wbuf: E.tensor_scalar(
            out=wbuf, in0=wbuf, scalar1=K1, scalar2=K1, op0=ALU.add,
            op1=ALU.subtract))
        vop(lambda E, w3=w3, sc_b=sc_b: E.tensor_tensor(
            out=w3, in0=w3, in1=sc_b, op=ALU.mult), drain=False)

    wq_sb = V(14 * KB, 128, KD * HID, BF16)
    wk_sb = V(22 * KB, 128, KD * HID, BF16)
    wv_sb = V(30 * KB, 128, KD * HID, BF16)
    eqtbA = V(38 * KB, 64, 4 * T, BF16)
    uT = V(54 * KB, 128, KD * T, BF16)
    eqtbB = V(86 * KB, 64, 4 * T, BF16)
    vb = V(102 * KB, 128, NT * HID, BF16)
    ektb = V(118 * KB, 128, NT * HID, BF16)
    wo_sb = V(144 * KB, 64, H * D, BF16)
    aob = V(102 * KB, 64, H * T, BF16)
    xa = [V(4 * KB + i * 4 * KB, 128, 1024) for i in range(2)]
    ut = [V(12 * KB + i * 4 * KB, 128, 1024) for i in range(2)]
    eqs = [V(20 * KB + i * 2 * KB, 128, 512) for i in range(2)]
    bias_stg = V(24 * KB, 1, J)
    ekw = V(4 * KB, 128, 2048)
    xb = [V(12 * KB + i * 4 * KB, 128, 1024) for i in range(2)]
    ox = [V(20 * KB + i * 4 * KB, 128, 1024) for i in range(2)]
    stg4 = [V(28 * KB + i * 4 * KB, 128, 1024) for i in range(2)]

    dma(consts, consts_h, "c0")
    d0 = dma(ident, ident_h, "c0")
    p.wait("vector", "c0", d0)
    vop(lambda E: E.memset(det, 1.0))
    vop(lambda E: E.tensor_scalar(out=det, in0=det, scalar1=EPS40, scalar2=1.0,
                                  op0=ALU.add, op1=ALU.subtract))
    vop(lambda E: E.tensor_scalar(out=det, in0=det, scalar1=T40, scalar2=None,
                                  op0=ALU.mult))
    vop(lambda E: E.tensor_scalar(out=det1m, in0=det, scalar1=-1.0, scalar2=1.0,
                                  op0=ALU.mult, op1=ALU.add))
    vop(lambda E: E.tensor_scalar(out=K1, in0=det1m, scalar1=C23, scalar2=None,
                                  op0=ALU.mult))
    vop(lambda E: E.scalar_tensor_tensor(out=K1, in0=det, scalar=C52, in1=K1,
                                         op0=ALU.mult, op1=ALU.add))
    vop(lambda E: E.tensor_scalar(out=K1S, in0=det1m, scalar1=C23,
                                  scalar2=None, op0=ALU.mult))
    vop(lambda E: E.memset(ones1, 1.0), drain=False)
    p.tick("vector", "v")

    def load_cast(dst_bf, src_h, parts, n):
        dv = dma(bias_stg[0:parts, 0:n], src_h, "bst")
        p.wait("scalar", "bst", dv)
        p.op("scalar", lambda E, d=dst_bf, s=bias_stg[0:parts, 0:n]:
             E.activation(out=d, in_=s, func=AF.Copy), inc=("s", 1))
        p.wait("sync", "s", p.cnt["s"])

    load_cast(bo_bf, bo_h, 1, D)
    load_cast(bpj_bf, bpj_h, 1, D)
    load_cast(bfc_bf, bfc_h, 1, J)

    si = [0]
    stg_free = [0, 0]
    bias_done = p.cnt["s"]
    p.wait("gpsimd", "s", bias_done)

    def stage_cast(src_ap, dst_bf, parts, n):
        b = si[0] % 2
        si[0] += 1
        if stg_free[b]:
            p.wait("gpsimd", "s", stg_free[b])
        dv = gdma(stg4[b][0:parts, 0:n], src_ap, "st%d" % b)
        p.wait("scalar", "st%d" % b, dv)
        p.op("scalar", lambda E, d=dst_bf, s=stg4[b][0:parts, 0:n]:
             E.activation(out=d, in_=s, func=AF.Copy), inc=("s", 1))
        stg_free[b] = p.cnt["s"]

    CAST_ITEMS = []
    for k in range(KD):
        CAST_ITEMS.append((wq_h[128 * k:128 * k + 128, :],
                           wq_sb[:, HID * k:HID * (k + 1)], 128, HID))
        CAST_ITEMS.append((wv_h[128 * k:128 * k + 128, :],
                           wv_sb[:, HID * k:HID * (k + 1)], 128, HID))
    for k in range(KD):
        CAST_ITEMS.append((wk_h[128 * k:128 * k + 128, :],
                           wk_sb[:, HID * k:HID * (k + 1)], 128, HID))
    for hh in range(H):
        CAST_ITEMS.append((wo8_h[:, D * hh:D * (hh + 1)],
                           wo_sb[0:64, D * hh:D * (hh + 1)], 64, 1024))
    qv_cast_s = [0]

    SH = 8
    xa_free = [0, 0]
    ut_free = [0, 0]
    pA_lo_free_s = [0]
    pA_free_list = []
    pq_free_s = [0]
    eqs_free_t = [0, 0]
    eqt_evac_s = [0]

    def ln_part(c):
        b = c % 2
        if xa_free[b]:
            p.wait("sync", "s", xa_free[b])
        dv = dma(xa[b], x_h[128 * c:128 * c + 128, :], "xa%d" % b)
        p.wait("vector", "xa%d" % b, dv)
        sv = ln_stats(xa[b][:, 0:512], xa[b][:, 512:1024])
        p.wait("scalar", "v", sv)
        if ut_free[b]:
            p.wait("scalar", "t", ut_free[b])
        xa_free[b] = normalize(ut[b], xa[b])
        p.wait("tensor", "s", p.cnt["s"])
        if pA_lo_free_s[0]:
            p.wait("tensor", "s", pA_lo_free_s[0])
        for k in range(KD):
            p.op("tensor", lambda E, k=k, b=b:
                 E.transpose(pA[:, 128 * k:128 * k + 128],
                             ut[b][:, 128 * k:128 * k + 128], ident),
                 inc=("t", 1) if k == KD - 1 else None)
        ut_free[b] = p.cnt["t"]
        p.wait("scalar", "t", p.cnt["t"])
        dst = uT[:, :].rearrange("p (k n) -> p k n", n=T)[:, :,
                                                          128 * c:128 * c + 128]
        p.op("scalar", lambda E, d=dst: E.activation(
            out=d, in_=pA[:, 0:1024].rearrange("p (k n) -> p k n", n=128),
            func=AF.Copy), inc=("s", 1))
        pA_lo_free_s[0] = p.cnt["s"]
        pA_free_list.append(p.cnt["s"])

    def qv_part(j):
        b = j % 2
        p.wait("tensor", "s", qv_cast_s[0])
        p.wait("tensor", "s", pA_free_list[j])
        if pq_free_s[0]:
            p.wait("tensor", "s", pq_free_s[0])
        for k in range(KD):
            lhs = uT[:, :].rearrange("p (k n) -> p k n", n=T)[
                :, k, 128 * j:128 * j + 128]
            p.op("tensor", lambda E, l=lhs, k=k:
                 E.matmul(pC[:, 0:512], l, wq_sb[:, HID * k:HID * (k + 1)],
                          start=(k == 0), stop=(k == KD - 1)))
            p.op("tensor", lambda E, l=lhs, k=k:
                 E.matmul(pC[:, 512:1024], l, wv_sb[:, HID * k:HID * (k + 1)],
                          start=(k == 0), stop=(k == KD - 1)),
                 inc=("t", 1) if k == KD - 1 else None)
        tq = p.cnt["t"]
        p.wait("scalar", "t", tq)
        p.op("scalar", lambda E, j=j:
             E.activation(out=vb[:, HID * j:HID * (j + 1)],
                          in_=pC[:, 512:1024], func=AF.Copy), inc=("s", 1))
        if eqs_free_t[b]:
            p.wait("scalar", "t", eqs_free_t[b])
        p.op("scalar", lambda E, b=b: E.activation(
            out=eqs[b], in_=pC[:, 0:512], func=AF.Exp), inc=("s", 1))
        pq_free_s[0] = p.cnt["s"]
        p.wait("vector", "s", pq_free_s[0])
        e3 = eqs[b].rearrange("p (h e) -> p h e", e=64)
        vop(lambda E, e3=e3: E.tensor_reduce(out=q_sq, in_=e3, axis=AX.X,
                                             op=ALU.add))
        vop(lambda E: E.reciprocal(out=q_si, in_=q_sq))
        vop(lambda E: E.tensor_scalar(out=q_si, in0=q_si, scalar1=ATTN_SCALE,
                                      scalar2=None, op0=ALU.mult))
        sib = q_si.unsqueeze(2).broadcast_to([128, 8, 64])
        vop(lambda E, e3=e3, sib=sib: E.tensor_tensor(
            out=e3, in0=e3, in1=sib, op=ALU.mult), drain=False)
        vq_t = p.tick("vector", "v")
        p.wait("tensor", "v", vq_t)
        if eqt_evac_s[0]:
            p.wait("tensor", "s", eqt_evac_s[0])
        for hh in range(H):
            src = eqs[b][:, 64 * hh:64 * (hh + 1)]
            p.op("tensor", lambda E, s=src, hh=hh:
                 E.transpose(pA[0:64, 1024 + 128 * hh:1024 + 128 * (hh + 1)],
                             s, ident),
                 inc=("t", 1) if hh == H - 1 else None)
        eqs_free_t[b] = p.cnt["t"]
        p.wait("scalar", "t", p.cnt["t"])
        dstA = eqtbA.rearrange("p (h n) -> p h n", n=T)[:, :,
                                                        128 * j:128 * j + 128]
        dstB = eqtbB.rearrange("p (h n) -> p h n", n=T)[:, :,
                                                        128 * j:128 * j + 128]
        p.op("scalar", lambda E, d=dstA: E.activation(
            out=d, in_=pA[0:64, 1024:1536].rearrange("p (h n) -> p h n",
                                                     n=128),
            func=AF.Copy), inc=("s", 1))
        p.op("scalar", lambda E, d=dstB: E.activation(
            out=d, in_=pA[0:64, 1536:2048].rearrange("p (h n) -> p h n",
                                                     n=128),
            func=AF.Copy), inc=("s", 1))
        eqt_evac_s[0] = p.cnt["s"]

    ci = [0]

    def emit_casts(n_):
        for _ in range(n_):
            if ci[0] < len(CAST_ITEMS):
                stage_cast(*CAST_ITEMS[ci[0]])
                ci[0] += 1
                if ci[0] == 16:
                    qv_cast_s[0] = p.cnt["s"]

    for c in range(NT):
        emit_casts(2)
        ln_part(c)
        if c >= SH:
            qv_part(c - SH)
    for j in range(NT - SH, NT):
        qv_part(j)

    pk_free_s = 0
    ekw_free_t = [0]
    ek_pbc_free_s = [0]
    for hc in range(NHC):
        if hc == 0:
            p.wait("tensor", "s", eqt_evac_s[0])
            p.wait("tensor", "s", pq_free_s[0])
        if pk_free_s:
            p.wait("tensor", "s", pk_free_s)
        for k in range(KD):
            lhs = wk_sb[:, HID * k + 128 * hc:HID * k + 128 * (hc + 1)]
            for ns in range(4):
                mov = uT[:, :].rearrange("p (k n) -> p k n", n=T)[
                    :, k, 512 * ns:512 * (ns + 1)]
                p.op("tensor", lambda E, l=lhs, m=mov, ns=ns, k=k:
                     E.matmul(pA[:, 512 * ns:512 * (ns + 1)], l, m,
                              start=(k == 0), stop=(k == KD - 1)),
                     inc=("t", 1) if (k == KD - 1 and ns == 3) else None)
        tk = p.cnt["t"]
        p.wait("scalar", "t", tk)
        if ekw_free_t[0]:
            p.wait("scalar", "t", ekw_free_t[0])
        p.op("scalar", lambda E: E.activation(
            out=ekw[:, 0:1024], in_=pA[:, 0:1024], func=AF.Exp,
            accum_out=k_sk))
        p.drain("scalar")
        p.op("scalar", lambda E: E.activation(
            out=ekw[:, 1024:2048], in_=pA[:, 1024:2048],
            func=AF.Exp, accum_out=k_ski), inc=("s", 1))
        pk_free_s = p.cnt["s"]
        p.wait("vector", "s", p.cnt["s"])
        vop(lambda E: E.tensor_tensor(out=k_sk, in0=k_sk, in1=k_ski,
                                      op=ALU.add))
        vop(lambda E: E.reciprocal(out=k_ski, in_=k_sk))
        vop(lambda E: E.tensor_scalar(
            out=ekw[:, 0:2048], in0=ekw[:, 0:2048],
            scalar1=k_ski, scalar2=None, op0=ALU.mult), drain=False)
        ekvt = p.tick("vector", "v")
        p.wait("tensor", "v", ekvt)
        if ek_pbc_free_s[0]:
            p.wait("tensor", "s", ek_pbc_free_s[0])
        for i in range(8):
            p.op("tensor", lambda E, i=i:
                 E.transpose(pB[:, 128 * i:128 * (i + 1)],
                             ekw[:, 128 * i:128 * (i + 1)], ident))
        for i in range(8):
            p.op("tensor", lambda E, i=i:
                 E.transpose(pC[:, 128 * i:128 * (i + 1)],
                             ekw[:, 1024 + 128 * i:1024 + 128 * (i + 1)],
                             ident),
                 inc=("t", 1) if i == 7 else None)
        ekw_free_t[0] = p.cnt["t"]
        p.wait("scalar", "t", p.cnt["t"])
        d1 = ektb.rearrange("p (c x) -> p c x", x=HID)[
            :, 0:8, 128 * hc:128 * (hc + 1)]
        d2 = ektb.rearrange("p (c x) -> p c x", x=HID)[
            :, 8:16, 128 * hc:128 * (hc + 1)]
        p.op("scalar", lambda E, d=d1: E.activation(
            out=d, in_=pB[:, 0:1024].rearrange("p (i n) -> p i n", n=128),
            func=AF.Copy), inc=("s", 1))
        p.op("scalar", lambda E, d=d2: E.activation(
            out=d, in_=pC[:, 0:1024].rearrange("p (i n) -> p i n", n=128),
            func=AF.Copy), inc=("s", 1))
        ek_pbc_free_s[0] = p.cnt["s"]

    p.barrier()

    for hh in range(H):
        for c in range(NT):
            lhs = ektb[:, HID * c + 64 * hh:HID * c + 64 * (hh + 1)]
            mov = vb[:, HID * c + 64 * hh:HID * c + 64 * (hh + 1)]
            p.op("tensor", lambda E, l=lhs, m=mov, c=c, hh=hh:
                 E.matmul(pA[0:64, 64 * hh:64 * (hh + 1)], l, m,
                          start=(c == 0), stop=(c == NT - 1)),
                 inc=("t", 1) if (c == NT - 1 and hh == H - 1) else None)
    p.wait("scalar", "t", p.cnt["t"])
    p.op("scalar", lambda E: E.activation(out=c_sb, in_=pA[0:64, 0:512],
                                          func=AF.Copy), inc=("s", 1))
    p.wait("tensor", "s", p.cnt["s"])
    po_free = [0, 0]
    pi = 0
    for hh in range(H):
        for ns in range(4):
            if po_free[pi % 2]:
                p.wait("tensor", "s", po_free[pi % 2])
            lhs = c_sb[0:64, 64 * hh:64 * (hh + 1)]
            arr = eqtbA if hh < 4 else eqtbB
            mov = arr[0:64, T * (hh % 4) + 512 * ns:
                      T * (hh % 4) + 512 * (ns + 1)]
            reg = pB[0:64, 512 * (pi % 2):512 * (pi % 2) + 512]
            p.op("tensor", lambda E, l=lhs, m=mov, r=reg:
                 E.matmul(r, l, m, start=True, stop=True), inc=("t", 1))
            p.wait("scalar", "t", p.cnt["t"])
            p.op("scalar", lambda E, hh=hh, ns=ns, r=reg: E.activation(
                out=aob[0:64, T * hh + 512 * ns:T * hh + 512 * (ns + 1)],
                in_=r, func=AF.Copy), inc=("s", 1))
            po_free[pi % 2] = p.cnt["s"]
            pi += 1

    p.wait("tensor", "s", p.cnt["s"])
    p.wait("sync", "t", p.cnt["t"])
    ox_free = [0, 0]
    xb_free = [0, 0]
    pwo_free = 0
    for c in range(NT):
        b = c % 2
        if xb_free[b]:
            p.wait("sync", "v", xb_free[b])
        dv = dma(xb[b], x_h[128 * c:128 * c + 128, :], "xb%d" % b)
        if pwo_free:
            p.wait("tensor", "s", pwo_free)
        for hh in range(H):
            lhs = aob[0:64, T * hh + 128 * c:T * hh + 128 * (c + 1)]
            for kk in range(2):
                p.op("tensor", lambda E, l=lhs, hh=hh, kk=kk:
                     E.matmul(pC[:, 512 * kk:512 * (kk + 1)], l,
                              wo_sb[0:64, D * hh + 512 * kk:
                                    D * hh + 512 * (kk + 1)],
                              start=(hh == 0), stop=False))
        for kk in range(2):
            p.op("tensor", lambda E, kk=kk:
                 E.matmul(pC[:, 512 * kk:512 * (kk + 1)], ones1,
                          bo_bf[0:1, 512 * kk:512 * (kk + 1)],
                          start=False, stop=True),
                 inc=("t", 1) if kk == 1 else None)
        two = p.cnt["t"]
        p.wait("vector", "t", two)
        sv = ln_stats(pC[:, 0:512], pC[:, 512:1024])
        p.wait("scalar", "v", sv)
        if ox_free[b]:
            p.wait("scalar", "ox%d" % b, ox_free[b])
        pwo_free = normalize(ox[b], pC[:, 0:1024])
        p.wait("vector", "s", p.cnt["s"])
        p.wait("vector", "xb%d" % b, dv)
        vop(lambda E, b=b: E.tensor_tensor(out=ox[b], in0=ox[b], in1=xb[b],
                                           op=ALU.add), drain=False)
        vt = p.tick("vector", "v")
        xb_free[b] = vt
        p.wait("sync", "v", vt)
        ox_free[b] = dma(x2s_h[128 * c:128 * c + 128, :], ox[b], "ox%d" % b)

    p.barrier()

    stgW = [V(4 * KB + 8 * KB * i, 128, 2048) for i in range(5)]
    wfcq = V(14 * KB, 128, KD * J, BF16)
    wpjq = V(78 * KB, 128, NJ * D, BF16)

    stgW_free = [0, 0, 0, 0, 0]
    pB_free2 = [None]

    WORK = [(wfc_h[128 * jc:128 * (jc + 1), :], 1024, 32,
             [wfcq[:, J * k + 128 * jc:J * k + 128 * (jc + 1)]
              for k in range(KD)]) for jc in range(NJ)]
    WORK += [(wpj_h[128 * (idx // 2):128 * (idx // 2) + 128,
                    2048 * (idx % 2):2048 * (idx % 2 + 1)], 2048, 64,
              [wpjq[:, D * (16 * (idx % 2) + t) + 128 * (idx // 2):
                    D * (16 * (idx % 2) + t) + 128 * (idx // 2) + 128]
               for t in range(16)]) for idx in range(16)]

    def w_load(i):
        src_h_ap, n, nb, _ = WORK[i]
        b = i % 5
        if stgW_free[b]:
            p.wait("gpsimd", "t", stgW_free[b])
        gdma(stgW[b][:, 0:n], src_h_ap, "sw%d" % b)

    def w_amax_pair(pair):
        tot = 0
        for j in (2 * pair, 2 * pair + 1):
            _, n, nb, _ = WORK[j]
            b = j % 5
            p.wait("vector", "sw%d" % b, p.cnt["sw%d" % b])
            s3 = stgW[b][:, 0:n].rearrange("p (b k) -> p b k", k=32)
            vop(lambda E, s3=s3, nb=nb, lo=tot: E.tensor_reduce(
                out=am[:, lo:lo + nb], in_=s3, axis=AX.X, op=ALU.max,
                apply_absolute_value=True), drain=(j % 2 == 1))
            tot += nb
        return tot, qs_a(tot, 0)

    def w_chain(i, lo):
        _, n, nb, dst_tiles = WORK[i]
        b = i % 5
        s3 = stgW[b][:, 0:n].rearrange("p (b k) -> p b k", k=32)
        quant_chain(s3, nb, lo, stgW[b][:, 0:n])
        return p.tick("vector", "v")

    def w_te(i, vq):
        _, n, nb, dst_tiles = WORK[i]
        b = i % 5
        ntile = n // 128
        for base in range(0, ntile, 8):
            cnt = min(8, ntile - base)
            p.wait("tensor", "v", vq)
            if pB_free2[0]:
                p.wait("tensor", pB_free2[0][0], pB_free2[0][1])
            for i2 in range(cnt):
                src2 = stgW[b][:, 128 * (base + i2):128 * (base + i2 + 1)]
                p.op("tensor", lambda E, s=src2, i=i2:
                     E.transpose(pB[:, 128 * i:128 * (i + 1)], s, ident),
                     inc=("t", 1) if i2 == cnt - 1 else None)
            stgW_free[b] = p.cnt["t"]
            p.wait("scalar", "t", p.cnt["t"])
            for i2 in range(cnt):
                p.op("scalar", lambda E, d=dst_tiles[base + i2], i=i2:
                     E.activation(out=d, in_=pB[:, 128 * i:128 * (i + 1)],
                                  func=AF.Copy),
                     inc=("s", 1) if i2 == cnt - 1 else None)
            pB_free2[0] = ("s", p.cnt["s"])

    NP = len(WORK) // 2
    for j in range(4):
        w_load(j)
    tot0, sln0 = w_amax_pair(0)
    sexp0 = qs_b(tot0, 0, sln0)
    qs_c(tot0, 0, sexp0)
    for P in range(NP):
        i0, i1 = 2 * P, 2 * P + 1
        nxt = P + 1 < NP
        if 2 * P + 4 < len(WORK):
            w_load(2 * P + 4)
        if nxt:
            tot1, sln1 = w_amax_pair(P + 1)
        vq0 = w_chain(i0, 0)
        vq1 = w_chain(i1, WORK[i0][2])
        if nxt:
            sexp1 = qs_b(tot1, 0, sln1)
        w_te(i0, vq0)
        if 2 * P + 5 < len(WORK):
            w_load(2 * P + 5)
        if nxt:
            qs_c(tot1, 0, sexp1)
        w_te(i1, vq1)

    p.barrier()

    x2t = [V(4 * KB, 128, 1024), V(8 * KB, 128, 1024)]
    w1 = V(12 * KB, 128, 2048)
    w2 = V(20 * KB, 128, 2048)
    u2f = V(28 * KB, 128, 1024)
    u2qt = V(142 * KB, 128, 1024, BF16)
    tmpn = V(12 * KB, 128, 1024)
    gqt = V(144 * KB, 128, NJ * 128, BF16)

    x2t_free = [0, 0]
    u2_t_free = [0]
    u2qt_sfree = [0]
    w1_t_free = [0]
    w2_t_free = [0]
    gqt_read_t = [0]
    gqt_s_evac = [0]
    fcA_ok_v = [0]
    fcB_ok_s = [0]
    pCfree_v = [0]

    def u2path(c):
        b = c % 2
        if x2t_free[b]:
            p.wait("sync", "ou%d" % b, x2t_free[b])
        dv = dma(x2t[b], x2s_h[128 * c:128 * c + 128, :], "x2%d" % b)
        p.wait("vector", "x2%d" % b, dv)
        sv = ln_stats(x2t[b][:, 0:512], x2t[b][:, 512:1024])
        p.wait("scalar", "v", sv)
        if u2_t_free[0]:
            p.wait("scalar", "t", u2_t_free[0])
        normalize(u2f, x2t[b])
        p.wait("vector", "s", p.cnt["s"])
        u3 = u2f[:, :].rearrange("p (b k) -> p b k", k=32)
        vop(lambda E, u3=u3: E.tensor_reduce(out=am[:, 0:32], in_=u3,
                                             axis=AX.X, op=ALU.max,
                                             apply_absolute_value=True))
        quant_scales(32)
        quant_chain(u3, 32, 0, u2f)
        return p.tick("vector", "v")

    def transp_u2(c, u2v):
        p.wait("tensor", "v", u2v)
        if fcA_ok_v[0]:
            p.wait("tensor", "v", fcA_ok_v[0])
        if gqt_s_evac[0]:
            p.wait("tensor", "s", gqt_s_evac[0])
        for k in range(KD):
            p.op("tensor", lambda E, k=k:
                 E.transpose(pA[:, 128 * k:128 * k + 128],
                             u2f[:, 128 * k:128 * k + 128], ident),
                 inc=("t", 1) if k == KD - 1 else None)
        u2_t_free[0] = p.cnt["t"]
        p.wait("scalar", "t", p.cnt["t"])
        p.op("scalar", lambda E: E.activation(
            out=u2qt, in_=pA[:, 0:1024], func=AF.Copy), inc=("s", 1))
        u2qt_sfree[0] = p.cnt["s"]

    uv0 = u2path(0)
    transp_u2(0, uv0)

    for c in range(NT):
        b = c % 2
        p.wait("tensor", "s", u2qt_sfree[0])
        for g in range(2):
            pz = pA[:, 1024 * g:1024 * (g + 1)]
            for k in range(KD):
                lhs = u2qt[:, 128 * k:128 * (k + 1)]
                for js in range(2):
                    mov = wfcq[:, J * k + 1024 * g + 512 * js:
                               J * k + 1024 * g + 512 * (js + 1)]
                    p.op("tensor", lambda E, l=lhs, m=mov, pz=pz, js=js, k=k:
                         E.matmul(pz[:, 512 * js:512 * (js + 1)], l, m,
                                  start=(k == 0), stop=False))
            for js in range(2):
                mov = bfc_bf[0:1, 1024 * g + 512 * js:1024 * g + 512 * (js + 1)]
                p.op("tensor", lambda E, m=mov, pz=pz, js=js:
                     E.matmul(pz[:, 512 * js:512 * (js + 1)], ones1, m,
                              start=False, stop=True),
                     inc=("t", 1) if js == 1 else None)
        tfca = p.cnt["t"]
        if fcB_ok_s[0]:
            p.wait("tensor", "s", fcB_ok_s[0])
        if pCfree_v[0]:
            p.wait("tensor", "v", pCfree_v[0])
        for g in range(2):
            pz = (pB if g == 0 else pC)[:, 0:1024]
            for k in range(KD):
                lhs = u2qt[:, 128 * k:128 * (k + 1)]
                for js in range(2):
                    mov = wfcq[:, J * k + 2048 + 1024 * g + 512 * js:
                               J * k + 2048 + 1024 * g + 512 * (js + 1)]
                    p.op("tensor", lambda E, l=lhs, m=mov, pz=pz, js=js, k=k:
                         E.matmul(pz[:, 512 * js:512 * (js + 1)], l, m,
                                  start=(k == 0), stop=False))
            for js in range(2):
                mov = bfc_bf[0:1, 2048 + 1024 * g + 512 * js:
                             2048 + 1024 * g + 512 * (js + 1)]
                p.op("tensor", lambda E, m=mov, pz=pz, js=js:
                     E.matmul(pz[:, 512 * js:512 * (js + 1)], ones1, m,
                              start=False, stop=True),
                     inc=("t", 1) if js == 1 else None)
        tfcb = p.cnt["t"]
        p.wait("scalar", "t", tfca)
        p.op("scalar", lambda E: E.activation(out=pA[:, 0:2048],
                                              in_=pA[:, 0:2048], func=AF.Gelu),
             inc=("s", 1))
        sga = p.cnt["s"]
        p.wait("scalar", "t", tfcb)
        p.op("scalar", lambda E: E.activation(out=pB[:, 0:1024],
                                              in_=pB[:, 0:1024], func=AF.Gelu),
             inc=("s", 1))
        sgb = p.cnt["s"]
        p.op("scalar", lambda E: E.activation(out=pC[:, 0:1024],
                                              in_=pC[:, 0:1024], func=AF.Gelu),
             inc=("s", 1))
        sgc = p.cnt["s"]
        p.wait("vector", "s", sga)
        vop(lambda E: E.tensor_reduce(
            out=am[:, 0:64],
            in_=pA[:, 0:2048].rearrange("p (b k) -> p b k", k=32),
            axis=AX.X, op=ALU.max, apply_absolute_value=True), drain=False)
        p.wait("vector", "s", sgb)
        vop(lambda E: E.tensor_reduce(
            out=am[:, 64:96],
            in_=pB[:, 0:1024].rearrange("p (b k) -> p b k", k=32),
            axis=AX.X, op=ALU.max, apply_absolute_value=True), drain=False)
        p.wait("vector", "s", sgc)
        vop(lambda E: E.tensor_reduce(
            out=am[:, 96:128],
            in_=pC[:, 0:1024].rearrange("p (b k) -> p b k", k=32),
            axis=AX.X, op=ALU.max, apply_absolute_value=True))
        quant_scales(128)
        if w1_t_free[0]:
            p.wait("vector", "t", w1_t_free[0])
        vop(lambda E: E.tensor_tensor(
            out=w1[:, 0:2048].rearrange("p (b k) -> p b k", k=32),
            in0=pA[:, 0:2048].rearrange("p (b k) -> p b k", k=32),
            in1=iv_t[:, 0:64].unsqueeze(2).broadcast_to([128, 64, 32]),
            op=ALU.mult), drain=False)
        vra = p.tick("vector", "v")
        fcA_ok_v[0] = vra
        vop(lambda E: E.tensor_scalar(out=w1[:, 0:2048], in0=w1[:, 0:2048],
                                      scalar1=K1, scalar2=K1, op0=ALU.add,
                                      op1=ALU.subtract))
        vop(lambda E: E.tensor_tensor(
            out=w1[:, 0:2048].rearrange("p (b k) -> p b k", k=32),
            in0=w1[:, 0:2048].rearrange("p (b k) -> p b k", k=32),
            in1=sc_t[:, 0:64].unsqueeze(2).broadcast_to([128, 64, 32]),
            op=ALU.mult), drain=False)
        vqa = p.tick("vector", "v")
        p.wait("tensor", "v", vqa)
        for i in range(16):
            p.op("tensor", lambda E, i=i:
                 E.transpose(pA[:, 128 * i:128 * (i + 1)],
                             w1[:, 128 * i:128 * (i + 1)], ident),
                 inc=("t", 1) if i == 15 else None)
        w1_t_free[0] = p.cnt["t"]
        p.wait("scalar", "t", p.cnt["t"])
        if gqt_read_t[0]:
            p.wait("scalar", "t", gqt_read_t[0])
        p.op("scalar", lambda E: E.activation(out=gqt[:, 0:2048],
                                              in_=pA[:, 0:2048],
                                              func=AF.Copy), inc=("s", 1))
        evacA_s = p.cnt["s"]
        if w2_t_free[0]:
            p.wait("vector", "t", w2_t_free[0])
        vop(lambda E: E.tensor_tensor(
            out=w2[:, 0:1024].rearrange("p (b k) -> p b k", k=32),
            in0=pB[:, 0:1024].rearrange("p (b k) -> p b k", k=32),
            in1=iv_t[:, 64:96].unsqueeze(2).broadcast_to([128, 32, 32]),
            op=ALU.mult), drain=False)
        vop(lambda E: E.tensor_tensor(
            out=w2[:, 1024:2048].rearrange("p (b k) -> p b k", k=32),
            in0=pC[:, 0:1024].rearrange("p (b k) -> p b k", k=32),
            in1=iv_t[:, 96:128].unsqueeze(2).broadcast_to([128, 32, 32]),
            op=ALU.mult), drain=False)
        vrb = p.tick("vector", "v")
        pCfree_v[0] = vrb
        vop(lambda E: E.tensor_scalar(out=w2[:, 0:2048], in0=w2[:, 0:2048],
                                      scalar1=K1, scalar2=K1, op0=ALU.add,
                                      op1=ALU.subtract))
        vop(lambda E: E.tensor_tensor(
            out=w2[:, 0:2048].rearrange("p (b k) -> p b k", k=32),
            in0=w2[:, 0:2048].rearrange("p (b k) -> p b k", k=32),
            in1=sc_t[:, 64:128].unsqueeze(2).broadcast_to([128, 64, 32]),
            op=ALU.mult), drain=False)
        vqb = p.tick("vector", "v")
        p.wait("tensor", "s", evacA_s)
        p.wait("tensor", "v", pCfree_v[0])
        for jc in range(NJ // 2):
            lhs = gqt[:, 128 * jc:128 * (jc + 1)]
            for kk in range(2):
                mov = wpjq[:, D * jc + 512 * kk:D * jc + 512 * (kk + 1)]
                p.op("tensor", lambda E, l=lhs, m=mov, kk=kk, jc=jc:
                     E.matmul(pC[:, 512 * kk:512 * (kk + 1)], l, m,
                              start=(jc == 0), stop=False))
        p.wait("tensor", "v", vqb)
        p.wait("tensor", "s", evacA_s)
        for i in range(16):
            p.op("tensor", lambda E, i=i:
                 E.transpose(pA[:, 128 * i:128 * (i + 1)],
                             w2[:, 128 * i:128 * (i + 1)], ident),
                 inc=("t", 1) if i == 15 else None)
        w2_t_free[0] = p.cnt["t"]
        p.wait("scalar", "t", p.cnt["t"])
        p.op("scalar", lambda E: E.activation(out=gqt[:, 2048:4096],
                                              in_=pA[:, 0:2048],
                                              func=AF.Copy), inc=("s", 1))
        gqt_s_evac[0] = p.cnt["s"]
        if c + 1 < NT:
            uv = u2path(c + 1)
            transp_u2(c + 1, uv)
        p.wait("tensor", "s", gqt_s_evac[0])
        for jc in range(NJ // 2, NJ):
            lhs = gqt[:, 128 * jc:128 * (jc + 1)]
            for kk in range(2):
                mov = wpjq[:, D * jc + 512 * kk:D * jc + 512 * (kk + 1)]
                p.op("tensor", lambda E, l=lhs, m=mov, kk=kk, jc=jc:
                     E.matmul(pC[:, 512 * kk:512 * (kk + 1)], l, m,
                              start=False, stop=False))
        for kk in range(2):
            p.op("tensor", lambda E, kk=kk:
                 E.matmul(pC[:, 512 * kk:512 * (kk + 1)], ones1,
                          bpj_bf[0:1, 512 * kk:512 * (kk + 1)],
                          start=False, stop=True),
                 inc=("t", 1) if kk == 1 else None)
        gqt_read_t[0] = p.cnt["t"]
        tpj = p.cnt["t"]
        p.wait("vector", "t", tpj)
        sv = ln_stats(pC[:, 0:512], pC[:, 512:1024])
        p.wait("scalar", "v", sv)
        p.wait("scalar", "t", w1_t_free[0])
        fcB_ok_s[0] = normalize(tmpn, pC[:, 0:1024])
        p.wait("vector", "s", p.cnt["s"])
        vop(lambda E, b=b: E.tensor_tensor(out=x2t[b], in0=tmpn, in1=x2t[b],
                                           op=ALU.add), drain=False)
        vo = p.tick("vector", "v")
        p.wait("sync", "v", vo)
        x2t_free[b] = dma(out_h[128 * c:128 * c + 128, :], x2t[b],
                          "ou%d" % b)

    p.wait("sync", "ou0", p.cnt["ou0"])
    p.wait("sync", "ou1", p.cnt["ou1"])

    with nc.Block() as blk:
        @blk.sync
        def _(E):
            for f in p.steps["sync"]:
                f(E)

        @blk.vector
        def _(E):
            for f in p.steps["vector"]:
                f(E)

        @blk.scalar
        def _(E):
            for f in p.steps["scalar"]:
                f(E)

        @blk.tensor
        def _(E):
            for f in p.steps["tensor"]:
                f(E)

        @blk.gpsimd
        def _(E):
            for f in p.steps["gpsimd"]:
                f(E)

    es.close()
    return nc


_NC_CACHE = None


def kernel(L0_attn_x, L0_attn_Wq, L0_attn_Wk, L0_attn_Wv, L0_attn_Wo,
           L0_attn_bo, L0_ffn_w_fc, L0_ffn_b_fc, L0_ffn_w_proj, L0_ffn_b_proj):
    global _NC_CACHE
    x = np.ascontiguousarray(np.asarray(L0_attn_x, dtype=np.float32))
    wq = np.ascontiguousarray(np.asarray(L0_attn_Wq, dtype=np.float32))
    wk = np.ascontiguousarray(np.asarray(L0_attn_Wk, dtype=np.float32))
    wv = np.ascontiguousarray(np.asarray(L0_attn_Wv, dtype=np.float32))
    wo = np.asarray(L0_attn_Wo, dtype=np.float32)
    bo = np.ascontiguousarray(
        np.asarray(L0_attn_bo, dtype=np.float32).reshape(1, D))
    wfc = np.ascontiguousarray(np.asarray(L0_ffn_w_fc, dtype=np.float32))
    bfc = np.ascontiguousarray(
        np.asarray(L0_ffn_b_fc, dtype=np.float32).reshape(1, J))
    wpj = np.ascontiguousarray(np.asarray(L0_ffn_w_proj, dtype=np.float32))
    bpj = np.ascontiguousarray(
        np.asarray(L0_ffn_b_proj, dtype=np.float32).reshape(1, D))
    wo8 = np.ascontiguousarray(
        wo.reshape(H, DH, D).transpose(1, 0, 2).reshape(DH, H * D))
    ident = np.eye(128, dtype=np.float32)
    consts = np.ascontiguousarray(np.tile(np.array(
        [LN2, -3.0 * LN2, -LN2, 3.0 * LN2, LN_EPS, -0.5, IL2_LO, FLOOR_DELTA],
        dtype=np.float32), (128, 1)))

    if _NC_CACHE is None:
        _NC_CACHE = build_nc()
    nc = _NC_CACHE

    in_maps = []
    for i in range(B):
        in_maps.append(dict(
            x=np.ascontiguousarray(x[i]), wq=wq, wk=wk, wv=wv, wo8=wo8, bo=bo,
            wfc=wfc, bfc=bfc, wpj=wpj, bpj=bpj, ident=ident, consts=consts))
    res = run_bass_kernel_spmd(nc, in_maps, core_ids=list(range(B)))
    out = np.stack([np.asarray(res.results[i]["out"]) for i in range(B)],
                   axis=0)
    return out


# version 16
# speedup vs baseline: 1.0931x; 1.0288x over previous
import math
from collections import Counter
from contextlib import ExitStack

import numpy as np
import ml_dtypes
import concourse.bass as bass
import concourse.mybir as mybir
from concourse.bass_utils import run_bass_kernel_spmd

F32 = mybir.dt.float32
BF16 = mybir.dt.bfloat16
U8 = mybir.dt.uint8
AF = mybir.ActivationFunctionType
ALU = mybir.AluOpType
AX = mybir.AxisListType

B, T, D, HID, H, DH, J = 8, 2048, 1024, 512, 8, 64, 4096
NT, KD, NHC, NJ = 16, 8, 4, 32

C23 = float(np.float32(1.5 * 2.0 ** 23))
IL2_HI = float(np.float32(1.0 / math.log(2.0)))
IL2_LO = float(np.float32(1.0 / math.log(2.0) - np.float32(1.0 / math.log(2.0))))
L2_HI = float(np.float32(math.log(2.0)))
L2_LO = float(np.float32(math.log(2.0) - np.float32(math.log(2.0))))
C52 = 1.5 * 2.0 ** 52
LN2_64 = float(math.log(2.0))
EPS40 = float(np.float32(2.0 ** -40))
T40 = float(np.float32(2.0 ** 40))
LN2 = float(np.float32(math.log(2.0)))
INVLN2 = float(np.float32(1.0 / math.log(2.0)))
FLOOR_DELTA = 4e-8
ATTN_SCALE = 0.125
LN_EPS = 1e-5
KB = 1024
F32SZ = 44 * KB
BFSZ = 160 * KB

ENGINES = ("sync", "vector", "scalar", "tensor", "gpsimd")
ESEM = {"sync": "d2", "vector": "v", "scalar": "s", "tensor": "t",
        "gpsimd": "g"}


class Plan:
    def __init__(self):
        self.steps = {e: [] for e in ENGINES}
        self.cnt = Counter()
        self.sems = {}
        self.dma_sems = set()

    def op(self, eng, f, inc=None):
        if inc is not None:
            self.cnt[inc[0]] += inc[1]

        def run(E, f=f, inc=inc, sems=self.sems):
            r = f(E)
            if inc is not None:
                r.then_inc(sems[inc[0]], inc[1])
        self.steps[eng].append(run)

    def wait(self, eng, sem, val=None):
        if val is None:
            val = self.cnt[sem]
        if val <= 0:
            return
        self.steps[eng].append(
            lambda E, s=sem, v=val, sems=self.sems: E.wait_ge(sems[s], v))

    def tick(self, eng, sem):
        self.cnt[sem] += 1

        def run(E, s=sem, sems=self.sems):
            E.drain()
            E.sem_inc(sems[s], 1)
        self.steps[eng].append(run)
        return self.cnt[sem]

    def drain(self, eng):
        self.steps[eng].append(lambda E: E.drain())

    def barrier(self):
        vals = {}
        for e in ("vector", "scalar", "tensor", "gpsimd"):
            vals[ESEM[e]] = self.tick(e, ESEM[e])
        for ds in sorted(self.dma_sems):
            self.wait("sync", ds, self.cnt[ds])
        vals["d2"] = self.tick("sync", "d2")
        for e in ENGINES:
            for s, v in vals.items():
                if s != ESEM[e]:
                    self.wait(e, s, v)


def build_nc():
    nc = bass.Bass("TRN2", target_bir_lowering=False, debug=False)
    p = Plan()

    x_h = nc.dram_tensor("x", [T, D], F32, kind="ExternalInput").ap()
    wq_h = nc.dram_tensor("wq", [D, HID], F32, kind="ExternalInput").ap()
    wk_h = nc.dram_tensor("wk", [D, HID], F32, kind="ExternalInput").ap()
    wv_h = nc.dram_tensor("wv", [D, HID], F32, kind="ExternalInput").ap()
    wo8_h = nc.dram_tensor("wo8", [DH, H * D], F32, kind="ExternalInput").ap()
    bo_h = nc.dram_tensor("bo", [1, D], F32, kind="ExternalInput").ap()
    wfc_h = nc.dram_tensor("wfc", [J, D], F32, kind="ExternalInput").ap()
    bfc_h = nc.dram_tensor("bfc", [1, J], F32, kind="ExternalInput").ap()
    wpj_h = nc.dram_tensor("wpj", [D, J], F32, kind="ExternalInput").ap()
    bpj_h = nc.dram_tensor("bpj", [1, D], F32, kind="ExternalInput").ap()
    ident_h = nc.dram_tensor("ident", [128, 128], F32, kind="ExternalInput").ap()
    consts_h = nc.dram_tensor("consts", [128, 8], F32, kind="ExternalInput").ap()
    out_h = nc.dram_tensor("out", [T, D], F32, kind="ExternalOutput").ap()
    x2s_h = nc.dram_tensor("x2s", [T, D], F32, kind="Internal").ap()

    es = ExitStack()
    arF = es.enter_context(nc.sbuf_tensor("arF", [128, F32SZ // 4], F32))
    arB = es.enter_context(nc.sbuf_tensor("arB", [128, BFSZ // 2], BF16))

    def V(off, parts, n, dtype=F32):
        isz = mybir.dt.size(dtype)
        nb = n * isz
        assert off % 64 == 0, off
        if dtype == F32:
            assert off + nb <= F32SZ, (off, nb)
            return arF[0:parts, off // 4:off // 4 + n]
        assert off + nb <= BFSZ, (off, nb)
        return arB[0:parts, off // 2:off // 2 + n]

    consts = V(0, 128, 8)
    det = V(64, 128, 1)
    det1m = V(128, 128, 1)
    K1 = V(192, 128, 1)
    ident = V(256, 128, 128)
    q_mx = V(768, 128, 8)
    q_sq = V(832, 128, 8)
    q_si = V(896, 128, 8)
    k_mx = V(960, 128, 1)
    k_nmx = V(1024, 128, 1)
    k_sk = V(1088, 128, 1)
    k_ski = V(1152, 128, 1)
    mv6 = V(1216, 128, 12)
    mv = V(1280, 128, 2)
    rstd = V(1344, 128, 1)
    negmr = V(1408, 128, 1)
    am = V(1472, 128, 128)
    am2 = V(1984, 128, 128)
    am3 = V(2496, 128, 128)
    sc_t = V(3008, 128, 128)
    iv_t = V(3520, 128, 128)
    K1S = V(4032, 128, 1)
    bo_bf = V(0, 1, D, BF16)
    bpj_bf = V(2048, 1, D, BF16)
    bfc_bf = V(4096, 1, J, BF16)
    ones1 = V(12288, 1, 128, BF16)
    c_sb = V(12608, 64, 512, BF16)

    pA = es.enter_context(nc.psum_tensor("pA", [128, 2048], F32))
    pB = es.enter_context(nc.psum_tensor("pB", [128, 1024], F32))
    pC = es.enter_context(nc.psum_tensor("pC", [128, 1024], F32))

    DMA_SEMS = ("c0", "bst", "st0", "st1", "st2", "xa0", "xa1",
                "xb0", "xb1", "ox0", "ox1", "sw0", "sw1", "sw2", "sw3", "sw4",
                "x20", "x21", "x22", "x23", "ou0", "ou1", "ou2", "ou3")
    for s in ("d2", "v", "s", "t", "g") + DMA_SEMS:
        p.sems[s] = es.enter_context(nc.semaphore(s))
    p.dma_sems.update(DMA_SEMS)

    def dma(dst, src, sem):
        p.op("sync", lambda E, d=dst, s=src: E.dma_start(out=d, in_=s),
             inc=(sem, 16))
        return p.cnt[sem]

    def gdma(dst, src, sem):
        p.op("gpsimd", lambda E, d=dst, s=src: E.dma_start(out=d, in_=s),
             inc=(sem, 16))
        return p.cnt[sem]

    def vop(f, drain=True):
        p.op("vector", f)
        if drain:
            p.drain("vector")

    smalls_free = [0]

    def ln_stats(src_a, src_b):
        vop(lambda E, a=src_a: E.bn_stats(out=mv6[:, 0:6], in_=a), drain=False)
        vop(lambda E, b2=src_b: E.bn_stats(out=mv6[:, 6:12], in_=b2))
        vop(lambda E: E.bn_aggr(out=mv, in_=mv6), drain=False)
        vt = p.tick("vector", "v")
        p.wait("scalar", "v", vt)
        p.op("scalar", lambda E: E.activation(out=rstd, in_=mv[:, 1:2],
                                              func=AF.Ln, bias=consts[:, 4:5]))
        p.drain("scalar")
        p.op("scalar", lambda E: E.activation(out=rstd, in_=rstd, func=AF.Exp,
                                              scale=consts[:, 5:6]),
             inc=("s", 1))
        p.wait("vector", "s", p.cnt["s"])
        if smalls_free[0]:
            p.wait("vector", "s", smalls_free[0])
        vop(lambda E: E.tensor_scalar(out=negmr, in0=mv[:, 0:1], scalar1=rstd,
                                      scalar2=-1.0, op0=ALU.mult,
                                      op1=ALU.mult), drain=False)
        return p.tick("vector", "v")

    def normalize(dst, src):
        p.op("scalar", lambda E, d=dst, s=src: E.activation(
            out=d, in_=s, func=AF.Identity, bias=negmr, scale=rstd),
            inc=("s", 1))
        smalls_free[0] = p.cnt["s"]
        return p.cnt["s"]

    def qs_a(nb, lo):
        vop(lambda E, nb=nb, lo=lo: E.tensor_scalar(
            out=am[:, lo:lo + nb], in0=am[:, lo:lo + nb], scalar1=2.0 ** -16,
            scalar2=None, op0=ALU.max), drain=False)
        vt = p.tick("vector", "v")
        p.wait("scalar", "v", vt)
        p.drain("scalar")
        p.op("scalar", lambda E, nb=nb, lo=lo: E.activation(
            out=am2[:, 0:nb], in_=am[:, lo:lo + nb], func=AF.Ln),
            inc=("s", 1))
        return p.cnt["s"]

    def qs_b(nb, lo, s_ln):
        p.wait("vector", "s", s_ln)
        vop(lambda E, nb=nb: E.tensor_scalar(out=am3[:, 0:nb],
                                             in0=am2[:, 0:nb],
                                             scalar1=consts[:, 6:7],
                                             scalar2=consts[:, 7:8],
                                             op0=ALU.mult, op1=ALU.add))
        vop(lambda E, nb=nb: E.tensor_scalar(out=am2[:, 0:nb],
                                             in0=am2[:, 0:nb],
                                             scalar1=IL2_HI, scalar2=3.5,
                                             op0=ALU.mult, op1=ALU.subtract))
        vop(lambda E, nb=nb: E.tensor_tensor(out=am2[:, 0:nb],
                                             in0=am2[:, 0:nb],
                                             in1=am3[:, 0:nb], op=ALU.add))
        vop(lambda E, nb=nb: E.tensor_scalar(out=am2[:, 0:nb],
                                             in0=am2[:, 0:nb], scalar1=K1,
                                             scalar2=K1, op0=ALU.add,
                                             op1=ALU.subtract))
        vop(lambda E, nb=nb: E.tensor_scalar(out=am2[:, 0:nb],
                                             in0=am2[:, 0:nb],
                                             scalar1=LN2_64, scalar2=None,
                                             op0=ALU.mult), drain=False)
        vt = p.tick("vector", "v")
        p.wait("scalar", "v", vt)
        p.op("scalar", lambda E, nb=nb, lo=lo: E.activation(
            out=sc_t[:, lo:lo + nb], in_=am2[:, 0:nb], func=AF.Exp),
            inc=("s", 1))
        return p.cnt["s"]

    def qs_c(nb, lo, s_exp):
        p.wait("vector", "s", s_exp)
        vop(lambda E, nb=nb, lo=lo: E.tensor_scalar(
            out=am2[:, 0:nb], in0=sc_t[:, lo:lo + nb], scalar1=K1S,
            scalar2=None, op0=ALU.mult))
        vop(lambda E, nb=nb, lo=lo: E.tensor_tensor(
            out=am3[:, 0:nb], in0=sc_t[:, lo:lo + nb], in1=am2[:, 0:nb],
            op=ALU.add))
        vop(lambda E, nb=nb, lo=lo: E.tensor_tensor(
            out=sc_t[:, lo:lo + nb], in0=am3[:, 0:nb], in1=am2[:, 0:nb],
            op=ALU.subtract))
        vop(lambda E, nb=nb, lo=lo: E.reciprocal(
            out=iv_t[:, lo:lo + nb], in_=sc_t[:, lo:lo + nb]), drain=False)
        return p.tick("vector", "v")

    def quant_scales(nb, lo=0, syncv=True):
        s_ln = qs_a(nb, lo)
        s_exp = qs_b(nb, lo, s_ln)
        vt2 = qs_c(nb, lo, s_exp)
        return ("v", vt2)

    def quant_chain(src3, nb, sc_lo, wbuf):
        w3 = wbuf.rearrange("p (b k) -> p b k", k=32)
        iv_b = iv_t[:, sc_lo:sc_lo + nb].unsqueeze(2).broadcast_to([128, nb, 32])
        sc_b = sc_t[:, sc_lo:sc_lo + nb].unsqueeze(2).broadcast_to([128, nb, 32])
        vop(lambda E, w3=w3, src3=src3, iv_b=iv_b: E.tensor_tensor(
            out=w3, in0=src3, in1=iv_b, op=ALU.mult))
        vop(lambda E, wbuf=wbuf: E.tensor_scalar(
            out=wbuf, in0=wbuf, scalar1=K1, scalar2=K1, op0=ALU.add,
            op1=ALU.subtract))
        vop(lambda E, w3=w3, sc_b=sc_b: E.tensor_tensor(
            out=w3, in0=w3, in1=sc_b, op=ALU.mult), drain=False)

    wq_sb = V(14 * KB, 128, KD * HID, BF16)
    wk_sb = V(22 * KB, 128, KD * HID, BF16)
    wv_sb = V(30 * KB, 128, KD * HID, BF16)
    eqtbA = V(38 * KB, 64, 4 * T, BF16)
    uT = V(54 * KB, 128, KD * T, BF16)
    eqtbB = V(86 * KB, 64, 4 * T, BF16)
    vb = V(102 * KB, 128, NT * HID, BF16)
    ektb = V(118 * KB, 128, NT * HID, BF16)
    wo_sb = V(144 * KB, 64, H * D, BF16)
    aob = V(102 * KB, 64, H * T, BF16)
    xa = [V(4 * KB + i * 4 * KB, 128, 1024) for i in range(2)]
    ut = [V(12 * KB + i * 4 * KB, 128, 1024) for i in range(2)]
    eqs = [V(20 * KB + i * 2 * KB, 128, 512) for i in range(2)]
    bias_stg = V(24 * KB, 1, J)
    ekw = V(4 * KB, 128, 2048)
    xb = [V(12 * KB + i * 4 * KB, 128, 1024) for i in range(2)]
    ox = [V(20 * KB + i * 4 * KB, 128, 1024) for i in range(2)]
    stg4 = [V(28 * KB + i * 4 * KB, 128, 1024) for i in range(2)]

    dma(consts, consts_h, "c0")
    d0 = dma(ident, ident_h, "c0")
    p.wait("vector", "c0", d0)
    vop(lambda E: E.memset(det, 1.0))
    vop(lambda E: E.tensor_scalar(out=det, in0=det, scalar1=EPS40, scalar2=1.0,
                                  op0=ALU.add, op1=ALU.subtract))
    vop(lambda E: E.tensor_scalar(out=det, in0=det, scalar1=T40, scalar2=None,
                                  op0=ALU.mult))
    vop(lambda E: E.tensor_scalar(out=det1m, in0=det, scalar1=-1.0, scalar2=1.0,
                                  op0=ALU.mult, op1=ALU.add))
    vop(lambda E: E.tensor_scalar(out=K1, in0=det1m, scalar1=C23, scalar2=None,
                                  op0=ALU.mult))
    vop(lambda E: E.scalar_tensor_tensor(out=K1, in0=det, scalar=C52, in1=K1,
                                         op0=ALU.mult, op1=ALU.add))
    vop(lambda E: E.tensor_scalar(out=K1S, in0=det1m, scalar1=C23,
                                  scalar2=None, op0=ALU.mult))
    vop(lambda E: E.memset(ones1, 1.0), drain=False)
    p.tick("vector", "v")

    def load_cast(dst_bf, src_h, parts, n):
        dv = dma(bias_stg[0:parts, 0:n], src_h, "bst")
        p.wait("scalar", "bst", dv)
        p.op("scalar", lambda E, d=dst_bf, s=bias_stg[0:parts, 0:n]:
             E.activation(out=d, in_=s, func=AF.Copy), inc=("s", 1))
        p.wait("sync", "s", p.cnt["s"])

    load_cast(bo_bf, bo_h, 1, D)
    load_cast(bpj_bf, bpj_h, 1, D)
    load_cast(bfc_bf, bfc_h, 1, J)

    si = [0]
    stg_free = [0, 0]
    bias_done = p.cnt["s"]
    p.wait("gpsimd", "s", bias_done)

    def stage_cast(src_ap, dst_bf, parts, n):
        b = si[0] % 2
        si[0] += 1
        if stg_free[b]:
            p.wait("gpsimd", "s", stg_free[b])
        dv = gdma(stg4[b][0:parts, 0:n], src_ap, "st%d" % b)
        p.wait("scalar", "st%d" % b, dv)
        p.op("scalar", lambda E, d=dst_bf, s=stg4[b][0:parts, 0:n]:
             E.activation(out=d, in_=s, func=AF.Copy), inc=("s", 1))
        stg_free[b] = p.cnt["s"]

    CAST_ITEMS = []
    for k in range(KD):
        CAST_ITEMS.append((wq_h[128 * k:128 * k + 128, :],
                           wq_sb[:, HID * k:HID * (k + 1)], 128, HID))
        CAST_ITEMS.append((wv_h[128 * k:128 * k + 128, :],
                           wv_sb[:, HID * k:HID * (k + 1)], 128, HID))
    for k in range(KD):
        CAST_ITEMS.append((wk_h[128 * k:128 * k + 128, :],
                           wk_sb[:, HID * k:HID * (k + 1)], 128, HID))
    for hh in range(H):
        CAST_ITEMS.append((wo8_h[:, D * hh:D * (hh + 1)],
                           wo_sb[0:64, D * hh:D * (hh + 1)], 64, 1024))
    qv_cast_s = [0]

    SH = 8
    xa_free = [0, 0]
    ut_free = [0, 0]
    pA_lo_free_s = [0]
    pA_free_list = []
    pq_free_s = [0]
    eqs_free_t = [0, 0]
    eqt_evac_s = [0]

    def ln_part(c):
        b = c % 2
        if xa_free[b]:
            p.wait("sync", "s", xa_free[b])
        dv = dma(xa[b], x_h[128 * c:128 * c + 128, :], "xa%d" % b)
        p.wait("vector", "xa%d" % b, dv)
        sv = ln_stats(xa[b][:, 0:512], xa[b][:, 512:1024])
        p.wait("scalar", "v", sv)
        if ut_free[b]:
            p.wait("scalar", "t", ut_free[b])
        xa_free[b] = normalize(ut[b], xa[b])
        p.wait("tensor", "s", p.cnt["s"])
        if pA_lo_free_s[0]:
            p.wait("tensor", "s", pA_lo_free_s[0])
        for k in range(KD):
            p.op("tensor", lambda E, k=k, b=b:
                 E.transpose(pA[:, 128 * k:128 * k + 128],
                             ut[b][:, 128 * k:128 * k + 128], ident),
                 inc=("t", 1) if k == KD - 1 else None)
        ut_free[b] = p.cnt["t"]
        p.wait("scalar", "t", p.cnt["t"])
        dst = uT[:, :].rearrange("p (k n) -> p k n", n=T)[:, :,
                                                          128 * c:128 * c + 128]
        p.op("scalar", lambda E, d=dst: E.activation(
            out=d, in_=pA[:, 0:1024].rearrange("p (k n) -> p k n", n=128),
            func=AF.Copy), inc=("s", 1))
        pA_lo_free_s[0] = p.cnt["s"]
        pA_free_list.append(p.cnt["s"])

    def qv_part(j):
        b = j % 2
        p.wait("tensor", "s", qv_cast_s[0])
        p.wait("tensor", "s", pA_free_list[j])
        if pq_free_s[0]:
            p.wait("tensor", "s", pq_free_s[0])
        for k in range(KD):
            lhs = uT[:, :].rearrange("p (k n) -> p k n", n=T)[
                :, k, 128 * j:128 * j + 128]
            p.op("tensor", lambda E, l=lhs, k=k:
                 E.matmul(pC[:, 0:512], l, wq_sb[:, HID * k:HID * (k + 1)],
                          start=(k == 0), stop=(k == KD - 1)))
            p.op("tensor", lambda E, l=lhs, k=k:
                 E.matmul(pC[:, 512:1024], l, wv_sb[:, HID * k:HID * (k + 1)],
                          start=(k == 0), stop=(k == KD - 1)),
                 inc=("t", 1) if k == KD - 1 else None)
        tq = p.cnt["t"]
        p.wait("scalar", "t", tq)
        p.op("scalar", lambda E, j=j:
             E.activation(out=vb[:, HID * j:HID * (j + 1)],
                          in_=pC[:, 512:1024], func=AF.Copy), inc=("s", 1))
        if eqs_free_t[b]:
            p.wait("scalar", "t", eqs_free_t[b])
        p.op("scalar", lambda E, b=b: E.activation(
            out=eqs[b], in_=pC[:, 0:512], func=AF.Exp), inc=("s", 1))
        pq_free_s[0] = p.cnt["s"]
        p.wait("vector", "s", pq_free_s[0])
        e3 = eqs[b].rearrange("p (h e) -> p h e", e=64)
        vop(lambda E, e3=e3: E.tensor_reduce(out=q_sq, in_=e3, axis=AX.X,
                                             op=ALU.add))
        vop(lambda E: E.reciprocal(out=q_si, in_=q_sq))
        vop(lambda E: E.tensor_scalar(out=q_si, in0=q_si, scalar1=ATTN_SCALE,
                                      scalar2=None, op0=ALU.mult))
        sib = q_si.unsqueeze(2).broadcast_to([128, 8, 64])
        vop(lambda E, e3=e3, sib=sib: E.tensor_tensor(
            out=e3, in0=e3, in1=sib, op=ALU.mult), drain=False)
        vq_t = p.tick("vector", "v")
        p.wait("tensor", "v", vq_t)
        if eqt_evac_s[0]:
            p.wait("tensor", "s", eqt_evac_s[0])
        for hh in range(H):
            src = eqs[b][:, 64 * hh:64 * (hh + 1)]
            p.op("tensor", lambda E, s=src, hh=hh:
                 E.transpose(pA[0:64, 1024 + 128 * hh:1024 + 128 * (hh + 1)],
                             s, ident),
                 inc=("t", 1) if hh == H - 1 else None)
        eqs_free_t[b] = p.cnt["t"]
        p.wait("scalar", "t", p.cnt["t"])
        dstA = eqtbA.rearrange("p (h n) -> p h n", n=T)[:, :,
                                                        128 * j:128 * j + 128]
        dstB = eqtbB.rearrange("p (h n) -> p h n", n=T)[:, :,
                                                        128 * j:128 * j + 128]
        p.op("scalar", lambda E, d=dstA: E.activation(
            out=d, in_=pA[0:64, 1024:1536].rearrange("p (h n) -> p h n",
                                                     n=128),
            func=AF.Copy), inc=("s", 1))
        p.op("scalar", lambda E, d=dstB: E.activation(
            out=d, in_=pA[0:64, 1536:2048].rearrange("p (h n) -> p h n",
                                                     n=128),
            func=AF.Copy), inc=("s", 1))
        eqt_evac_s[0] = p.cnt["s"]

    ci = [0]

    def emit_casts(n_):
        for _ in range(n_):
            if ci[0] < len(CAST_ITEMS):
                stage_cast(*CAST_ITEMS[ci[0]])
                ci[0] += 1
                if ci[0] == 16:
                    qv_cast_s[0] = p.cnt["s"]

    for c in range(NT):
        emit_casts(2)
        ln_part(c)
        if c >= SH:
            qv_part(c - SH)
    for j in range(NT - SH, NT):
        qv_part(j)

    pk_free_s = 0
    ekw_free_t = [0]
    ek_pbc_free_s = [0]
    for hc in range(NHC):
        if hc == 0:
            p.wait("tensor", "s", eqt_evac_s[0])
            p.wait("tensor", "s", pq_free_s[0])
        if pk_free_s:
            p.wait("tensor", "s", pk_free_s)
        for k in range(KD):
            lhs = wk_sb[:, HID * k + 128 * hc:HID * k + 128 * (hc + 1)]
            for ns in range(4):
                mov = uT[:, :].rearrange("p (k n) -> p k n", n=T)[
                    :, k, 512 * ns:512 * (ns + 1)]
                p.op("tensor", lambda E, l=lhs, m=mov, ns=ns, k=k:
                     E.matmul(pA[:, 512 * ns:512 * (ns + 1)], l, m,
                              start=(k == 0), stop=(k == KD - 1)),
                     inc=("t", 1) if (k == KD - 1 and ns == 3) else None)
        tk = p.cnt["t"]
        p.wait("scalar", "t", tk)
        if ekw_free_t[0]:
            p.wait("scalar", "t", ekw_free_t[0])
        p.op("scalar", lambda E: E.activation(
            out=ekw[:, 0:1024], in_=pA[:, 0:1024], func=AF.Exp,
            accum_out=k_sk))
        p.drain("scalar")
        p.op("scalar", lambda E: E.activation(
            out=ekw[:, 1024:2048], in_=pA[:, 1024:2048],
            func=AF.Exp, accum_out=k_ski), inc=("s", 1))
        pk_free_s = p.cnt["s"]
        p.wait("vector", "s", p.cnt["s"])
        vop(lambda E: E.tensor_tensor(out=k_sk, in0=k_sk, in1=k_ski,
                                      op=ALU.add))
        vop(lambda E: E.reciprocal(out=k_ski, in_=k_sk))
        vop(lambda E: E.tensor_scalar(
            out=ekw[:, 0:2048], in0=ekw[:, 0:2048],
            scalar1=k_ski, scalar2=None, op0=ALU.mult), drain=False)
        ekvt = p.tick("vector", "v")
        p.wait("tensor", "v", ekvt)
        if ek_pbc_free_s[0]:
            p.wait("tensor", "s", ek_pbc_free_s[0])
        for i in range(8):
            p.op("tensor", lambda E, i=i:
                 E.transpose(pB[:, 128 * i:128 * (i + 1)],
                             ekw[:, 128 * i:128 * (i + 1)], ident))
        for i in range(8):
            p.op("tensor", lambda E, i=i:
                 E.transpose(pC[:, 128 * i:128 * (i + 1)],
                             ekw[:, 1024 + 128 * i:1024 + 128 * (i + 1)],
                             ident),
                 inc=("t", 1) if i == 7 else None)
        ekw_free_t[0] = p.cnt["t"]
        p.wait("scalar", "t", p.cnt["t"])
        d1 = ektb.rearrange("p (c x) -> p c x", x=HID)[
            :, 0:8, 128 * hc:128 * (hc + 1)]
        d2 = ektb.rearrange("p (c x) -> p c x", x=HID)[
            :, 8:16, 128 * hc:128 * (hc + 1)]
        p.op("scalar", lambda E, d=d1: E.activation(
            out=d, in_=pB[:, 0:1024].rearrange("p (i n) -> p i n", n=128),
            func=AF.Copy), inc=("s", 1))
        p.op("scalar", lambda E, d=d2: E.activation(
            out=d, in_=pC[:, 0:1024].rearrange("p (i n) -> p i n", n=128),
            func=AF.Copy), inc=("s", 1))
        ek_pbc_free_s[0] = p.cnt["s"]

    p.barrier()

    for hh in range(H):
        for c in range(NT):
            lhs = ektb[:, HID * c + 64 * hh:HID * c + 64 * (hh + 1)]
            mov = vb[:, HID * c + 64 * hh:HID * c + 64 * (hh + 1)]
            p.op("tensor", lambda E, l=lhs, m=mov, c=c, hh=hh:
                 E.matmul(pA[0:64, 64 * hh:64 * (hh + 1)], l, m,
                          start=(c == 0), stop=(c == NT - 1)),
                 inc=("t", 1) if (c == NT - 1 and hh == H - 1) else None)
    p.wait("scalar", "t", p.cnt["t"])
    p.op("scalar", lambda E: E.activation(out=c_sb, in_=pA[0:64, 0:512],
                                          func=AF.Copy), inc=("s", 1))
    p.wait("tensor", "s", p.cnt["s"])
    po_free = [0, 0]
    pi = 0
    for hh in range(H):
        for ns in range(4):
            if po_free[pi % 2]:
                p.wait("tensor", "s", po_free[pi % 2])
            lhs = c_sb[0:64, 64 * hh:64 * (hh + 1)]
            arr = eqtbA if hh < 4 else eqtbB
            mov = arr[0:64, T * (hh % 4) + 512 * ns:
                      T * (hh % 4) + 512 * (ns + 1)]
            reg = pB[0:64, 512 * (pi % 2):512 * (pi % 2) + 512]
            p.op("tensor", lambda E, l=lhs, m=mov, r=reg:
                 E.matmul(r, l, m, start=True, stop=True), inc=("t", 1))
            p.wait("scalar", "t", p.cnt["t"])
            p.op("scalar", lambda E, hh=hh, ns=ns, r=reg: E.activation(
                out=aob[0:64, T * hh + 512 * ns:T * hh + 512 * (ns + 1)],
                in_=r, func=AF.Copy), inc=("s", 1))
            po_free[pi % 2] = p.cnt["s"]
            pi += 1

    p.wait("tensor", "s", p.cnt["s"])
    p.wait("sync", "t", p.cnt["t"])
    ox_free = [0, 0]
    xb_free = [0, 0]
    pwo_free = 0
    for c in range(NT):
        b = c % 2
        if xb_free[b]:
            p.wait("sync", "v", xb_free[b])
        dv = dma(xb[b], x_h[128 * c:128 * c + 128, :], "xb%d" % b)
        if pwo_free:
            p.wait("tensor", "s", pwo_free)
        for hh in range(H):
            lhs = aob[0:64, T * hh + 128 * c:T * hh + 128 * (c + 1)]
            for kk in range(2):
                p.op("tensor", lambda E, l=lhs, hh=hh, kk=kk:
                     E.matmul(pC[:, 512 * kk:512 * (kk + 1)], l,
                              wo_sb[0:64, D * hh + 512 * kk:
                                    D * hh + 512 * (kk + 1)],
                              start=(hh == 0), stop=False))
        for kk in range(2):
            p.op("tensor", lambda E, kk=kk:
                 E.matmul(pC[:, 512 * kk:512 * (kk + 1)], ones1,
                          bo_bf[0:1, 512 * kk:512 * (kk + 1)],
                          start=False, stop=True),
                 inc=("t", 1) if kk == 1 else None)
        two = p.cnt["t"]
        p.wait("vector", "t", two)
        sv = ln_stats(pC[:, 0:512], pC[:, 512:1024])
        p.wait("scalar", "v", sv)
        if ox_free[b]:
            p.wait("scalar", "ox%d" % b, ox_free[b])
        pwo_free = normalize(ox[b], pC[:, 0:1024])
        p.wait("vector", "s", p.cnt["s"])
        p.wait("vector", "xb%d" % b, dv)
        vop(lambda E, b=b: E.tensor_tensor(out=ox[b], in0=ox[b], in1=xb[b],
                                           op=ALU.add), drain=False)
        vt = p.tick("vector", "v")
        xb_free[b] = vt
        p.wait("sync", "v", vt)
        ox_free[b] = dma(x2s_h[128 * c:128 * c + 128, :], ox[b], "ox%d" % b)

    p.barrier()

    stgW = [V(4 * KB + 8 * KB * i, 128, 2048) for i in range(5)]
    wfcq = V(14 * KB, 128, KD * J, BF16)
    wpjq = V(78 * KB, 128, NJ * D, BF16)

    stgW_free = [0, 0, 0, 0, 0]
    pB_free2 = [None]

    WORK = [(wfc_h[128 * jc:128 * (jc + 1), :], 1024, 32,
             [wfcq[:, J * k + 128 * jc:J * k + 128 * (jc + 1)]
              for k in range(KD)]) for jc in range(NJ)]
    WORK += [(wpj_h[128 * (idx // 2):128 * (idx // 2) + 128,
                    2048 * (idx % 2):2048 * (idx % 2 + 1)], 2048, 64,
              [wpjq[:, D * (16 * (idx % 2) + t) + 128 * (idx // 2):
                    D * (16 * (idx % 2) + t) + 128 * (idx // 2) + 128]
               for t in range(16)]) for idx in range(16)]

    def w_load(i):
        src_h_ap, n, nb, _ = WORK[i]
        b = i % 5
        if stgW_free[b]:
            p.wait("gpsimd", "t", stgW_free[b])
        gdma(stgW[b][:, 0:n], src_h_ap, "sw%d" % b)

    def w_amax_pair(pair):
        tot = 0
        for j in (2 * pair, 2 * pair + 1):
            _, n, nb, _ = WORK[j]
            b = j % 5
            p.wait("vector", "sw%d" % b, p.cnt["sw%d" % b])
            s3 = stgW[b][:, 0:n].rearrange("p (b k) -> p b k", k=32)
            vop(lambda E, s3=s3, nb=nb, lo=tot: E.tensor_reduce(
                out=am[:, lo:lo + nb], in_=s3, axis=AX.X, op=ALU.max,
                apply_absolute_value=True), drain=(j % 2 == 1))
            tot += nb
        return tot, qs_a(tot, 0)

    def w_chain(i, lo):
        _, n, nb, dst_tiles = WORK[i]
        b = i % 5
        s3 = stgW[b][:, 0:n].rearrange("p (b k) -> p b k", k=32)
        quant_chain(s3, nb, lo, stgW[b][:, 0:n])
        return p.tick("vector", "v")

    def w_te(i, vq):
        _, n, nb, dst_tiles = WORK[i]
        b = i % 5
        ntile = n // 128
        for base in range(0, ntile, 8):
            cnt = min(8, ntile - base)
            p.wait("tensor", "v", vq)
            if pB_free2[0]:
                p.wait("tensor", pB_free2[0][0], pB_free2[0][1])
            for i2 in range(cnt):
                src2 = stgW[b][:, 128 * (base + i2):128 * (base + i2 + 1)]
                p.op("tensor", lambda E, s=src2, i=i2:
                     E.transpose(pB[:, 128 * i:128 * (i + 1)], s, ident),
                     inc=("t", 1) if i2 == cnt - 1 else None)
            stgW_free[b] = p.cnt["t"]
            p.wait("scalar", "t", p.cnt["t"])
            for i2 in range(cnt):
                p.op("scalar", lambda E, d=dst_tiles[base + i2], i=i2:
                     E.activation(out=d, in_=pB[:, 128 * i:128 * (i + 1)],
                                  func=AF.Copy),
                     inc=("s", 1) if i2 == cnt - 1 else None)
            pB_free2[0] = ("s", p.cnt["s"])

    NP = len(WORK) // 2
    for j in range(4):
        w_load(j)
    tot0, sln0 = w_amax_pair(0)
    sexp0 = qs_b(tot0, 0, sln0)
    qs_c(tot0, 0, sexp0)
    for P in range(NP):
        i0, i1 = 2 * P, 2 * P + 1
        nxt = P + 1 < NP
        if 2 * P + 4 < len(WORK):
            w_load(2 * P + 4)
        if nxt:
            tot1, sln1 = w_amax_pair(P + 1)
        vq0 = w_chain(i0, 0)
        vq1 = w_chain(i1, WORK[i0][2])
        if nxt:
            sexp1 = qs_b(tot1, 0, sln1)
        w_te(i0, vq0)
        if 2 * P + 5 < len(WORK):
            w_load(2 * P + 5)
        if nxt:
            qs_c(tot1, 0, sexp1)
        w_te(i1, vq1)

    p.barrier()

    x2tS = [V(4 * KB, 128, 1024), V(8 * KB, 128, 1024),
            V(32 * KB, 128, 1024), V(36 * KB, 128, 1024)]
    w1 = V(12 * KB, 128, 2048)
    w2 = V(20 * KB, 128, 2048)
    u2f = V(28 * KB, 128, 1024)
    tmpn = V(12 * KB, 128, 1024)
    u2qtS = [V(142 * KB, 128, 1024, BF16),
             V(0, 128, 1024, BF16)]
    gqtS = [V(144 * KB, 128, NJ * 128, BF16),
            V(152 * KB, 128, NJ * 128, BF16)]

    x2t_free = [0, 0, 0, 0]
    u2_t_free = [0]
    u2qt_sfree = [0, 0]
    w1_t_free = [0]
    w2_t_free = [0]
    gqt_read_t = [0, 0]
    normY_s = [0]

    def u2path(j):
        b4 = j % 4
        if x2t_free[b4]:
            p.wait("sync", "ou%d" % b4, x2t_free[b4])
        dv = dma(x2tS[b4], x2s_h[128 * j:128 * j + 128, :], "x2%d" % b4)
        p.wait("vector", "x2%d" % b4, dv)
        sv = ln_stats(x2tS[b4][:, 0:512], x2tS[b4][:, 512:1024])
        p.wait("scalar", "v", sv)
        if u2_t_free[0]:
            p.wait("scalar", "t", u2_t_free[0])
        normalize(u2f, x2tS[b4])
        p.wait("vector", "s", p.cnt["s"])
        u3 = u2f[:, :].rearrange("p (b k) -> p b k", k=32)
        vop(lambda E, u3=u3: E.tensor_reduce(out=am[:, 0:32], in_=u3,
                                             axis=AX.X, op=ALU.max,
                                             apply_absolute_value=True))
        quant_scales(32)
        quant_chain(u3, 32, 0, u2f)
        return p.tick("vector", "v")

    def transp_u2(j, u2v, preg, gate_s):
        p.wait("tensor", "v", u2v)
        if gate_s:
            p.wait("tensor", "s", gate_s)
        for k in range(KD):
            p.op("tensor", lambda E, k=k, preg=preg:
                 E.transpose(preg[:, 128 * k:128 * k + 128],
                             u2f[:, 128 * k:128 * k + 128], ident),
                 inc=("t", 1) if k == KD - 1 else None)
        u2_t_free[0] = p.cnt["t"]
        p.wait("scalar", "t", p.cnt["t"])
        p.op("scalar", lambda E, j=j, preg=preg: E.activation(
            out=u2qtS[j % 2], in_=preg[:, 0:1024], func=AF.Copy),
            inc=("s", 1))
        u2qt_sfree[j % 2] = p.cnt["s"]

    def fc_half(j, half, gates):
        u2l = u2qtS[j % 2]
        for g_s in gates:
            if g_s:
                p.wait("tensor", "s", g_s)
        for g in range(2):
            if j % 2 == 0:
                pz = pA[:, 1024 * g:1024 * (g + 1)]
            else:
                pz = (pB if g == 0 else pC)[:, 0:1024]
            base = 2048 * half + 1024 * g
            for k in range(KD):
                lhs = u2l[:, 128 * k:128 * (k + 1)]
                for js in range(2):
                    mov = wfcq[:, J * k + base + 512 * js:
                               J * k + base + 512 * (js + 1)]
                    p.op("tensor", lambda E, l=lhs, m=mov, pz=pz, js=js, k=k:
                         E.matmul(pz[:, 512 * js:512 * (js + 1)], l, m,
                                  start=(k == 0), stop=False))
            for js in range(2):
                mov = bfc_bf[0:1, base + 512 * js:base + 512 * (js + 1)]
                p.op("tensor", lambda E, m=mov, pz=pz, js=js:
                     E.matmul(pz[:, 512 * js:512 * (js + 1)], ones1, m,
                              start=False, stop=True),
                     inc=("t", 1) if js == 1 else None)
        return p.cnt["t"]

    def gelu_amax(j, tfc):
        lo = 64 * (j % 2)
        p.wait("scalar", "t", tfc)
        if j % 2 == 0:
            p.op("scalar", lambda E: E.activation(
                out=pA[:, 0:2048], in_=pA[:, 0:2048], func=AF.Gelu),
                inc=("s", 1))
            sg = p.cnt["s"]
            p.wait("vector", "s", sg)
            vop(lambda E, lo=lo: E.tensor_reduce(
                out=am[:, lo:lo + 64],
                in_=pA[:, 0:2048].rearrange("p (b k) -> p b k", k=32),
                axis=AX.X, op=ALU.max, apply_absolute_value=True))
        else:
            p.op("scalar", lambda E: E.activation(
                out=pB[:, 0:1024], in_=pB[:, 0:1024], func=AF.Gelu),
                inc=("s", 1))
            sgb = p.cnt["s"]
            p.op("scalar", lambda E: E.activation(
                out=pC[:, 0:1024], in_=pC[:, 0:1024], func=AF.Gelu),
                inc=("s", 1))
            sgc = p.cnt["s"]
            p.wait("vector", "s", sgb)
            vop(lambda E, lo=lo: E.tensor_reduce(
                out=am[:, lo:lo + 32],
                in_=pB[:, 0:1024].rearrange("p (b k) -> p b k", k=32),
                axis=AX.X, op=ALU.max, apply_absolute_value=True),
                drain=False)
            p.wait("vector", "s", sgc)
            vop(lambda E, lo=lo: E.tensor_reduce(
                out=am[:, lo + 32:lo + 64],
                in_=pC[:, 0:1024].rearrange("p (b k) -> p b k", k=32),
                axis=AX.X, op=ALU.max, apply_absolute_value=True))

    def chain_half(j, wbuf, wfree_t):
        lo = 64 * (j % 2)
        if wfree_t[0]:
            p.wait("vector", "t", wfree_t[0])
        if j % 2 == 0:
            vop(lambda E, wbuf=wbuf, lo=lo: E.tensor_tensor(
                out=wbuf[:, 0:2048].rearrange("p (b k) -> p b k", k=32),
                in0=pA[:, 0:2048].rearrange("p (b k) -> p b k", k=32),
                in1=iv_t[:, lo:lo + 64].unsqueeze(2).broadcast_to(
                    [128, 64, 32]),
                op=ALU.mult), drain=False)
        else:
            vop(lambda E, wbuf=wbuf, lo=lo: E.tensor_tensor(
                out=wbuf[:, 0:1024].rearrange("p (b k) -> p b k", k=32),
                in0=pB[:, 0:1024].rearrange("p (b k) -> p b k", k=32),
                in1=iv_t[:, lo:lo + 32].unsqueeze(2).broadcast_to(
                    [128, 32, 32]),
                op=ALU.mult), drain=False)
            vop(lambda E, wbuf=wbuf, lo=lo: E.tensor_tensor(
                out=wbuf[:, 1024:2048].rearrange("p (b k) -> p b k", k=32),
                in0=pC[:, 0:1024].rearrange("p (b k) -> p b k", k=32),
                in1=iv_t[:, lo + 32:lo + 64].unsqueeze(2).broadcast_to(
                    [128, 32, 32]),
                op=ALU.mult), drain=False)
        p.drain("vector")
        vop(lambda E, wbuf=wbuf: E.tensor_scalar(
            out=wbuf[:, 0:2048], in0=wbuf[:, 0:2048], scalar1=K1, scalar2=K1,
            op0=ALU.add, op1=ALU.subtract))
        vop(lambda E, wbuf=wbuf, lo=lo: E.tensor_tensor(
            out=wbuf[:, 0:2048].rearrange("p (b k) -> p b k", k=32),
            in0=wbuf[:, 0:2048].rearrange("p (b k) -> p b k", k=32),
            in1=sc_t[:, lo:lo + 64].unsqueeze(2).broadcast_to([128, 64, 32]),
            op=ALU.mult), drain=False)
        return p.tick("vector", "v")

    def tr_evac(j, half, wbuf, wfree_t, vq):
        p.wait("tensor", "v", vq)
        for i in range(16):
            if j % 2 == 0:
                tgt = pA[:, 128 * i:128 * (i + 1)]
            elif i < 8:
                tgt = pB[:, 128 * i:128 * (i + 1)]
            else:
                tgt = pC[:, 128 * (i - 8):128 * (i - 7)]
            p.op("tensor", lambda E, i=i, tgt=tgt, wbuf=wbuf:
                 E.transpose(tgt, wbuf[:, 128 * i:128 * (i + 1)], ident),
                 inc=("t", 1) if i == 15 else None)
        wfree_t[0] = p.cnt["t"]
        p.wait("scalar", "t", p.cnt["t"])
        if gqt_read_t[j % 2]:
            p.wait("scalar", "t", gqt_read_t[j % 2])
        gq = gqtS[j % 2]
        if j % 2 == 0:
            p.op("scalar", lambda E, gq=gq, half=half: E.activation(
                out=gq[:, 2048 * half:2048 * (half + 1)], in_=pA[:, 0:2048],
                func=AF.Copy), inc=("s", 1))
        else:
            p.op("scalar", lambda E, gq=gq, half=half: E.activation(
                out=gq[:, 2048 * half:2048 * half + 1024], in_=pB[:, 0:1024],
                func=AF.Copy), inc=("s", 1))
            p.op("scalar", lambda E, gq=gq, half=half: E.activation(
                out=gq[:, 2048 * half + 1024:2048 * (half + 1)],
                in_=pC[:, 0:1024], func=AF.Copy), inc=("s", 1))
        return p.cnt["s"]

    def proj(j, preg, gates):
        gq = gqtS[j % 2]
        for g_s in gates:
            if g_s:
                p.wait("tensor", "s", g_s)
        for jc in range(NJ):
            lhs = gq[:, 128 * jc:128 * (jc + 1)]
            for kk in range(2):
                mov = wpjq[:, D * jc + 512 * kk:D * jc + 512 * (kk + 1)]
                p.op("tensor", lambda E, l=lhs, m=mov, kk=kk, jc=jc,
                     preg=preg:
                     E.matmul(preg[:, 512 * kk:512 * (kk + 1)], l, m,
                              start=(jc == 0), stop=False))
        for kk in range(2):
            p.op("tensor", lambda E, kk=kk, preg=preg:
                 E.matmul(preg[:, 512 * kk:512 * (kk + 1)], ones1,
                          bpj_bf[0:1, 512 * kk:512 * (kk + 1)],
                          start=False, stop=True),
                 inc=("t", 1) if kk == 1 else None)
        gqt_read_t[j % 2] = p.cnt["t"]
        return p.cnt["t"]

    def postln(j, preg, tpj):
        b4 = j % 4
        p.wait("vector", "t", tpj)
        sv = ln_stats(preg[:, 0:512], preg[:, 512:1024])
        p.wait("scalar", "v", sv)
        p.wait("scalar", "t", w1_t_free[0])
        ns = normalize(tmpn, preg[:, 0:1024])
        p.wait("vector", "s", p.cnt["s"])
        vop(lambda E, b4=b4: E.tensor_tensor(out=x2tS[b4], in0=tmpn,
                                             in1=x2tS[b4], op=ALU.add),
            drain=False)
        vo = p.tick("vector", "v")
        p.wait("sync", "v", vo)
        x2t_free[b4] = dma(out_h[128 * j:128 * j + 128, :], x2tS[b4],
                           "ou%d" % b4)
        return ns

    uv = u2path(0)
    transp_u2(0, uv, pA, 0)
    uv = u2path(1)
    transp_u2(1, uv, pB, 0)

    for X in range(0, NT, 2):
        Y = X + 1
        tfaX = fc_half(X, 0, [u2qt_sfree[0], normY_s[0]])
        tfaY = fc_half(Y, 0, [u2qt_sfree[1], normY_s[0]])
        gelu_amax(X, tfaX)
        gelu_amax(Y, tfaY)
        quant_scales(128)
        vqAX = chain_half(X, w1, w1_t_free)
        vqAY = chain_half(Y, w2, w2_t_free)
        eAX = tr_evac(X, 0, w1, w1_t_free, vqAX)
        eAY = tr_evac(Y, 0, w2, w2_t_free, vqAY)
        tfbX = fc_half(X, 1, [eAX])
        tfbY = fc_half(Y, 1, [eAY])
        gelu_amax(X, tfbX)
        gelu_amax(Y, tfbY)
        quant_scales(128)
        vqBX = chain_half(X, w1, w1_t_free)
        vqBY = chain_half(Y, w2, w2_t_free)
        eBX = tr_evac(X, 1, w1, w1_t_free, vqBX)
        eBY = tr_evac(Y, 1, w2, w2_t_free, vqBY)
        if X + 2 < NT:
            uv2 = u2path(X + 2)
        tpjX = proj(X, pC, [eBX, eBY])
        if X + 2 < NT:
            transp_u2(X + 2, uv2, pA, eBX)
        nsX = postln(X, pC, tpjX)
        if X + 3 < NT:
            uv3 = u2path(X + 3)
        tpjY = proj(Y, pA, [u2qt_sfree[0]])
        if X + 3 < NT:
            transp_u2(X + 3, uv3, pB, eBY)
        normY_s[0] = postln(Y, pA, tpjY)

    for b4 in range(4):
        p.wait("sync", "ou%d" % b4, p.cnt["ou%d" % b4])

    with nc.Block() as blk:
        @blk.sync
        def _(E):
            for f in p.steps["sync"]:
                f(E)

        @blk.vector
        def _(E):
            for f in p.steps["vector"]:
                f(E)

        @blk.scalar
        def _(E):
            for f in p.steps["scalar"]:
                f(E)

        @blk.tensor
        def _(E):
            for f in p.steps["tensor"]:
                f(E)

        @blk.gpsimd
        def _(E):
            for f in p.steps["gpsimd"]:
                f(E)

    es.close()
    return nc


_NC_CACHE = None


def kernel(L0_attn_x, L0_attn_Wq, L0_attn_Wk, L0_attn_Wv, L0_attn_Wo,
           L0_attn_bo, L0_ffn_w_fc, L0_ffn_b_fc, L0_ffn_w_proj, L0_ffn_b_proj):
    global _NC_CACHE
    x = np.ascontiguousarray(np.asarray(L0_attn_x, dtype=np.float32))
    wq = np.ascontiguousarray(np.asarray(L0_attn_Wq, dtype=np.float32))
    wk = np.ascontiguousarray(np.asarray(L0_attn_Wk, dtype=np.float32))
    wv = np.ascontiguousarray(np.asarray(L0_attn_Wv, dtype=np.float32))
    wo = np.asarray(L0_attn_Wo, dtype=np.float32)
    bo = np.ascontiguousarray(
        np.asarray(L0_attn_bo, dtype=np.float32).reshape(1, D))
    wfc = np.ascontiguousarray(np.asarray(L0_ffn_w_fc, dtype=np.float32))
    bfc = np.ascontiguousarray(
        np.asarray(L0_ffn_b_fc, dtype=np.float32).reshape(1, J))
    wpj = np.ascontiguousarray(np.asarray(L0_ffn_w_proj, dtype=np.float32))
    bpj = np.ascontiguousarray(
        np.asarray(L0_ffn_b_proj, dtype=np.float32).reshape(1, D))
    wo8 = np.ascontiguousarray(
        wo.reshape(H, DH, D).transpose(1, 0, 2).reshape(DH, H * D))
    ident = np.eye(128, dtype=np.float32)
    consts = np.ascontiguousarray(np.tile(np.array(
        [LN2, -3.0 * LN2, -LN2, 3.0 * LN2, LN_EPS, -0.5, IL2_LO, FLOOR_DELTA],
        dtype=np.float32), (128, 1)))

    if _NC_CACHE is None:
        _NC_CACHE = build_nc()
    nc = _NC_CACHE

    in_maps = []
    for i in range(B):
        in_maps.append(dict(
            x=np.ascontiguousarray(x[i]), wq=wq, wk=wk, wv=wv, wo8=wo8, bo=bo,
            wfc=wfc, bfc=bfc, wpj=wpj, bpj=bpj, ident=ident, consts=consts))
    res = run_bass_kernel_spmd(nc, in_maps, core_ids=list(range(B)))
    out = np.stack([np.asarray(res.results[i]["out"]) for i in range(B)],
                   axis=0)
    return out


# version 17
# speedup vs baseline: 1.1248x; 1.0290x over previous
import math
from collections import Counter
from contextlib import ExitStack

import numpy as np
import ml_dtypes
import concourse.bass as bass
import concourse.mybir as mybir
from concourse.bass_utils import run_bass_kernel_spmd

F32 = mybir.dt.float32
BF16 = mybir.dt.bfloat16
U8 = mybir.dt.uint8
AF = mybir.ActivationFunctionType
ALU = mybir.AluOpType
AX = mybir.AxisListType

B, T, D, HID, H, DH, J = 8, 2048, 1024, 512, 8, 64, 4096
NT, KD, NHC, NJ = 16, 8, 4, 32

C23 = float(np.float32(1.5 * 2.0 ** 23))
IL2_HI = float(np.float32(1.0 / math.log(2.0)))
IL2_LO = float(np.float32(1.0 / math.log(2.0) - np.float32(1.0 / math.log(2.0))))
L2_HI = float(np.float32(math.log(2.0)))
L2_LO = float(np.float32(math.log(2.0) - np.float32(math.log(2.0))))
C52 = 1.5 * 2.0 ** 52
LN2_64 = float(math.log(2.0))
EPS40 = float(np.float32(2.0 ** -40))
T40 = float(np.float32(2.0 ** 40))
LN2 = float(np.float32(math.log(2.0)))
INVLN2 = float(np.float32(1.0 / math.log(2.0)))
FLOOR_DELTA = 4e-8
ATTN_SCALE = 0.125
LN_EPS = 1e-5
KB = 1024
F32SZ = 44 * KB
BFSZ = 160 * KB

ENGINES = ("sync", "vector", "scalar", "tensor", "gpsimd")
ESEM = {"sync": "d2", "vector": "v", "scalar": "s", "tensor": "t",
        "gpsimd": "g"}


class Plan:
    def __init__(self):
        self.steps = {e: [] for e in ENGINES}
        self.cnt = Counter()
        self.sems = {}
        self.dma_sems = set()

    def op(self, eng, f, inc=None):
        if inc is not None:
            self.cnt[inc[0]] += inc[1]

        def run(E, f=f, inc=inc, sems=self.sems):
            r = f(E)
            if inc is not None:
                r.then_inc(sems[inc[0]], inc[1])
        self.steps[eng].append(run)

    def wait(self, eng, sem, val=None):
        if val is None:
            val = self.cnt[sem]
        if val <= 0:
            return
        self.steps[eng].append(
            lambda E, s=sem, v=val, sems=self.sems: E.wait_ge(sems[s], v))

    def tick(self, eng, sem):
        self.cnt[sem] += 1

        def run(E, s=sem, sems=self.sems):
            E.drain()
            E.sem_inc(sems[s], 1)
        self.steps[eng].append(run)
        return self.cnt[sem]

    def drain(self, eng):
        self.steps[eng].append(lambda E: E.drain())

    def barrier(self):
        vals = {}
        for e in ("vector", "scalar", "tensor", "gpsimd"):
            vals[ESEM[e]] = self.tick(e, ESEM[e])
        for ds in sorted(self.dma_sems):
            self.wait("sync", ds, self.cnt[ds])
        vals["d2"] = self.tick("sync", "d2")
        for e in ENGINES:
            for s, v in vals.items():
                if s != ESEM[e]:
                    self.wait(e, s, v)


def build_nc():
    nc = bass.Bass("TRN2", target_bir_lowering=False, debug=False)
    p = Plan()

    x_h = nc.dram_tensor("x", [T, D], F32, kind="ExternalInput").ap()
    wq_h = nc.dram_tensor("wq", [D, HID], F32, kind="ExternalInput").ap()
    wk_h = nc.dram_tensor("wk", [D, HID], F32, kind="ExternalInput").ap()
    wv_h = nc.dram_tensor("wv", [D, HID], F32, kind="ExternalInput").ap()
    wo8_h = nc.dram_tensor("wo8", [DH, H * D], F32, kind="ExternalInput").ap()
    bo_h = nc.dram_tensor("bo", [1, D], F32, kind="ExternalInput").ap()
    wfc_h = nc.dram_tensor("wfc", [J, D], F32, kind="ExternalInput").ap()
    bfc_h = nc.dram_tensor("bfc", [1, J], F32, kind="ExternalInput").ap()
    wpj_h = nc.dram_tensor("wpj", [D, J], F32, kind="ExternalInput").ap()
    bpj_h = nc.dram_tensor("bpj", [1, D], F32, kind="ExternalInput").ap()
    ident_h = nc.dram_tensor("ident", [128, 128], F32, kind="ExternalInput").ap()
    consts_h = nc.dram_tensor("consts", [128, 8], F32, kind="ExternalInput").ap()
    out_h = nc.dram_tensor("out", [T, D], F32, kind="ExternalOutput").ap()
    x2s_h = nc.dram_tensor("x2s", [T, D], F32, kind="Internal").ap()

    es = ExitStack()
    arF = es.enter_context(nc.sbuf_tensor("arF", [128, F32SZ // 4], F32))
    arB = es.enter_context(nc.sbuf_tensor("arB", [128, BFSZ // 2], BF16))

    def V(off, parts, n, dtype=F32):
        isz = mybir.dt.size(dtype)
        nb = n * isz
        assert off % 64 == 0, off
        if dtype == F32:
            assert off + nb <= F32SZ, (off, nb)
            return arF[0:parts, off // 4:off // 4 + n]
        assert off + nb <= BFSZ, (off, nb)
        return arB[0:parts, off // 2:off // 2 + n]

    consts = V(0, 128, 8)
    det = V(64, 128, 1)
    det1m = V(128, 128, 1)
    K1 = V(192, 128, 1)
    ident = V(256, 128, 128)
    q_mx = V(768, 128, 8)
    q_sq = V(832, 128, 8)
    q_si = V(896, 128, 8)
    k_mx = V(960, 128, 1)
    k_nmx = V(1024, 128, 1)
    k_sk = V(1088, 128, 1)
    k_ski = V(1152, 128, 1)
    mv6 = V(1216, 128, 12)
    mv = V(1280, 128, 2)
    rstd = V(1344, 128, 1)
    negmr = V(1408, 128, 1)
    am = V(1472, 128, 128)
    am2 = V(1984, 128, 128)
    am3 = V(2496, 128, 128)
    sc_t = V(3008, 128, 128)
    iv_t = V(3520, 128, 128)
    K1S = V(4032, 128, 1)
    bo_bf = V(0, 1, D, BF16)
    bpj_bf = V(2048, 1, D, BF16)
    bfc_bf = V(4096, 1, J, BF16)
    ones1 = V(12288, 1, 128, BF16)
    c_sb = V(12608, 64, 512, BF16)

    pA = es.enter_context(nc.psum_tensor("pA", [128, 2048], F32))
    pB = es.enter_context(nc.psum_tensor("pB", [128, 1024], F32))
    pC = es.enter_context(nc.psum_tensor("pC", [128, 1024], F32))

    DMA_SEMS = ("c0", "bst", "st0", "st1", "st2", "xa0", "xa1",
                "xb0", "xb1", "ox0", "ox1", "sw0", "sw1", "sw2", "sw3", "sw4",
                "x20", "x21", "x22", "x23", "ou0", "ou1", "ou2", "ou3")
    for s in ("d2", "v", "s", "t", "g") + DMA_SEMS:
        p.sems[s] = es.enter_context(nc.semaphore(s))
    p.dma_sems.update(DMA_SEMS)

    def dma(dst, src, sem):
        p.op("sync", lambda E, d=dst, s=src: E.dma_start(out=d, in_=s),
             inc=(sem, 16))
        return p.cnt[sem]

    def gdma(dst, src, sem):
        p.op("gpsimd", lambda E, d=dst, s=src: E.dma_start(out=d, in_=s),
             inc=(sem, 16))
        return p.cnt[sem]

    def vop(f, drain=True):
        p.op("vector", f)
        if drain:
            p.drain("vector")

    smalls_free = [0]

    def ln_stats(src_a, src_b):
        vop(lambda E, a=src_a: E.bn_stats(out=mv6[:, 0:6], in_=a), drain=False)
        vop(lambda E, b2=src_b: E.bn_stats(out=mv6[:, 6:12], in_=b2))
        vop(lambda E: E.bn_aggr(out=mv, in_=mv6), drain=False)
        vt = p.tick("vector", "v")
        p.wait("scalar", "v", vt)
        p.op("scalar", lambda E: E.activation(out=rstd, in_=mv[:, 1:2],
                                              func=AF.Ln, bias=consts[:, 4:5]))
        p.drain("scalar")
        p.op("scalar", lambda E: E.activation(out=rstd, in_=rstd, func=AF.Exp,
                                              scale=consts[:, 5:6]),
             inc=("s", 1))
        p.wait("vector", "s", p.cnt["s"])
        if smalls_free[0]:
            p.wait("vector", "s", smalls_free[0])
        vop(lambda E: E.tensor_scalar(out=negmr, in0=mv[:, 0:1], scalar1=rstd,
                                      scalar2=-1.0, op0=ALU.mult,
                                      op1=ALU.mult), drain=False)
        return p.tick("vector", "v")

    def normalize(dst, src):
        p.op("scalar", lambda E, d=dst, s=src: E.activation(
            out=d, in_=s, func=AF.Identity, bias=negmr, scale=rstd),
            inc=("s", 1))
        smalls_free[0] = p.cnt["s"]
        return p.cnt["s"]

    def qs_a(nb, lo):
        vop(lambda E, nb=nb, lo=lo: E.tensor_scalar(
            out=am[:, lo:lo + nb], in0=am[:, lo:lo + nb], scalar1=2.0 ** -16,
            scalar2=None, op0=ALU.max), drain=False)
        vt = p.tick("vector", "v")
        p.wait("scalar", "v", vt)
        p.drain("scalar")
        p.op("scalar", lambda E, nb=nb, lo=lo: E.activation(
            out=am2[:, 0:nb], in_=am[:, lo:lo + nb], func=AF.Ln),
            inc=("s", 1))
        return p.cnt["s"]

    def qs_b(nb, lo, s_ln):
        p.wait("vector", "s", s_ln)
        vop(lambda E, nb=nb: E.tensor_scalar(out=am3[:, 0:nb],
                                             in0=am2[:, 0:nb],
                                             scalar1=consts[:, 6:7],
                                             scalar2=consts[:, 7:8],
                                             op0=ALU.mult, op1=ALU.add))
        vop(lambda E, nb=nb: E.tensor_scalar(out=am2[:, 0:nb],
                                             in0=am2[:, 0:nb],
                                             scalar1=IL2_HI, scalar2=3.5,
                                             op0=ALU.mult, op1=ALU.subtract))
        vop(lambda E, nb=nb: E.tensor_tensor(out=am2[:, 0:nb],
                                             in0=am2[:, 0:nb],
                                             in1=am3[:, 0:nb], op=ALU.add))
        vop(lambda E, nb=nb: E.tensor_scalar(out=am2[:, 0:nb],
                                             in0=am2[:, 0:nb], scalar1=K1,
                                             scalar2=K1, op0=ALU.add,
                                             op1=ALU.subtract))
        vop(lambda E, nb=nb: E.tensor_scalar(out=am2[:, 0:nb],
                                             in0=am2[:, 0:nb],
                                             scalar1=LN2_64, scalar2=None,
                                             op0=ALU.mult), drain=False)
        vt = p.tick("vector", "v")
        p.wait("scalar", "v", vt)
        p.op("scalar", lambda E, nb=nb, lo=lo: E.activation(
            out=sc_t[:, lo:lo + nb], in_=am2[:, 0:nb], func=AF.Exp),
            inc=("s", 1))
        return p.cnt["s"]

    def qs_c(nb, lo, s_exp):
        p.wait("vector", "s", s_exp)
        vop(lambda E, nb=nb, lo=lo: E.tensor_scalar(
            out=am2[:, 0:nb], in0=sc_t[:, lo:lo + nb], scalar1=K1S,
            scalar2=None, op0=ALU.mult))
        vop(lambda E, nb=nb, lo=lo: E.tensor_tensor(
            out=am3[:, 0:nb], in0=sc_t[:, lo:lo + nb], in1=am2[:, 0:nb],
            op=ALU.add))
        vop(lambda E, nb=nb, lo=lo: E.tensor_tensor(
            out=sc_t[:, lo:lo + nb], in0=am3[:, 0:nb], in1=am2[:, 0:nb],
            op=ALU.subtract))
        vop(lambda E, nb=nb, lo=lo: E.reciprocal(
            out=iv_t[:, lo:lo + nb], in_=sc_t[:, lo:lo + nb]), drain=False)
        return p.tick("vector", "v")

    def quant_scales(nb, lo=0, syncv=True):
        s_ln = qs_a(nb, lo)
        s_exp = qs_b(nb, lo, s_ln)
        vt2 = qs_c(nb, lo, s_exp)
        return ("v", vt2)

    def quant_chain(src3, nb, sc_lo, wbuf):
        w3 = wbuf.rearrange("p (b k) -> p b k", k=32)
        iv_b = iv_t[:, sc_lo:sc_lo + nb].unsqueeze(2).broadcast_to([128, nb, 32])
        sc_b = sc_t[:, sc_lo:sc_lo + nb].unsqueeze(2).broadcast_to([128, nb, 32])
        vop(lambda E, w3=w3, src3=src3, iv_b=iv_b: E.tensor_tensor(
            out=w3, in0=src3, in1=iv_b, op=ALU.mult))
        vop(lambda E, wbuf=wbuf: E.tensor_scalar(
            out=wbuf, in0=wbuf, scalar1=K1, scalar2=K1, op0=ALU.add,
            op1=ALU.subtract))
        vop(lambda E, w3=w3, sc_b=sc_b: E.tensor_tensor(
            out=w3, in0=w3, in1=sc_b, op=ALU.mult), drain=False)

    wq_sb = V(14 * KB, 128, KD * HID, BF16)
    wk_sb = V(22 * KB, 128, KD * HID, BF16)
    wv_sb = V(30 * KB, 128, KD * HID, BF16)
    eqtbA = V(38 * KB, 64, 4 * T, BF16)
    uT = V(54 * KB, 128, KD * T, BF16)
    eqtbB = V(86 * KB, 64, 4 * T, BF16)
    vb = V(102 * KB, 128, NT * HID, BF16)
    ektb = V(118 * KB, 128, NT * HID, BF16)
    wo_sb = V(144 * KB, 64, H * D, BF16)
    aob = V(102 * KB, 64, H * T, BF16)
    xa = [V(4 * KB + i * 4 * KB, 128, 1024) for i in range(2)]
    ut = [V(12 * KB + i * 4 * KB, 128, 1024) for i in range(2)]
    eqs = [V(20 * KB + i * 2 * KB, 128, 512) for i in range(2)]
    bias_stg = V(24 * KB, 1, J)
    ekw = V(4 * KB, 128, 2048)
    xb = [V(12 * KB + i * 4 * KB, 128, 1024) for i in range(2)]
    ox = [V(20 * KB + i * 4 * KB, 128, 1024) for i in range(2)]
    stg4 = [V(28 * KB + i * 4 * KB, 128, 1024) for i in range(2)]

    dma(consts, consts_h, "c0")
    d0 = dma(ident, ident_h, "c0")
    p.wait("vector", "c0", d0)
    vop(lambda E: E.memset(det, 1.0))
    vop(lambda E: E.tensor_scalar(out=det, in0=det, scalar1=EPS40, scalar2=1.0,
                                  op0=ALU.add, op1=ALU.subtract))
    vop(lambda E: E.tensor_scalar(out=det, in0=det, scalar1=T40, scalar2=None,
                                  op0=ALU.mult))
    vop(lambda E: E.tensor_scalar(out=det1m, in0=det, scalar1=-1.0, scalar2=1.0,
                                  op0=ALU.mult, op1=ALU.add))
    vop(lambda E: E.tensor_scalar(out=K1, in0=det1m, scalar1=C23, scalar2=None,
                                  op0=ALU.mult))
    vop(lambda E: E.scalar_tensor_tensor(out=K1, in0=det, scalar=C52, in1=K1,
                                         op0=ALU.mult, op1=ALU.add))
    vop(lambda E: E.tensor_scalar(out=K1S, in0=det1m, scalar1=C23,
                                  scalar2=None, op0=ALU.mult))
    vop(lambda E: E.memset(ones1, 1.0), drain=False)
    p.tick("vector", "v")

    def load_cast(dst_bf, src_h, parts, n):
        dv = dma(bias_stg[0:parts, 0:n], src_h, "bst")
        p.wait("scalar", "bst", dv)
        p.op("scalar", lambda E, d=dst_bf, s=bias_stg[0:parts, 0:n]:
             E.activation(out=d, in_=s, func=AF.Copy), inc=("s", 1))
        p.wait("sync", "s", p.cnt["s"])

    load_cast(bo_bf, bo_h, 1, D)
    load_cast(bpj_bf, bpj_h, 1, D)
    load_cast(bfc_bf, bfc_h, 1, J)

    si = [0]
    stg_free = [0, 0]
    bias_done = p.cnt["s"]
    p.wait("gpsimd", "s", bias_done)

    def stage_cast(src_ap, dst_bf, parts, n):
        b = si[0] % 2
        si[0] += 1
        if stg_free[b]:
            p.wait("gpsimd", "s", stg_free[b])
        dv = gdma(stg4[b][0:parts, 0:n], src_ap, "st%d" % b)
        p.wait("scalar", "st%d" % b, dv)
        p.op("scalar", lambda E, d=dst_bf, s=stg4[b][0:parts, 0:n]:
             E.activation(out=d, in_=s, func=AF.Copy), inc=("s", 1))
        stg_free[b] = p.cnt["s"]

    CAST_ITEMS = []
    for k in range(KD):
        CAST_ITEMS.append((wq_h[128 * k:128 * k + 128, :],
                           wq_sb[:, HID * k:HID * (k + 1)], 128, HID))
        CAST_ITEMS.append((wv_h[128 * k:128 * k + 128, :],
                           wv_sb[:, HID * k:HID * (k + 1)], 128, HID))
    for k in range(KD):
        CAST_ITEMS.append((wk_h[128 * k:128 * k + 128, :],
                           wk_sb[:, HID * k:HID * (k + 1)], 128, HID))
    for hh in range(H):
        CAST_ITEMS.append((wo8_h[:, D * hh:D * (hh + 1)],
                           wo_sb[0:64, D * hh:D * (hh + 1)], 64, 1024))
    qv_cast_s = [0]

    SH = 8
    xa_free = [0, 0]
    ut_free = [0, 0]
    pA_lo_free_s = [0]
    pA_free_list = []
    pq_free_s = [0]
    eqs_free_t = [0, 0]
    eqt_evac_s = [0]

    def ln_part(c):
        b = c % 2
        if xa_free[b]:
            p.wait("sync", "s", xa_free[b])
        dv = dma(xa[b], x_h[128 * c:128 * c + 128, :], "xa%d" % b)
        p.wait("vector", "xa%d" % b, dv)
        sv = ln_stats(xa[b][:, 0:512], xa[b][:, 512:1024])
        p.wait("scalar", "v", sv)
        if ut_free[b]:
            p.wait("scalar", "t", ut_free[b])
        xa_free[b] = normalize(ut[b], xa[b])
        p.wait("tensor", "s", p.cnt["s"])
        if pA_lo_free_s[0]:
            p.wait("tensor", "s", pA_lo_free_s[0])
        for k in range(KD):
            p.op("tensor", lambda E, k=k, b=b:
                 E.transpose(pA[:, 128 * k:128 * k + 128],
                             ut[b][:, 128 * k:128 * k + 128], ident),
                 inc=("t", 1) if k == KD - 1 else None)
        ut_free[b] = p.cnt["t"]
        p.wait("scalar", "t", p.cnt["t"])
        dst = uT[:, :].rearrange("p (k n) -> p k n", n=T)[:, :,
                                                          128 * c:128 * c + 128]
        p.op("scalar", lambda E, d=dst: E.activation(
            out=d, in_=pA[:, 0:1024].rearrange("p (k n) -> p k n", n=128),
            func=AF.Copy), inc=("s", 1))
        pA_lo_free_s[0] = p.cnt["s"]
        pA_free_list.append(p.cnt["s"])

    def qv_part(j):
        b = j % 2
        p.wait("tensor", "s", qv_cast_s[0])
        p.wait("tensor", "s", pA_free_list[j])
        if pq_free_s[0]:
            p.wait("tensor", "s", pq_free_s[0])
        for k in range(KD):
            lhs = uT[:, :].rearrange("p (k n) -> p k n", n=T)[
                :, k, 128 * j:128 * j + 128]
            p.op("tensor", lambda E, l=lhs, k=k:
                 E.matmul(pC[:, 0:512], l, wq_sb[:, HID * k:HID * (k + 1)],
                          start=(k == 0), stop=(k == KD - 1)))
            p.op("tensor", lambda E, l=lhs, k=k:
                 E.matmul(pC[:, 512:1024], l, wv_sb[:, HID * k:HID * (k + 1)],
                          start=(k == 0), stop=(k == KD - 1)),
                 inc=("t", 1) if k == KD - 1 else None)
        tq = p.cnt["t"]
        p.wait("scalar", "t", tq)
        p.op("scalar", lambda E, j=j:
             E.activation(out=vb[:, HID * j:HID * (j + 1)],
                          in_=pC[:, 512:1024], func=AF.Copy), inc=("s", 1))
        if eqs_free_t[b]:
            p.wait("scalar", "t", eqs_free_t[b])
        p.op("scalar", lambda E, b=b: E.activation(
            out=eqs[b], in_=pC[:, 0:512], func=AF.Exp), inc=("s", 1))
        pq_free_s[0] = p.cnt["s"]
        p.wait("vector", "s", pq_free_s[0])
        e3 = eqs[b].rearrange("p (h e) -> p h e", e=64)
        vop(lambda E, e3=e3: E.tensor_reduce(out=q_sq, in_=e3, axis=AX.X,
                                             op=ALU.add))
        vop(lambda E: E.reciprocal(out=q_si, in_=q_sq))
        vop(lambda E: E.tensor_scalar(out=q_si, in0=q_si, scalar1=ATTN_SCALE,
                                      scalar2=None, op0=ALU.mult))
        sib = q_si.unsqueeze(2).broadcast_to([128, 8, 64])
        vop(lambda E, e3=e3, sib=sib: E.tensor_tensor(
            out=e3, in0=e3, in1=sib, op=ALU.mult), drain=False)
        vq_t = p.tick("vector", "v")
        p.wait("tensor", "v", vq_t)
        if eqt_evac_s[0]:
            p.wait("tensor", "s", eqt_evac_s[0])
        for hh in range(H):
            src = eqs[b][:, 64 * hh:64 * (hh + 1)]
            p.op("tensor", lambda E, s=src, hh=hh:
                 E.transpose(pA[0:64, 1024 + 128 * hh:1024 + 128 * (hh + 1)],
                             s, ident),
                 inc=("t", 1) if hh == H - 1 else None)
        eqs_free_t[b] = p.cnt["t"]
        p.wait("scalar", "t", p.cnt["t"])
        dstA = eqtbA.rearrange("p (h n) -> p h n", n=T)[:, :,
                                                        128 * j:128 * j + 128]
        dstB = eqtbB.rearrange("p (h n) -> p h n", n=T)[:, :,
                                                        128 * j:128 * j + 128]
        p.op("scalar", lambda E, d=dstA: E.activation(
            out=d, in_=pA[0:64, 1024:1536].rearrange("p (h n) -> p h n",
                                                     n=128),
            func=AF.Copy), inc=("s", 1))
        p.op("scalar", lambda E, d=dstB: E.activation(
            out=d, in_=pA[0:64, 1536:2048].rearrange("p (h n) -> p h n",
                                                     n=128),
            func=AF.Copy), inc=("s", 1))
        eqt_evac_s[0] = p.cnt["s"]

    ci = [0]

    def emit_casts(n_):
        for _ in range(n_):
            if ci[0] < len(CAST_ITEMS):
                stage_cast(*CAST_ITEMS[ci[0]])
                ci[0] += 1
                if ci[0] == 16:
                    qv_cast_s[0] = p.cnt["s"]

    for c in range(NT):
        emit_casts(2)
        ln_part(c)
        if c >= SH:
            qv_part(c - SH)
    for j in range(NT - SH, NT):
        qv_part(j)

    pk_free_s = 0
    ekw_free_t = [0]
    ek_pbc_free_s = [0]
    for hc in range(NHC):
        if hc == 0:
            p.wait("tensor", "s", eqt_evac_s[0])
            p.wait("tensor", "s", pq_free_s[0])
        if pk_free_s:
            p.wait("tensor", "s", pk_free_s)
        for k in range(KD):
            lhs = wk_sb[:, HID * k + 128 * hc:HID * k + 128 * (hc + 1)]
            for ns in range(4):
                mov = uT[:, :].rearrange("p (k n) -> p k n", n=T)[
                    :, k, 512 * ns:512 * (ns + 1)]
                p.op("tensor", lambda E, l=lhs, m=mov, ns=ns, k=k:
                     E.matmul(pA[:, 512 * ns:512 * (ns + 1)], l, m,
                              start=(k == 0), stop=(k == KD - 1)),
                     inc=("t", 1) if (k == KD - 1 and ns == 3) else None)
        tk = p.cnt["t"]
        p.wait("scalar", "t", tk)
        if ekw_free_t[0]:
            p.wait("scalar", "t", ekw_free_t[0])
        p.op("scalar", lambda E: E.activation(
            out=ekw[:, 0:1024], in_=pA[:, 0:1024], func=AF.Exp,
            accum_out=k_sk))
        p.drain("scalar")
        p.op("scalar", lambda E: E.activation(
            out=ekw[:, 1024:2048], in_=pA[:, 1024:2048],
            func=AF.Exp, accum_out=k_ski), inc=("s", 1))
        pk_free_s = p.cnt["s"]
        p.wait("vector", "s", p.cnt["s"])
        vop(lambda E: E.tensor_tensor(out=k_sk, in0=k_sk, in1=k_ski,
                                      op=ALU.add))
        vop(lambda E: E.reciprocal(out=k_ski, in_=k_sk))
        vop(lambda E: E.tensor_scalar(
            out=ekw[:, 0:2048], in0=ekw[:, 0:2048],
            scalar1=k_ski, scalar2=None, op0=ALU.mult), drain=False)
        ekvt = p.tick("vector", "v")
        p.wait("tensor", "v", ekvt)
        if ek_pbc_free_s[0]:
            p.wait("tensor", "s", ek_pbc_free_s[0])
        for i in range(8):
            p.op("tensor", lambda E, i=i:
                 E.transpose(pB[:, 128 * i:128 * (i + 1)],
                             ekw[:, 128 * i:128 * (i + 1)], ident))
        for i in range(8):
            p.op("tensor", lambda E, i=i:
                 E.transpose(pC[:, 128 * i:128 * (i + 1)],
                             ekw[:, 1024 + 128 * i:1024 + 128 * (i + 1)],
                             ident),
                 inc=("t", 1) if i == 7 else None)
        ekw_free_t[0] = p.cnt["t"]
        p.wait("scalar", "t", p.cnt["t"])
        d1 = ektb.rearrange("p (c x) -> p c x", x=HID)[
            :, 0:8, 128 * hc:128 * (hc + 1)]
        d2 = ektb.rearrange("p (c x) -> p c x", x=HID)[
            :, 8:16, 128 * hc:128 * (hc + 1)]
        p.op("scalar", lambda E, d=d1: E.activation(
            out=d, in_=pB[:, 0:1024].rearrange("p (i n) -> p i n", n=128),
            func=AF.Copy), inc=("s", 1))
        p.op("scalar", lambda E, d=d2: E.activation(
            out=d, in_=pC[:, 0:1024].rearrange("p (i n) -> p i n", n=128),
            func=AF.Copy), inc=("s", 1))
        ek_pbc_free_s[0] = p.cnt["s"]

    p.barrier()

    for hh in range(H):
        for c in range(NT):
            lhs = ektb[:, HID * c + 64 * hh:HID * c + 64 * (hh + 1)]
            mov = vb[:, HID * c + 64 * hh:HID * c + 64 * (hh + 1)]
            p.op("tensor", lambda E, l=lhs, m=mov, c=c, hh=hh:
                 E.matmul(pA[0:64, 64 * hh:64 * (hh + 1)], l, m,
                          start=(c == 0), stop=(c == NT - 1)),
                 inc=("t", 1) if (c == NT - 1 and hh == H - 1) else None)
    p.wait("scalar", "t", p.cnt["t"])
    p.op("scalar", lambda E: E.activation(out=c_sb, in_=pA[0:64, 0:512],
                                          func=AF.Copy), inc=("s", 1))
    p.wait("tensor", "s", p.cnt["s"])
    po_free = [0, 0]
    pi = 0
    for hh in range(H):
        for ns in range(4):
            if po_free[pi % 2]:
                p.wait("tensor", "s", po_free[pi % 2])
            lhs = c_sb[0:64, 64 * hh:64 * (hh + 1)]
            arr = eqtbA if hh < 4 else eqtbB
            mov = arr[0:64, T * (hh % 4) + 512 * ns:
                      T * (hh % 4) + 512 * (ns + 1)]
            reg = pB[0:64, 512 * (pi % 2):512 * (pi % 2) + 512]
            p.op("tensor", lambda E, l=lhs, m=mov, r=reg:
                 E.matmul(r, l, m, start=True, stop=True), inc=("t", 1))
            p.wait("scalar", "t", p.cnt["t"])
            p.op("scalar", lambda E, hh=hh, ns=ns, r=reg: E.activation(
                out=aob[0:64, T * hh + 512 * ns:T * hh + 512 * (ns + 1)],
                in_=r, func=AF.Copy), inc=("s", 1))
            po_free[pi % 2] = p.cnt["s"]
            pi += 1

    p.wait("tensor", "s", p.cnt["s"])
    p.wait("sync", "t", p.cnt["t"])
    ox_free = [0, 0]
    xb_free = [0, 0]
    pwo_free = 0
    for c in range(NT):
        b = c % 2
        if xb_free[b]:
            p.wait("sync", "v", xb_free[b])
        dv = dma(xb[b], x_h[128 * c:128 * c + 128, :], "xb%d" % b)
        if pwo_free:
            p.wait("tensor", "s", pwo_free)
        for hh in range(H):
            lhs = aob[0:64, T * hh + 128 * c:T * hh + 128 * (c + 1)]
            for kk in range(2):
                p.op("tensor", lambda E, l=lhs, hh=hh, kk=kk:
                     E.matmul(pC[:, 512 * kk:512 * (kk + 1)], l,
                              wo_sb[0:64, D * hh + 512 * kk:
                                    D * hh + 512 * (kk + 1)],
                              start=(hh == 0), stop=False))
        for kk in range(2):
            p.op("tensor", lambda E, kk=kk:
                 E.matmul(pC[:, 512 * kk:512 * (kk + 1)], ones1,
                          bo_bf[0:1, 512 * kk:512 * (kk + 1)],
                          start=False, stop=True),
                 inc=("t", 1) if kk == 1 else None)
        two = p.cnt["t"]
        p.wait("vector", "t", two)
        sv = ln_stats(pC[:, 0:512], pC[:, 512:1024])
        p.wait("scalar", "v", sv)
        if ox_free[b]:
            p.wait("scalar", "ox%d" % b, ox_free[b])
        pwo_free = normalize(ox[b], pC[:, 0:1024])
        p.wait("vector", "s", p.cnt["s"])
        p.wait("vector", "xb%d" % b, dv)
        vop(lambda E, b=b: E.tensor_tensor(out=ox[b], in0=ox[b], in1=xb[b],
                                           op=ALU.add), drain=False)
        vt = p.tick("vector", "v")
        xb_free[b] = vt
        p.wait("sync", "v", vt)
        ox_free[b] = dma(x2s_h[128 * c:128 * c + 128, :], ox[b], "ox%d" % b)

    p.barrier()

    stgW = [V(4 * KB + 8 * KB * i, 128, 2048) for i in range(5)]
    wfcq = V(14 * KB, 128, KD * J, BF16)
    wpjq = V(78 * KB, 128, NJ * D, BF16)

    stgW_free = [0, 0, 0, 0, 0]
    pB_free2 = [None]

    WORK = [(wfc_h[128 * jc:128 * (jc + 1), :], 1024, 32,
             [wfcq[:, J * k + 128 * jc:J * k + 128 * (jc + 1)]
              for k in range(KD)]) for jc in range(NJ)]
    WORK += [(wpj_h[128 * (idx // 2):128 * (idx // 2) + 128,
                    2048 * (idx % 2):2048 * (idx % 2 + 1)], 2048, 64,
              [wpjq[:, D * (16 * (idx % 2) + t) + 128 * (idx // 2):
                    D * (16 * (idx % 2) + t) + 128 * (idx // 2) + 128]
               for t in range(16)]) for idx in range(16)]

    def w_load(i):
        src_h_ap, n, nb, _ = WORK[i]
        b = i % 5
        if stgW_free[b]:
            p.wait("gpsimd", "t", stgW_free[b])
        gdma(stgW[b][:, 0:n], src_h_ap, "sw%d" % b)

    def w_amax_pair(pair):
        tot = 0
        for j in (2 * pair, 2 * pair + 1):
            _, n, nb, _ = WORK[j]
            b = j % 5
            p.wait("vector", "sw%d" % b, p.cnt["sw%d" % b])
            s3 = stgW[b][:, 0:n].rearrange("p (b k) -> p b k", k=32)
            vop(lambda E, s3=s3, nb=nb, lo=tot: E.tensor_reduce(
                out=am[:, lo:lo + nb], in_=s3, axis=AX.X, op=ALU.max,
                apply_absolute_value=True), drain=(j % 2 == 1))
            tot += nb
        return tot, qs_a(tot, 0)

    def w_chain(i, lo):
        _, n, nb, dst_tiles = WORK[i]
        b = i % 5
        s3 = stgW[b][:, 0:n].rearrange("p (b k) -> p b k", k=32)
        quant_chain(s3, nb, lo, stgW[b][:, 0:n])
        return p.tick("vector", "v")

    def w_te(i, vq):
        _, n, nb, dst_tiles = WORK[i]
        b = i % 5
        ntile = n // 128
        for base in range(0, ntile, 8):
            cnt = min(8, ntile - base)
            p.wait("tensor", "v", vq)
            if pB_free2[0]:
                p.wait("tensor", pB_free2[0][0], pB_free2[0][1])
            for i2 in range(cnt):
                src2 = stgW[b][:, 128 * (base + i2):128 * (base + i2 + 1)]
                p.op("tensor", lambda E, s=src2, i=i2:
                     E.transpose(pB[:, 128 * i:128 * (i + 1)], s, ident),
                     inc=("t", 1) if i2 == cnt - 1 else None)
            stgW_free[b] = p.cnt["t"]
            p.wait("scalar", "t", p.cnt["t"])
            for i2 in range(cnt):
                p.op("scalar", lambda E, d=dst_tiles[base + i2], i=i2:
                     E.activation(out=d, in_=pB[:, 128 * i:128 * (i + 1)],
                                  func=AF.Copy),
                     inc=("s", 1) if i2 == cnt - 1 else None)
            pB_free2[0] = ("s", p.cnt["s"])

    NP = len(WORK) // 2
    for j in range(4):
        w_load(j)
    tot0, sln0 = w_amax_pair(0)
    sexp0 = qs_b(tot0, 0, sln0)
    qs_c(tot0, 0, sexp0)
    for P in range(NP):
        i0, i1 = 2 * P, 2 * P + 1
        nxt = P + 1 < NP
        if 2 * P + 4 < len(WORK):
            w_load(2 * P + 4)
        if nxt:
            tot1, sln1 = w_amax_pair(P + 1)
        vq0 = w_chain(i0, 0)
        vq1 = w_chain(i1, WORK[i0][2])
        if nxt:
            sexp1 = qs_b(tot1, 0, sln1)
        w_te(i0, vq0)
        if 2 * P + 5 < len(WORK):
            w_load(2 * P + 5)
        if nxt:
            qs_c(tot1, 0, sexp1)
        w_te(i1, vq1)

    p.barrier()

    x2tS = [V(4 * KB, 128, 1024), V(8 * KB, 128, 1024),
            V(32 * KB, 128, 1024), V(36 * KB, 128, 1024)]
    w1 = V(12 * KB, 128, 2048)
    w2 = V(20 * KB, 128, 2048)
    u2f = V(28 * KB, 128, 1024)
    tmpn = V(12 * KB, 128, 1024)
    u2qtS = [V(142 * KB, 128, 1024, BF16),
             V(0, 128, 1024, BF16)]
    gqtS = [V(144 * KB, 128, NJ * 128, BF16),
            V(152 * KB, 128, NJ * 128, BF16)]

    x2t_free = [0, 0, 0, 0]
    u2_t_free = [0]
    u2qt_sfree = [0, 0]
    w1_t_free = [0]
    w2_t_free = [0]
    gqt_read_t = [0, 0]
    normY_s = [0]

    def u2path(j):
        b4 = j % 4
        if x2t_free[b4]:
            p.wait("sync", "ou%d" % b4, x2t_free[b4])
        dv = dma(x2tS[b4], x2s_h[128 * j:128 * j + 128, :], "x2%d" % b4)
        p.wait("vector", "x2%d" % b4, dv)
        sv = ln_stats(x2tS[b4][:, 0:512], x2tS[b4][:, 512:1024])
        p.wait("scalar", "v", sv)
        if u2_t_free[0]:
            p.wait("scalar", "t", u2_t_free[0])
        normalize(u2f, x2tS[b4])
        p.wait("vector", "s", p.cnt["s"])
        u3 = u2f[:, :].rearrange("p (b k) -> p b k", k=32)
        vop(lambda E, u3=u3: E.tensor_reduce(out=am[:, 0:32], in_=u3,
                                             axis=AX.X, op=ALU.max,
                                             apply_absolute_value=True))
        quant_scales(32)
        quant_chain(u3, 32, 0, u2f)
        return p.tick("vector", "v")

    def transp_u2(j, u2v, preg, gate_s):
        p.wait("tensor", "v", u2v)
        if gate_s:
            p.wait("tensor", "s", gate_s)
        for k in range(KD):
            p.op("tensor", lambda E, k=k, preg=preg:
                 E.transpose(preg[:, 128 * k:128 * k + 128],
                             u2f[:, 128 * k:128 * k + 128], ident),
                 inc=("t", 1) if k == KD - 1 else None)
        u2_t_free[0] = p.cnt["t"]
        p.wait("scalar", "t", p.cnt["t"])
        p.op("scalar", lambda E, j=j, preg=preg: E.activation(
            out=u2qtS[j % 2], in_=preg[:, 0:1024], func=AF.Copy),
            inc=("s", 1))
        u2qt_sfree[j % 2] = p.cnt["s"]

    def fc_half(j, half, gates):
        u2l = u2qtS[j % 2]
        for g_s in gates:
            if g_s:
                p.wait("tensor", "s", g_s)
        for g in range(2):
            if j % 2 == 0:
                pz = pA[:, 1024 * g:1024 * (g + 1)]
            else:
                pz = (pB if g == 0 else pC)[:, 0:1024]
            base = 2048 * half + 1024 * g
            for k in range(KD):
                lhs = u2l[:, 128 * k:128 * (k + 1)]
                for js in range(2):
                    mov = wfcq[:, J * k + base + 512 * js:
                               J * k + base + 512 * (js + 1)]
                    p.op("tensor", lambda E, l=lhs, m=mov, pz=pz, js=js, k=k:
                         E.matmul(pz[:, 512 * js:512 * (js + 1)], l, m,
                                  start=(k == 0), stop=False))
            for js in range(2):
                mov = bfc_bf[0:1, base + 512 * js:base + 512 * (js + 1)]
                p.op("tensor", lambda E, m=mov, pz=pz, js=js:
                     E.matmul(pz[:, 512 * js:512 * (js + 1)], ones1, m,
                              start=False, stop=True),
                     inc=("t", 1) if js == 1 else None)
        return p.cnt["t"]

    def gelu_amax(j, tfc):
        lo = 64 * (j % 2)
        p.wait("scalar", "t", tfc)
        if j % 2 == 0:
            p.op("scalar", lambda E: E.activation(
                out=pA[:, 0:2048], in_=pA[:, 0:2048], func=AF.Gelu),
                inc=("s", 1))
            sg = p.cnt["s"]
            p.wait("vector", "s", sg)
            vop(lambda E, lo=lo: E.tensor_reduce(
                out=am[:, lo:lo + 64],
                in_=pA[:, 0:2048].rearrange("p (b k) -> p b k", k=32),
                axis=AX.X, op=ALU.max, apply_absolute_value=True))
        else:
            p.op("scalar", lambda E: E.activation(
                out=pB[:, 0:1024], in_=pB[:, 0:1024], func=AF.Gelu),
                inc=("s", 1))
            sgb = p.cnt["s"]
            p.op("scalar", lambda E: E.activation(
                out=pC[:, 0:1024], in_=pC[:, 0:1024], func=AF.Gelu),
                inc=("s", 1))
            sgc = p.cnt["s"]
            p.wait("vector", "s", sgb)
            vop(lambda E, lo=lo: E.tensor_reduce(
                out=am[:, lo:lo + 32],
                in_=pB[:, 0:1024].rearrange("p (b k) -> p b k", k=32),
                axis=AX.X, op=ALU.max, apply_absolute_value=True),
                drain=False)
            p.wait("vector", "s", sgc)
            vop(lambda E, lo=lo: E.tensor_reduce(
                out=am[:, lo + 32:lo + 64],
                in_=pC[:, 0:1024].rearrange("p (b k) -> p b k", k=32),
                axis=AX.X, op=ALU.max, apply_absolute_value=True))

    def chain_half(j, wbuf, wfree_t):
        lo = 64 * (j % 2)
        if wfree_t[0]:
            p.wait("vector", "t", wfree_t[0])
        if j % 2 == 0:
            vop(lambda E, wbuf=wbuf, lo=lo: E.tensor_tensor(
                out=wbuf[:, 0:2048].rearrange("p (b k) -> p b k", k=32),
                in0=pA[:, 0:2048].rearrange("p (b k) -> p b k", k=32),
                in1=iv_t[:, lo:lo + 64].unsqueeze(2).broadcast_to(
                    [128, 64, 32]),
                op=ALU.mult), drain=False)
        else:
            vop(lambda E, wbuf=wbuf, lo=lo: E.tensor_tensor(
                out=wbuf[:, 0:1024].rearrange("p (b k) -> p b k", k=32),
                in0=pB[:, 0:1024].rearrange("p (b k) -> p b k", k=32),
                in1=iv_t[:, lo:lo + 32].unsqueeze(2).broadcast_to(
                    [128, 32, 32]),
                op=ALU.mult), drain=False)
            vop(lambda E, wbuf=wbuf, lo=lo: E.tensor_tensor(
                out=wbuf[:, 1024:2048].rearrange("p (b k) -> p b k", k=32),
                in0=pC[:, 0:1024].rearrange("p (b k) -> p b k", k=32),
                in1=iv_t[:, lo + 32:lo + 64].unsqueeze(2).broadcast_to(
                    [128, 32, 32]),
                op=ALU.mult), drain=False)
        p.drain("vector")
        vop(lambda E, wbuf=wbuf: E.tensor_scalar(
            out=wbuf[:, 0:2048], in0=wbuf[:, 0:2048], scalar1=K1, scalar2=K1,
            op0=ALU.add, op1=ALU.subtract))
        vop(lambda E, wbuf=wbuf, lo=lo: E.tensor_tensor(
            out=wbuf[:, 0:2048].rearrange("p (b k) -> p b k", k=32),
            in0=wbuf[:, 0:2048].rearrange("p (b k) -> p b k", k=32),
            in1=sc_t[:, lo:lo + 64].unsqueeze(2).broadcast_to([128, 64, 32]),
            op=ALU.mult), drain=False)
        return p.tick("vector", "v")

    def tr_evac(j, half, wbuf, wfree_t, vq):
        p.wait("tensor", "v", vq)
        for i in range(16):
            if j % 2 == 0:
                tgt = pA[:, 128 * i:128 * (i + 1)]
            elif i < 8:
                tgt = pB[:, 128 * i:128 * (i + 1)]
            else:
                tgt = pC[:, 128 * (i - 8):128 * (i - 7)]
            p.op("tensor", lambda E, i=i, tgt=tgt, wbuf=wbuf:
                 E.transpose(tgt, wbuf[:, 128 * i:128 * (i + 1)], ident),
                 inc=("t", 1) if i == 15 else None)
        wfree_t[0] = p.cnt["t"]
        p.wait("scalar", "t", p.cnt["t"])
        if gqt_read_t[j % 2]:
            p.wait("scalar", "t", gqt_read_t[j % 2])
        gq = gqtS[j % 2]
        if j % 2 == 0:
            p.op("scalar", lambda E, gq=gq, half=half: E.activation(
                out=gq[:, 2048 * half:2048 * (half + 1)], in_=pA[:, 0:2048],
                func=AF.Copy), inc=("s", 1))
        else:
            p.op("scalar", lambda E, gq=gq, half=half: E.activation(
                out=gq[:, 2048 * half:2048 * half + 1024], in_=pB[:, 0:1024],
                func=AF.Copy), inc=("s", 1))
            p.op("scalar", lambda E, gq=gq, half=half: E.activation(
                out=gq[:, 2048 * half + 1024:2048 * (half + 1)],
                in_=pC[:, 0:1024], func=AF.Copy), inc=("s", 1))
        return p.cnt["s"]

    def proj(j, preg, gates):
        gq = gqtS[j % 2]
        for g_s in gates:
            if g_s:
                p.wait("tensor", "s", g_s)
        for jc in range(NJ):
            lhs = gq[:, 128 * jc:128 * (jc + 1)]
            for kk in range(2):
                mov = wpjq[:, D * jc + 512 * kk:D * jc + 512 * (kk + 1)]
                p.op("tensor", lambda E, l=lhs, m=mov, kk=kk, jc=jc,
                     preg=preg:
                     E.matmul(preg[:, 512 * kk:512 * (kk + 1)], l, m,
                              start=(jc == 0), stop=False))
        for kk in range(2):
            p.op("tensor", lambda E, kk=kk, preg=preg:
                 E.matmul(preg[:, 512 * kk:512 * (kk + 1)], ones1,
                          bpj_bf[0:1, 512 * kk:512 * (kk + 1)],
                          start=False, stop=True),
                 inc=("t", 1) if kk == 1 else None)
        gqt_read_t[j % 2] = p.cnt["t"]
        return p.cnt["t"]

    def postln(j, preg, tpj):
        b4 = j % 4
        p.wait("vector", "t", tpj)
        sv = ln_stats(preg[:, 0:512], preg[:, 512:1024])
        p.wait("scalar", "v", sv)
        p.wait("scalar", "t", w1_t_free[0])
        ns = normalize(tmpn, preg[:, 0:1024])
        p.wait("vector", "s", p.cnt["s"])
        vop(lambda E, b4=b4: E.tensor_tensor(out=x2tS[b4], in0=tmpn,
                                             in1=x2tS[b4], op=ALU.add),
            drain=False)
        vo = p.tick("vector", "v")
        p.wait("sync", "v", vo)
        x2t_free[b4] = dma(out_h[128 * j:128 * j + 128, :], x2tS[b4],
                           "ou%d" % b4)
        return ns

    uv = u2path(0)
    transp_u2(0, uv, pA, 0)
    uv = u2path(1)
    transp_u2(1, uv, pB, 0)

    for X in range(0, NT, 2):
        Y = X + 1
        tfaX = fc_half(X, 0, [u2qt_sfree[0], u2qt_sfree[1], normY_s[0]])
        tfaY = fc_half(Y, 0, [normY_s[0]])
        gelu_amax(X, tfaX)
        quant_scales(64, 0)
        vqAX = chain_half(X, w1, w1_t_free)
        gelu_amax(Y, tfaY)
        quant_scales(64, 64)
        vqAY = chain_half(Y, w2, w2_t_free)
        eAX = tr_evac(X, 0, w1, w1_t_free, vqAX)
        tfbX = fc_half(X, 1, [eAX])
        eAY = tr_evac(Y, 0, w2, w2_t_free, vqAY)
        tfbY = fc_half(Y, 1, [eAY])
        gelu_amax(X, tfbX)
        quant_scales(64, 0)
        vqBX = chain_half(X, w1, w1_t_free)
        gelu_amax(Y, tfbY)
        quant_scales(64, 64)
        vqBY = chain_half(Y, w2, w2_t_free)
        eBX = tr_evac(X, 1, w1, w1_t_free, vqBX)
        if X + 2 < NT:
            uv2 = u2path(X + 2)
        tpjX = proj(X, pA, [eBX])
        eBY = tr_evac(Y, 1, w2, w2_t_free, vqBY)
        if X + 2 < NT:
            transp_u2(X + 2, uv2, pB, eBY)
        nsX = postln(X, pA, tpjX)
        if X + 3 < NT:
            uv3 = u2path(X + 3)
        tpjY = proj(Y, pC, [eBY])
        if X + 3 < NT:
            transp_u2(X + 3, uv3, pA, nsX)
        normY_s[0] = postln(Y, pC, tpjY)

    for b4 in range(4):
        p.wait("sync", "ou%d" % b4, p.cnt["ou%d" % b4])

    with nc.Block() as blk:
        @blk.sync
        def _(E):
            for f in p.steps["sync"]:
                f(E)

        @blk.vector
        def _(E):
            for f in p.steps["vector"]:
                f(E)

        @blk.scalar
        def _(E):
            for f in p.steps["scalar"]:
                f(E)

        @blk.tensor
        def _(E):
            for f in p.steps["tensor"]:
                f(E)

        @blk.gpsimd
        def _(E):
            for f in p.steps["gpsimd"]:
                f(E)

    es.close()
    return nc


_NC_CACHE = None


def kernel(L0_attn_x, L0_attn_Wq, L0_attn_Wk, L0_attn_Wv, L0_attn_Wo,
           L0_attn_bo, L0_ffn_w_fc, L0_ffn_b_fc, L0_ffn_w_proj, L0_ffn_b_proj):
    global _NC_CACHE
    x = np.ascontiguousarray(np.asarray(L0_attn_x, dtype=np.float32))
    wq = np.ascontiguousarray(np.asarray(L0_attn_Wq, dtype=np.float32))
    wk = np.ascontiguousarray(np.asarray(L0_attn_Wk, dtype=np.float32))
    wv = np.ascontiguousarray(np.asarray(L0_attn_Wv, dtype=np.float32))
    wo = np.asarray(L0_attn_Wo, dtype=np.float32)
    bo = np.ascontiguousarray(
        np.asarray(L0_attn_bo, dtype=np.float32).reshape(1, D))
    wfc = np.ascontiguousarray(np.asarray(L0_ffn_w_fc, dtype=np.float32))
    bfc = np.ascontiguousarray(
        np.asarray(L0_ffn_b_fc, dtype=np.float32).reshape(1, J))
    wpj = np.ascontiguousarray(np.asarray(L0_ffn_w_proj, dtype=np.float32))
    bpj = np.ascontiguousarray(
        np.asarray(L0_ffn_b_proj, dtype=np.float32).reshape(1, D))
    wo8 = np.ascontiguousarray(
        wo.reshape(H, DH, D).transpose(1, 0, 2).reshape(DH, H * D))
    ident = np.eye(128, dtype=np.float32)
    consts = np.ascontiguousarray(np.tile(np.array(
        [LN2, -3.0 * LN2, -LN2, 3.0 * LN2, LN_EPS, -0.5, IL2_LO, FLOOR_DELTA],
        dtype=np.float32), (128, 1)))

    if _NC_CACHE is None:
        _NC_CACHE = build_nc()
    nc = _NC_CACHE

    in_maps = []
    for i in range(B):
        in_maps.append(dict(
            x=np.ascontiguousarray(x[i]), wq=wq, wk=wk, wv=wv, wo8=wo8, bo=bo,
            wfc=wfc, bfc=bfc, wpj=wpj, bpj=bpj, ident=ident, consts=consts))
    res = run_bass_kernel_spmd(nc, in_maps, core_ids=list(range(B)))
    out = np.stack([np.asarray(res.results[i]["out"]) for i in range(B)],
                   axis=0)
    return out


# version 18
# speedup vs baseline: 1.1710x; 1.0411x over previous
import math
from collections import Counter
from contextlib import ExitStack

import numpy as np
import ml_dtypes
import concourse.bass as bass
import concourse.mybir as mybir
from concourse.bass_utils import run_bass_kernel_spmd

F32 = mybir.dt.float32
BF16 = mybir.dt.bfloat16
U8 = mybir.dt.uint8
AF = mybir.ActivationFunctionType
ALU = mybir.AluOpType
AX = mybir.AxisListType

B, T, D, HID, H, DH, J = 8, 2048, 1024, 512, 8, 64, 4096
NT, KD, NHC, NJ = 16, 8, 4, 32

C23 = float(np.float32(1.5 * 2.0 ** 23))
IL2_HI = float(np.float32(1.0 / math.log(2.0)))
IL2_LO = float(np.float32(1.0 / math.log(2.0) - np.float32(1.0 / math.log(2.0))))
L2_HI = float(np.float32(math.log(2.0)))
L2_LO = float(np.float32(math.log(2.0) - np.float32(math.log(2.0))))
C52 = 1.5 * 2.0 ** 52
LN2_64 = float(math.log(2.0))
EPS40 = float(np.float32(2.0 ** -40))
T40 = float(np.float32(2.0 ** 40))
LN2 = float(np.float32(math.log(2.0)))
INVLN2 = float(np.float32(1.0 / math.log(2.0)))
FLOOR_DELTA = 4e-8
ATTN_SCALE = 0.125
LN_EPS = 1e-5
KB = 1024
F32SZ = 44 * KB
BFSZ = 160 * KB

ENGINES = ("sync", "vector", "scalar", "tensor", "gpsimd")
ESEM = {"sync": "d2", "vector": "v", "scalar": "s", "tensor": "t",
        "gpsimd": "g"}


class Plan:
    def __init__(self):
        self.steps = {e: [] for e in ENGINES}
        self.cnt = Counter()
        self.sems = {}
        self.dma_sems = set()

    def op(self, eng, f, inc=None):
        if inc is not None:
            self.cnt[inc[0]] += inc[1]

        def run(E, f=f, inc=inc, sems=self.sems):
            r = f(E)
            if inc is not None:
                r.then_inc(sems[inc[0]], inc[1])
        self.steps[eng].append(run)

    def wait(self, eng, sem, val=None):
        if val is None:
            val = self.cnt[sem]
        if val <= 0:
            return
        self.steps[eng].append(
            lambda E, s=sem, v=val, sems=self.sems: E.wait_ge(sems[s], v))

    def tick(self, eng, sem):
        self.cnt[sem] += 1

        def run(E, s=sem, sems=self.sems):
            E.drain()
            E.sem_inc(sems[s], 1)
        self.steps[eng].append(run)
        return self.cnt[sem]

    def drain(self, eng):
        self.steps[eng].append(lambda E: E.drain())

    def barrier(self):
        vals = {}
        for e in ("vector", "scalar", "tensor", "gpsimd"):
            vals[ESEM[e]] = self.tick(e, ESEM[e])
        for ds in sorted(self.dma_sems):
            self.wait("sync", ds, self.cnt[ds])
        vals["d2"] = self.tick("sync", "d2")
        for e in ENGINES:
            for s, v in vals.items():
                if s != ESEM[e]:
                    self.wait(e, s, v)


def build_nc():
    nc = bass.Bass("TRN2", target_bir_lowering=False, debug=False)
    p = Plan()

    x_h = nc.dram_tensor("x", [T, D], F32, kind="ExternalInput").ap()
    wq_h = nc.dram_tensor("wq", [D, HID], F32, kind="ExternalInput").ap()
    wk_h = nc.dram_tensor("wk", [D, HID], F32, kind="ExternalInput").ap()
    wv_h = nc.dram_tensor("wv", [D, HID], F32, kind="ExternalInput").ap()
    wo8_h = nc.dram_tensor("wo8", [DH, H * D], F32, kind="ExternalInput").ap()
    bo_h = nc.dram_tensor("bo", [1, D], F32, kind="ExternalInput").ap()
    wfc_h = nc.dram_tensor("wfc", [J, D], F32, kind="ExternalInput").ap()
    bfc_h = nc.dram_tensor("bfc", [1, J], F32, kind="ExternalInput").ap()
    wpj_h = nc.dram_tensor("wpj", [D, J], F32, kind="ExternalInput").ap()
    bpj_h = nc.dram_tensor("bpj", [1, D], F32, kind="ExternalInput").ap()
    ident_h = nc.dram_tensor("ident", [128, 128], F32, kind="ExternalInput").ap()
    consts_h = nc.dram_tensor("consts", [128, 8], F32, kind="ExternalInput").ap()
    out_h = nc.dram_tensor("out", [T, D], F32, kind="ExternalOutput").ap()
    x2s_h = nc.dram_tensor("x2s", [T, D], F32, kind="Internal").ap()

    es = ExitStack()
    arF = es.enter_context(nc.sbuf_tensor("arF", [128, F32SZ // 4], F32))
    arB = es.enter_context(nc.sbuf_tensor("arB", [128, BFSZ // 2], BF16))

    def V(off, parts, n, dtype=F32):
        isz = mybir.dt.size(dtype)
        nb = n * isz
        assert off % 64 == 0, off
        if dtype == F32:
            assert off + nb <= F32SZ, (off, nb)
            return arF[0:parts, off // 4:off // 4 + n]
        assert off + nb <= BFSZ, (off, nb)
        return arB[0:parts, off // 2:off // 2 + n]

    consts = V(0, 128, 8)
    det = V(64, 128, 1)
    det1m = V(128, 128, 1)
    K1 = V(192, 128, 1)
    ident = V(256, 128, 128)
    q_mx = V(768, 128, 8)
    q_sq = V(832, 128, 8)
    q_si = V(896, 128, 8)
    k_mx = V(960, 128, 1)
    k_nmx = V(1024, 128, 1)
    k_sk = V(1088, 128, 1)
    k_ski = V(1152, 128, 1)
    mv6 = V(1216, 128, 12)
    mv = V(1280, 128, 2)
    rstd = V(1344, 128, 1)
    negmr = V(1408, 128, 1)
    am = V(1472, 128, 128)
    am2 = V(1984, 128, 128)
    am3 = V(2496, 128, 128)
    sc_t = V(3008, 128, 128)
    iv_t = V(3520, 128, 128)
    K1S = V(4032, 128, 1)
    bo_bf = V(0, 1, D, BF16)
    bpj_bf = V(2048, 1, D, BF16)
    bfc_bf = V(4096, 1, J, BF16)
    ones1 = V(12288, 1, 128, BF16)
    c_sb = V(12608, 64, 512, BF16)

    pA = es.enter_context(nc.psum_tensor("pA", [128, 2048], F32))
    pB = es.enter_context(nc.psum_tensor("pB", [128, 1024], F32))
    pC = es.enter_context(nc.psum_tensor("pC", [128, 1024], F32))

    DMA_SEMS = ("c0", "bst", "st0", "st1", "st2", "xa0", "xa1",
                "xb0", "xb1", "ox0", "ox1", "sw0", "sw1", "sw2", "sw3", "sw4",
                "x20", "x21", "x22", "x23", "ou0", "ou1", "ou2", "ou3")
    for s in ("d2", "v", "s", "t", "g") + DMA_SEMS:
        p.sems[s] = es.enter_context(nc.semaphore(s))
    p.dma_sems.update(DMA_SEMS)

    def dma(dst, src, sem):
        p.op("sync", lambda E, d=dst, s=src: E.dma_start(out=d, in_=s),
             inc=(sem, 16))
        return p.cnt[sem]

    def gdma(dst, src, sem):
        p.op("gpsimd", lambda E, d=dst, s=src: E.dma_start(out=d, in_=s),
             inc=(sem, 16))
        return p.cnt[sem]

    def vop(f, drain=True):
        p.op("vector", f)
        if drain:
            p.drain("vector")

    smalls_free = [0]

    def ln_stats(src_a, src_b):
        vop(lambda E, a=src_a: E.bn_stats(out=mv6[:, 0:6], in_=a), drain=False)
        vop(lambda E, b2=src_b: E.bn_stats(out=mv6[:, 6:12], in_=b2))
        vop(lambda E: E.bn_aggr(out=mv, in_=mv6), drain=False)
        vt = p.tick("vector", "v")
        p.wait("scalar", "v", vt)
        p.op("scalar", lambda E: E.activation(out=rstd, in_=mv[:, 1:2],
                                              func=AF.Ln, bias=consts[:, 4:5]))
        p.drain("scalar")
        p.op("scalar", lambda E: E.activation(out=rstd, in_=rstd, func=AF.Exp,
                                              scale=consts[:, 5:6]),
             inc=("s", 1))
        p.wait("vector", "s", p.cnt["s"])
        if smalls_free[0]:
            p.wait("vector", "s", smalls_free[0])
        vop(lambda E: E.tensor_scalar(out=negmr, in0=mv[:, 0:1], scalar1=rstd,
                                      scalar2=-1.0, op0=ALU.mult,
                                      op1=ALU.mult), drain=False)
        return p.tick("vector", "v")

    def normalize(dst, src):
        p.op("scalar", lambda E, d=dst, s=src: E.activation(
            out=d, in_=s, func=AF.Identity, bias=negmr, scale=rstd),
            inc=("s", 1))
        smalls_free[0] = p.cnt["s"]
        return p.cnt["s"]

    def qs_a(nb, lo):
        vop(lambda E, nb=nb, lo=lo: E.tensor_scalar(
            out=am[:, lo:lo + nb], in0=am[:, lo:lo + nb], scalar1=2.0 ** -16,
            scalar2=None, op0=ALU.max), drain=False)
        vt = p.tick("vector", "v")
        p.wait("scalar", "v", vt)
        p.drain("scalar")
        p.op("scalar", lambda E, nb=nb, lo=lo: E.activation(
            out=am2[:, 0:nb], in_=am[:, lo:lo + nb], func=AF.Ln),
            inc=("s", 1))
        return p.cnt["s"]

    def qs_b(nb, lo, s_ln):
        p.wait("vector", "s", s_ln)
        vop(lambda E, nb=nb: E.tensor_scalar(out=am3[:, 0:nb],
                                             in0=am2[:, 0:nb],
                                             scalar1=consts[:, 6:7],
                                             scalar2=consts[:, 7:8],
                                             op0=ALU.mult, op1=ALU.add))
        vop(lambda E, nb=nb: E.tensor_scalar(out=am2[:, 0:nb],
                                             in0=am2[:, 0:nb],
                                             scalar1=IL2_HI, scalar2=3.5,
                                             op0=ALU.mult, op1=ALU.subtract))
        vop(lambda E, nb=nb: E.tensor_tensor(out=am2[:, 0:nb],
                                             in0=am2[:, 0:nb],
                                             in1=am3[:, 0:nb], op=ALU.add))
        vop(lambda E, nb=nb: E.tensor_scalar(out=am2[:, 0:nb],
                                             in0=am2[:, 0:nb], scalar1=K1,
                                             scalar2=K1, op0=ALU.add,
                                             op1=ALU.subtract))
        vop(lambda E, nb=nb: E.tensor_scalar(out=am2[:, 0:nb],
                                             in0=am2[:, 0:nb],
                                             scalar1=LN2_64, scalar2=None,
                                             op0=ALU.mult), drain=False)
        vt = p.tick("vector", "v")
        p.wait("scalar", "v", vt)
        p.op("scalar", lambda E, nb=nb, lo=lo: E.activation(
            out=sc_t[:, lo:lo + nb], in_=am2[:, 0:nb], func=AF.Exp),
            inc=("s", 1))
        return p.cnt["s"]

    def qs_c(nb, lo, s_exp):
        p.wait("vector", "s", s_exp)
        vop(lambda E, nb=nb, lo=lo: E.tensor_scalar(
            out=am2[:, 0:nb], in0=sc_t[:, lo:lo + nb], scalar1=K1S,
            scalar2=None, op0=ALU.mult))
        vop(lambda E, nb=nb, lo=lo: E.tensor_tensor(
            out=am3[:, 0:nb], in0=sc_t[:, lo:lo + nb], in1=am2[:, 0:nb],
            op=ALU.add))
        vop(lambda E, nb=nb, lo=lo: E.tensor_tensor(
            out=sc_t[:, lo:lo + nb], in0=am3[:, 0:nb], in1=am2[:, 0:nb],
            op=ALU.subtract))
        vop(lambda E, nb=nb, lo=lo: E.reciprocal(
            out=iv_t[:, lo:lo + nb], in_=sc_t[:, lo:lo + nb]), drain=False)
        return p.tick("vector", "v")

    def quant_scales(nb, lo=0, syncv=True):
        s_ln = qs_a(nb, lo)
        s_exp = qs_b(nb, lo, s_ln)
        vt2 = qs_c(nb, lo, s_exp)
        return ("v", vt2)

    def quant_chain(src3, nb, sc_lo, wbuf):
        w3 = wbuf.rearrange("p (b k) -> p b k", k=32)
        iv_b = iv_t[:, sc_lo:sc_lo + nb].unsqueeze(2).broadcast_to([128, nb, 32])
        sc_b = sc_t[:, sc_lo:sc_lo + nb].unsqueeze(2).broadcast_to([128, nb, 32])
        vop(lambda E, w3=w3, src3=src3, iv_b=iv_b: E.tensor_tensor(
            out=w3, in0=src3, in1=iv_b, op=ALU.mult))
        vop(lambda E, wbuf=wbuf: E.tensor_scalar(
            out=wbuf, in0=wbuf, scalar1=K1, scalar2=K1, op0=ALU.add,
            op1=ALU.subtract))
        vop(lambda E, w3=w3, sc_b=sc_b: E.tensor_tensor(
            out=w3, in0=w3, in1=sc_b, op=ALU.mult), drain=False)

    wq_sb = V(14 * KB, 128, KD * HID, BF16)
    wk_sb = V(22 * KB, 128, KD * HID, BF16)
    wv_sb = V(30 * KB, 128, KD * HID, BF16)
    eqtbA = V(38 * KB, 64, 4 * T, BF16)
    uT = V(54 * KB, 128, KD * T, BF16)
    eqtbB = V(86 * KB, 64, 4 * T, BF16)
    vb = V(102 * KB, 128, NT * HID, BF16)
    ektb = V(118 * KB, 128, NT * HID, BF16)
    wo_sb = V(144 * KB, 64, H * D, BF16)
    aob = V(102 * KB, 64, H * T, BF16)
    xa = [V(4 * KB + i * 4 * KB, 128, 1024) for i in range(2)]
    ut = [V(12 * KB + i * 4 * KB, 128, 1024) for i in range(2)]
    eqs = [V(20 * KB + i * 2 * KB, 128, 512) for i in range(2)]
    bias_stg = V(24 * KB, 1, J)
    ekw = V(4 * KB, 128, 2048)
    xb = [V(12 * KB + i * 4 * KB, 128, 1024) for i in range(2)]
    ox = [V(20 * KB + i * 4 * KB, 128, 1024) for i in range(2)]
    stg4 = [V(28 * KB + i * 4 * KB, 128, 1024) for i in range(2)]

    dma(consts, consts_h, "c0")
    d0 = dma(ident, ident_h, "c0")
    p.wait("vector", "c0", d0)
    vop(lambda E: E.memset(det, 1.0))
    vop(lambda E: E.tensor_scalar(out=det, in0=det, scalar1=EPS40, scalar2=1.0,
                                  op0=ALU.add, op1=ALU.subtract))
    vop(lambda E: E.tensor_scalar(out=det, in0=det, scalar1=T40, scalar2=None,
                                  op0=ALU.mult))
    vop(lambda E: E.tensor_scalar(out=det1m, in0=det, scalar1=-1.0, scalar2=1.0,
                                  op0=ALU.mult, op1=ALU.add))
    vop(lambda E: E.tensor_scalar(out=K1, in0=det1m, scalar1=C23, scalar2=None,
                                  op0=ALU.mult))
    vop(lambda E: E.scalar_tensor_tensor(out=K1, in0=det, scalar=C52, in1=K1,
                                         op0=ALU.mult, op1=ALU.add))
    vop(lambda E: E.tensor_scalar(out=K1S, in0=det1m, scalar1=C23,
                                  scalar2=None, op0=ALU.mult))
    vop(lambda E: E.memset(ones1, 1.0), drain=False)
    p.tick("vector", "v")

    def load_cast(dst_bf, src_h, parts, n):
        dv = dma(bias_stg[0:parts, 0:n], src_h, "bst")
        p.wait("scalar", "bst", dv)
        p.op("scalar", lambda E, d=dst_bf, s=bias_stg[0:parts, 0:n]:
             E.activation(out=d, in_=s, func=AF.Copy), inc=("s", 1))
        p.wait("sync", "s", p.cnt["s"])

    load_cast(bo_bf, bo_h, 1, D)
    load_cast(bpj_bf, bpj_h, 1, D)
    load_cast(bfc_bf, bfc_h, 1, J)

    si = [0]
    stg_free = [0, 0]
    bias_done = p.cnt["s"]
    p.wait("gpsimd", "s", bias_done)

    def stage_cast(src_ap, dst_bf, parts, n):
        b = si[0] % 2
        si[0] += 1
        if stg_free[b]:
            p.wait("gpsimd", "s", stg_free[b])
        dv = gdma(stg4[b][0:parts, 0:n], src_ap, "st%d" % b)
        p.wait("scalar", "st%d" % b, dv)
        p.op("scalar", lambda E, d=dst_bf, s=stg4[b][0:parts, 0:n]:
             E.activation(out=d, in_=s, func=AF.Copy), inc=("s", 1))
        stg_free[b] = p.cnt["s"]

    CAST_ITEMS = []
    for k in range(KD):
        CAST_ITEMS.append((wq_h[128 * k:128 * k + 128, :],
                           wq_sb[:, HID * k:HID * (k + 1)], 128, HID))
        CAST_ITEMS.append((wv_h[128 * k:128 * k + 128, :],
                           wv_sb[:, HID * k:HID * (k + 1)], 128, HID))
    for k in range(KD):
        CAST_ITEMS.append((wk_h[128 * k:128 * k + 128, :],
                           wk_sb[:, HID * k:HID * (k + 1)], 128, HID))
    for hh in range(H):
        CAST_ITEMS.append((wo8_h[:, D * hh:D * (hh + 1)],
                           wo_sb[0:64, D * hh:D * (hh + 1)], 64, 1024))
    qv_cast_s = [0]

    SH = 8
    xa_free = [0, 0]
    ut_free = [0, 0]
    pA_lo_free_s = [0]
    pA_free_list = []
    pq_free_s = [0]
    eqs_free_t = [0, 0]
    eqt_evac_s = [0]

    def ln_part(c):
        b = c % 2
        if xa_free[b]:
            p.wait("sync", "s", xa_free[b])
        dv = dma(xa[b], x_h[128 * c:128 * c + 128, :], "xa%d" % b)
        p.wait("vector", "xa%d" % b, dv)
        sv = ln_stats(xa[b][:, 0:512], xa[b][:, 512:1024])
        p.wait("scalar", "v", sv)
        if ut_free[b]:
            p.wait("scalar", "t", ut_free[b])
        xa_free[b] = normalize(ut[b], xa[b])
        p.wait("tensor", "s", p.cnt["s"])
        if pA_lo_free_s[0]:
            p.wait("tensor", "s", pA_lo_free_s[0])
        for k in range(KD):
            p.op("tensor", lambda E, k=k, b=b:
                 E.transpose(pA[:, 128 * k:128 * k + 128],
                             ut[b][:, 128 * k:128 * k + 128], ident),
                 inc=("t", 1) if k == KD - 1 else None)
        ut_free[b] = p.cnt["t"]
        p.wait("scalar", "t", p.cnt["t"])
        dst = uT[:, :].rearrange("p (k n) -> p k n", n=T)[:, :,
                                                          128 * c:128 * c + 128]
        p.op("scalar", lambda E, d=dst: E.activation(
            out=d, in_=pA[:, 0:1024].rearrange("p (k n) -> p k n", n=128),
            func=AF.Copy), inc=("s", 1))
        pA_lo_free_s[0] = p.cnt["s"]
        pA_free_list.append(p.cnt["s"])

    def qv_part(j):
        b = j % 2
        p.wait("tensor", "s", qv_cast_s[0])
        p.wait("tensor", "s", pA_free_list[j])
        if pq_free_s[0]:
            p.wait("tensor", "s", pq_free_s[0])
        for k in range(KD):
            lhs = uT[:, :].rearrange("p (k n) -> p k n", n=T)[
                :, k, 128 * j:128 * j + 128]
            p.op("tensor", lambda E, l=lhs, k=k:
                 E.matmul(pC[:, 0:512], l, wq_sb[:, HID * k:HID * (k + 1)],
                          start=(k == 0), stop=(k == KD - 1)))
            p.op("tensor", lambda E, l=lhs, k=k:
                 E.matmul(pC[:, 512:1024], l, wv_sb[:, HID * k:HID * (k + 1)],
                          start=(k == 0), stop=(k == KD - 1)),
                 inc=("t", 1) if k == KD - 1 else None)
        tq = p.cnt["t"]
        p.wait("scalar", "t", tq)
        p.op("scalar", lambda E, j=j:
             E.activation(out=vb[:, HID * j:HID * (j + 1)],
                          in_=pC[:, 512:1024], func=AF.Copy), inc=("s", 1))
        if eqs_free_t[b]:
            p.wait("scalar", "t", eqs_free_t[b])
        p.op("scalar", lambda E, b=b: E.activation(
            out=eqs[b], in_=pC[:, 0:512], func=AF.Exp), inc=("s", 1))
        pq_free_s[0] = p.cnt["s"]
        p.wait("vector", "s", pq_free_s[0])
        e3 = eqs[b].rearrange("p (h e) -> p h e", e=64)
        vop(lambda E, e3=e3: E.tensor_reduce(out=q_sq, in_=e3, axis=AX.X,
                                             op=ALU.add))
        vop(lambda E: E.reciprocal(out=q_si, in_=q_sq))
        vop(lambda E: E.tensor_scalar(out=q_si, in0=q_si, scalar1=ATTN_SCALE,
                                      scalar2=None, op0=ALU.mult))
        sib = q_si.unsqueeze(2).broadcast_to([128, 8, 64])
        vop(lambda E, e3=e3, sib=sib: E.tensor_tensor(
            out=e3, in0=e3, in1=sib, op=ALU.mult), drain=False)
        vq_t = p.tick("vector", "v")
        p.wait("tensor", "v", vq_t)
        if eqt_evac_s[0]:
            p.wait("tensor", "s", eqt_evac_s[0])
        for hh in range(H):
            src = eqs[b][:, 64 * hh:64 * (hh + 1)]
            p.op("tensor", lambda E, s=src, hh=hh:
                 E.transpose(pA[0:64, 1024 + 128 * hh:1024 + 128 * (hh + 1)],
                             s, ident),
                 inc=("t", 1) if hh == H - 1 else None)
        eqs_free_t[b] = p.cnt["t"]
        p.wait("scalar", "t", p.cnt["t"])
        dstA = eqtbA.rearrange("p (h n) -> p h n", n=T)[:, :,
                                                        128 * j:128 * j + 128]
        dstB = eqtbB.rearrange("p (h n) -> p h n", n=T)[:, :,
                                                        128 * j:128 * j + 128]
        p.op("scalar", lambda E, d=dstA: E.activation(
            out=d, in_=pA[0:64, 1024:1536].rearrange("p (h n) -> p h n",
                                                     n=128),
            func=AF.Copy), inc=("s", 1))
        p.op("scalar", lambda E, d=dstB: E.activation(
            out=d, in_=pA[0:64, 1536:2048].rearrange("p (h n) -> p h n",
                                                     n=128),
            func=AF.Copy), inc=("s", 1))
        eqt_evac_s[0] = p.cnt["s"]

    ci = [0]

    def emit_casts(n_):
        for _ in range(n_):
            if ci[0] < len(CAST_ITEMS):
                stage_cast(*CAST_ITEMS[ci[0]])
                ci[0] += 1
                if ci[0] == 16:
                    qv_cast_s[0] = p.cnt["s"]

    for c in range(NT):
        emit_casts(2)
        ln_part(c)
        if c >= SH:
            qv_part(c - SH)
    for j in range(NT - SH, NT):
        qv_part(j)

    pk_free_s = 0
    ekw_free_t = [0]
    ek_pbc_free_s = [0]
    for hc in range(NHC):
        if hc == 0:
            p.wait("tensor", "s", eqt_evac_s[0])
            p.wait("tensor", "s", pq_free_s[0])
        if pk_free_s:
            p.wait("tensor", "s", pk_free_s)
        for k in range(KD):
            lhs = wk_sb[:, HID * k + 128 * hc:HID * k + 128 * (hc + 1)]
            for ns in range(4):
                mov = uT[:, :].rearrange("p (k n) -> p k n", n=T)[
                    :, k, 512 * ns:512 * (ns + 1)]
                p.op("tensor", lambda E, l=lhs, m=mov, ns=ns, k=k:
                     E.matmul(pA[:, 512 * ns:512 * (ns + 1)], l, m,
                              start=(k == 0), stop=(k == KD - 1)),
                     inc=("t", 1) if (k == KD - 1 and ns == 3) else None)
        tk = p.cnt["t"]
        p.wait("scalar", "t", tk)
        if ekw_free_t[0]:
            p.wait("scalar", "t", ekw_free_t[0])
        p.op("scalar", lambda E: E.activation(
            out=ekw[:, 0:1024], in_=pA[:, 0:1024], func=AF.Exp,
            accum_out=k_sk))
        p.drain("scalar")
        p.op("scalar", lambda E: E.activation(
            out=ekw[:, 1024:2048], in_=pA[:, 1024:2048],
            func=AF.Exp, accum_out=k_ski), inc=("s", 1))
        pk_free_s = p.cnt["s"]
        p.wait("vector", "s", p.cnt["s"])
        vop(lambda E: E.tensor_tensor(out=k_sk, in0=k_sk, in1=k_ski,
                                      op=ALU.add))
        vop(lambda E: E.reciprocal(out=k_ski, in_=k_sk))
        vop(lambda E: E.tensor_scalar(
            out=ekw[:, 0:2048], in0=ekw[:, 0:2048],
            scalar1=k_ski, scalar2=None, op0=ALU.mult), drain=False)
        ekvt = p.tick("vector", "v")
        p.wait("tensor", "v", ekvt)
        if ek_pbc_free_s[0]:
            p.wait("tensor", "s", ek_pbc_free_s[0])
        for i in range(8):
            p.op("tensor", lambda E, i=i:
                 E.transpose(pB[:, 128 * i:128 * (i + 1)],
                             ekw[:, 128 * i:128 * (i + 1)], ident))
        for i in range(8):
            p.op("tensor", lambda E, i=i:
                 E.transpose(pC[:, 128 * i:128 * (i + 1)],
                             ekw[:, 1024 + 128 * i:1024 + 128 * (i + 1)],
                             ident),
                 inc=("t", 1) if i == 7 else None)
        ekw_free_t[0] = p.cnt["t"]
        p.wait("scalar", "t", p.cnt["t"])
        d1 = ektb.rearrange("p (c x) -> p c x", x=HID)[
            :, 0:8, 128 * hc:128 * (hc + 1)]
        d2 = ektb.rearrange("p (c x) -> p c x", x=HID)[
            :, 8:16, 128 * hc:128 * (hc + 1)]
        p.op("scalar", lambda E, d=d1: E.activation(
            out=d, in_=pB[:, 0:1024].rearrange("p (i n) -> p i n", n=128),
            func=AF.Copy), inc=("s", 1))
        p.op("scalar", lambda E, d=d2: E.activation(
            out=d, in_=pC[:, 0:1024].rearrange("p (i n) -> p i n", n=128),
            func=AF.Copy), inc=("s", 1))
        ek_pbc_free_s[0] = p.cnt["s"]

    p.barrier()

    for hh in range(H):
        for c in range(NT):
            lhs = ektb[:, HID * c + 64 * hh:HID * c + 64 * (hh + 1)]
            mov = vb[:, HID * c + 64 * hh:HID * c + 64 * (hh + 1)]
            p.op("tensor", lambda E, l=lhs, m=mov, c=c, hh=hh:
                 E.matmul(pA[0:64, 64 * hh:64 * (hh + 1)], l, m,
                          start=(c == 0), stop=(c == NT - 1)),
                 inc=("t", 1) if (c == NT - 1 and hh == H - 1) else None)
    p.wait("scalar", "t", p.cnt["t"])
    p.op("scalar", lambda E: E.activation(out=c_sb, in_=pA[0:64, 0:512],
                                          func=AF.Copy), inc=("s", 1))
    p.wait("tensor", "s", p.cnt["s"])
    p.wait("sync", "t", p.cnt["t"])
    po_free = [0, 0]
    pi = [0]
    ox_free = [0, 0]
    xb_free = [0, 0]
    pwo_free = [0, 0]
    aob_s = [0, 0, 0, 0]

    def out_fm_block(ns):
        for hh in range(H):
            if po_free[pi[0] % 2]:
                p.wait("tensor", "s", po_free[pi[0] % 2])
            lhs = c_sb[0:64, 64 * hh:64 * (hh + 1)]
            arr = eqtbA if hh < 4 else eqtbB
            mov = arr[0:64, T * (hh % 4) + 512 * ns:
                      T * (hh % 4) + 512 * (ns + 1)]
            reg = pB[0:64, 512 * (pi[0] % 2):512 * (pi[0] % 2) + 512]
            p.op("tensor", lambda E, l=lhs, m=mov, r=reg:
                 E.matmul(r, l, m, start=True, stop=True), inc=("t", 1))
            p.wait("scalar", "t", p.cnt["t"])
            p.op("scalar", lambda E, hh=hh, ns=ns, r=reg: E.activation(
                out=aob[0:64, T * hh + 512 * ns:T * hh + 512 * (ns + 1)],
                in_=r, func=AF.Copy), inc=("s", 1))
            po_free[pi[0] % 2] = p.cnt["s"]
            pi[0] += 1
        aob_s[ns] = p.cnt["s"]

    def p1e_chunk(c):
        b = c % 2
        wo_ps = pC if b == 0 else pA
        if xb_free[b]:
            p.wait("sync", "v", xb_free[b])
        dv = dma(xb[b], x_h[128 * c:128 * c + 128, :], "xb%d" % b)
        p.wait("tensor", "s", aob_s[c // 4])
        if pwo_free[b]:
            p.wait("tensor", "s", pwo_free[b])
        for hh in range(H):
            lhs = aob[0:64, T * hh + 128 * c:T * hh + 128 * (c + 1)]
            for kk in range(2):
                p.op("tensor", lambda E, l=lhs, hh=hh, kk=kk, w=wo_ps:
                     E.matmul(w[:, 512 * kk:512 * (kk + 1)], l,
                              wo_sb[0:64, D * hh + 512 * kk:
                                    D * hh + 512 * (kk + 1)],
                              start=(hh == 0), stop=False))
        for kk in range(2):
            p.op("tensor", lambda E, kk=kk, w=wo_ps:
                 E.matmul(w[:, 512 * kk:512 * (kk + 1)], ones1,
                          bo_bf[0:1, 512 * kk:512 * (kk + 1)],
                          start=False, stop=True),
                 inc=("t", 1) if kk == 1 else None)
        two = p.cnt["t"]
        p.wait("vector", "t", two)
        sv = ln_stats(wo_ps[:, 0:512], wo_ps[:, 512:1024])
        p.wait("scalar", "v", sv)
        if ox_free[b]:
            p.wait("scalar", "ox%d" % b, ox_free[b])
        pwo_free[b] = normalize(ox[b], wo_ps[:, 0:1024])
        p.wait("vector", "s", p.cnt["s"])
        p.wait("vector", "xb%d" % b, dv)
        vop(lambda E, b=b: E.tensor_tensor(out=ox[b], in0=ox[b], in1=xb[b],
                                           op=ALU.add), drain=False)
        vt = p.tick("vector", "v")
        xb_free[b] = vt
        p.wait("sync", "v", vt)
        ox_free[b] = dma(x2s_h[128 * c:128 * c + 128, :], ox[b], "ox%d" % b)

    for ns in range(4):
        out_fm_block(ns)
        for c in range(4 * ns, 4 * ns + 4):
            p1e_chunk(c)

    p.barrier()

    stgW = [V(4 * KB + 8 * KB * i, 128, 2048) for i in range(5)]
    wfcq = V(14 * KB, 128, KD * J, BF16)
    wpjq = V(78 * KB, 128, NJ * D, BF16)

    stgW_free = [0, 0, 0, 0, 0]
    pB_free2 = [None]

    WORK = [(wfc_h[128 * jc:128 * (jc + 1), :], 1024, 32,
             [wfcq[:, J * k + 128 * jc:J * k + 128 * (jc + 1)]
              for k in range(KD)]) for jc in range(NJ)]
    WORK += [(wpj_h[128 * (idx // 2):128 * (idx // 2) + 128,
                    2048 * (idx % 2):2048 * (idx % 2 + 1)], 2048, 64,
              [wpjq[:, D * (16 * (idx % 2) + t) + 128 * (idx // 2):
                    D * (16 * (idx % 2) + t) + 128 * (idx // 2) + 128]
               for t in range(16)]) for idx in range(16)]

    def w_load(i):
        src_h_ap, n, nb, _ = WORK[i]
        b = i % 5
        if stgW_free[b]:
            p.wait("gpsimd", "t", stgW_free[b])
        gdma(stgW[b][:, 0:n], src_h_ap, "sw%d" % b)

    def w_amax_pair(pair):
        tot = 0
        for j in (2 * pair, 2 * pair + 1):
            _, n, nb, _ = WORK[j]
            b = j % 5
            p.wait("vector", "sw%d" % b, p.cnt["sw%d" % b])
            s3 = stgW[b][:, 0:n].rearrange("p (b k) -> p b k", k=32)
            vop(lambda E, s3=s3, nb=nb, lo=tot: E.tensor_reduce(
                out=am[:, lo:lo + nb], in_=s3, axis=AX.X, op=ALU.max,
                apply_absolute_value=True), drain=(j % 2 == 1))
            tot += nb
        return tot, qs_a(tot, 0)

    def w_chain(i, lo):
        _, n, nb, dst_tiles = WORK[i]
        b = i % 5
        s3 = stgW[b][:, 0:n].rearrange("p (b k) -> p b k", k=32)
        quant_chain(s3, nb, lo, stgW[b][:, 0:n])
        return p.tick("vector", "v")

    def w_te(i, vq):
        _, n, nb, dst_tiles = WORK[i]
        b = i % 5
        ntile = n // 128
        for base in range(0, ntile, 8):
            cnt = min(8, ntile - base)
            p.wait("tensor", "v", vq)
            if pB_free2[0]:
                p.wait("tensor", pB_free2[0][0], pB_free2[0][1])
            for i2 in range(cnt):
                src2 = stgW[b][:, 128 * (base + i2):128 * (base + i2 + 1)]
                p.op("tensor", lambda E, s=src2, i=i2:
                     E.transpose(pB[:, 128 * i:128 * (i + 1)], s, ident),
                     inc=("t", 1) if i2 == cnt - 1 else None)
            stgW_free[b] = p.cnt["t"]
            p.wait("scalar", "t", p.cnt["t"])
            for i2 in range(cnt):
                p.op("scalar", lambda E, d=dst_tiles[base + i2], i=i2:
                     E.activation(out=d, in_=pB[:, 128 * i:128 * (i + 1)],
                                  func=AF.Copy),
                     inc=("s", 1) if i2 == cnt - 1 else None)
            pB_free2[0] = ("s", p.cnt["s"])

    NP = len(WORK) // 2
    for j in range(4):
        w_load(j)
    tot0, sln0 = w_amax_pair(0)
    sexp0 = qs_b(tot0, 0, sln0)
    qs_c(tot0, 0, sexp0)
    for P in range(NP):
        i0, i1 = 2 * P, 2 * P + 1
        nxt = P + 1 < NP
        if 2 * P + 4 < len(WORK):
            w_load(2 * P + 4)
        if nxt:
            tot1, sln1 = w_amax_pair(P + 1)
        vq0 = w_chain(i0, 0)
        vq1 = w_chain(i1, WORK[i0][2])
        if nxt:
            sexp1 = qs_b(tot1, 0, sln1)
        w_te(i0, vq0)
        if 2 * P + 5 < len(WORK):
            w_load(2 * P + 5)
        if nxt:
            qs_c(tot1, 0, sexp1)
        w_te(i1, vq1)

    p.barrier()

    x2tS = [V(4 * KB, 128, 1024), V(8 * KB, 128, 1024),
            V(32 * KB, 128, 1024), V(36 * KB, 128, 1024)]
    w1 = V(12 * KB, 128, 2048)
    w2 = V(20 * KB, 128, 2048)
    u2f = V(28 * KB, 128, 1024)
    tmpn = V(12 * KB, 128, 1024)
    u2qtS = [V(142 * KB, 128, 1024, BF16),
             V(0, 128, 1024, BF16)]
    gqtS = [V(144 * KB, 128, NJ * 128, BF16),
            V(152 * KB, 128, NJ * 128, BF16)]

    x2t_free = [0, 0, 0, 0]
    u2_t_free = [0]
    u2qt_sfree = [0, 0]
    w1_t_free = [0]
    w2_t_free = [0]
    gqt_read_t = [0, 0]
    normY_s = [0]

    def u2path(j):
        b4 = j % 4
        if x2t_free[b4]:
            p.wait("sync", "ou%d" % b4, x2t_free[b4])
        dv = dma(x2tS[b4], x2s_h[128 * j:128 * j + 128, :], "x2%d" % b4)
        p.wait("vector", "x2%d" % b4, dv)
        sv = ln_stats(x2tS[b4][:, 0:512], x2tS[b4][:, 512:1024])
        p.wait("scalar", "v", sv)
        if u2_t_free[0]:
            p.wait("scalar", "t", u2_t_free[0])
        normalize(u2f, x2tS[b4])
        p.wait("vector", "s", p.cnt["s"])
        u3 = u2f[:, :].rearrange("p (b k) -> p b k", k=32)
        vop(lambda E, u3=u3: E.tensor_reduce(out=am[:, 0:32], in_=u3,
                                             axis=AX.X, op=ALU.max,
                                             apply_absolute_value=True))
        quant_scales(32)
        quant_chain(u3, 32, 0, u2f)
        return p.tick("vector", "v")

    def transp_u2(j, u2v, preg, gate_s):
        p.wait("tensor", "v", u2v)
        if gate_s:
            p.wait("tensor", "s", gate_s)
        for k in range(KD):
            p.op("tensor", lambda E, k=k, preg=preg:
                 E.transpose(preg[:, 128 * k:128 * k + 128],
                             u2f[:, 128 * k:128 * k + 128], ident),
                 inc=("t", 1) if k == KD - 1 else None)
        u2_t_free[0] = p.cnt["t"]
        p.wait("scalar", "t", p.cnt["t"])
        p.op("scalar", lambda E, j=j, preg=preg: E.activation(
            out=u2qtS[j % 2], in_=preg[:, 0:1024], func=AF.Copy),
            inc=("s", 1))
        u2qt_sfree[j % 2] = p.cnt["s"]

    def fc_half(j, half, gates):
        u2l = u2qtS[j % 2]
        for g_s in gates:
            if g_s:
                p.wait("tensor", "s", g_s)
        for g in range(2):
            if j % 2 == 0:
                pz = pA[:, 1024 * g:1024 * (g + 1)]
            else:
                pz = (pB if g == 0 else pC)[:, 0:1024]
            base = 2048 * half + 1024 * g
            for k in range(KD):
                lhs = u2l[:, 128 * k:128 * (k + 1)]
                for js in range(2):
                    mov = wfcq[:, J * k + base + 512 * js:
                               J * k + base + 512 * (js + 1)]
                    p.op("tensor", lambda E, l=lhs, m=mov, pz=pz, js=js, k=k:
                         E.matmul(pz[:, 512 * js:512 * (js + 1)], l, m,
                                  start=(k == 0), stop=False))
            for js in range(2):
                mov = bfc_bf[0:1, base + 512 * js:base + 512 * (js + 1)]
                p.op("tensor", lambda E, m=mov, pz=pz, js=js:
                     E.matmul(pz[:, 512 * js:512 * (js + 1)], ones1, m,
                              start=False, stop=True),
                     inc=("t", 1) if js == 1 else None)
        return p.cnt["t"]

    def gelu_amax(j, tfc):
        lo = 64 * (j % 2)
        p.wait("scalar", "t", tfc)
        if j % 2 == 0:
            p.op("scalar", lambda E: E.activation(
                out=pA[:, 0:2048], in_=pA[:, 0:2048], func=AF.Gelu),
                inc=("s", 1))
            sg = p.cnt["s"]
            p.wait("vector", "s", sg)
            vop(lambda E, lo=lo: E.tensor_reduce(
                out=am[:, lo:lo + 64],
                in_=pA[:, 0:2048].rearrange("p (b k) -> p b k", k=32),
                axis=AX.X, op=ALU.max, apply_absolute_value=True))
        else:
            p.op("scalar", lambda E: E.activation(
                out=pB[:, 0:1024], in_=pB[:, 0:1024], func=AF.Gelu),
                inc=("s", 1))
            sgb = p.cnt["s"]
            p.op("scalar", lambda E: E.activation(
                out=pC[:, 0:1024], in_=pC[:, 0:1024], func=AF.Gelu),
                inc=("s", 1))
            sgc = p.cnt["s"]
            p.wait("vector", "s", sgb)
            vop(lambda E, lo=lo: E.tensor_reduce(
                out=am[:, lo:lo + 32],
                in_=pB[:, 0:1024].rearrange("p (b k) -> p b k", k=32),
                axis=AX.X, op=ALU.max, apply_absolute_value=True),
                drain=False)
            p.wait("vector", "s", sgc)
            vop(lambda E, lo=lo: E.tensor_reduce(
                out=am[:, lo + 32:lo + 64],
                in_=pC[:, 0:1024].rearrange("p (b k) -> p b k", k=32),
                axis=AX.X, op=ALU.max, apply_absolute_value=True))

    def chain_half(j, wbuf, wfree_t):
        lo = 64 * (j % 2)
        if wfree_t[0]:
            p.wait("vector", "t", wfree_t[0])
        if j % 2 == 0:
            vop(lambda E, wbuf=wbuf, lo=lo: E.tensor_tensor(
                out=wbuf[:, 0:2048].rearrange("p (b k) -> p b k", k=32),
                in0=pA[:, 0:2048].rearrange("p (b k) -> p b k", k=32),
                in1=iv_t[:, lo:lo + 64].unsqueeze(2).broadcast_to(
                    [128, 64, 32]),
                op=ALU.mult), drain=False)
        else:
            vop(lambda E, wbuf=wbuf, lo=lo: E.tensor_tensor(
                out=wbuf[:, 0:1024].rearrange("p (b k) -> p b k", k=32),
                in0=pB[:, 0:1024].rearrange("p (b k) -> p b k", k=32),
                in1=iv_t[:, lo:lo + 32].unsqueeze(2).broadcast_to(
                    [128, 32, 32]),
                op=ALU.mult), drain=False)
            vop(lambda E, wbuf=wbuf, lo=lo: E.tensor_tensor(
                out=wbuf[:, 1024:2048].rearrange("p (b k) -> p b k", k=32),
                in0=pC[:, 0:1024].rearrange("p (b k) -> p b k", k=32),
                in1=iv_t[:, lo + 32:lo + 64].unsqueeze(2).broadcast_to(
                    [128, 32, 32]),
                op=ALU.mult), drain=False)
        p.drain("vector")
        vop(lambda E, wbuf=wbuf: E.tensor_scalar(
            out=wbuf[:, 0:2048], in0=wbuf[:, 0:2048], scalar1=K1, scalar2=K1,
            op0=ALU.add, op1=ALU.subtract))
        vop(lambda E, wbuf=wbuf, lo=lo: E.tensor_tensor(
            out=wbuf[:, 0:2048].rearrange("p (b k) -> p b k", k=32),
            in0=wbuf[:, 0:2048].rearrange("p (b k) -> p b k", k=32),
            in1=sc_t[:, lo:lo + 64].unsqueeze(2).broadcast_to([128, 64, 32]),
            op=ALU.mult), drain=False)
        return p.tick("vector", "v")

    def tr_evac(j, half, wbuf, wfree_t, vq):
        p.wait("tensor", "v", vq)
        for i in range(16):
            if j % 2 == 0:
                tgt = pA[:, 128 * i:128 * (i + 1)]
            elif i < 8:
                tgt = pB[:, 128 * i:128 * (i + 1)]
            else:
                tgt = pC[:, 128 * (i - 8):128 * (i - 7)]
            p.op("tensor", lambda E, i=i, tgt=tgt, wbuf=wbuf:
                 E.transpose(tgt, wbuf[:, 128 * i:128 * (i + 1)], ident),
                 inc=("t", 1) if i == 15 else None)
        wfree_t[0] = p.cnt["t"]
        p.wait("scalar", "t", p.cnt["t"])
        if gqt_read_t[j % 2]:
            p.wait("scalar", "t", gqt_read_t[j % 2])
        gq = gqtS[j % 2]
        if j % 2 == 0:
            p.op("scalar", lambda E, gq=gq, half=half: E.activation(
                out=gq[:, 2048 * half:2048 * (half + 1)], in_=pA[:, 0:2048],
                func=AF.Copy), inc=("s", 1))
        else:
            p.op("scalar", lambda E, gq=gq, half=half: E.activation(
                out=gq[:, 2048 * half:2048 * half + 1024], in_=pB[:, 0:1024],
                func=AF.Copy), inc=("s", 1))
            p.op("scalar", lambda E, gq=gq, half=half: E.activation(
                out=gq[:, 2048 * half + 1024:2048 * (half + 1)],
                in_=pC[:, 0:1024], func=AF.Copy), inc=("s", 1))
        return p.cnt["s"]

    def proj(j, preg, gates):
        gq = gqtS[j % 2]
        for g_s in gates:
            if g_s:
                p.wait("tensor", "s", g_s)
        for jc in range(NJ):
            lhs = gq[:, 128 * jc:128 * (jc + 1)]
            for kk in range(2):
                mov = wpjq[:, D * jc + 512 * kk:D * jc + 512 * (kk + 1)]
                p.op("tensor", lambda E, l=lhs, m=mov, kk=kk, jc=jc,
                     preg=preg:
                     E.matmul(preg[:, 512 * kk:512 * (kk + 1)], l, m,
                              start=(jc == 0), stop=False))
        for kk in range(2):
            p.op("tensor", lambda E, kk=kk, preg=preg:
                 E.matmul(preg[:, 512 * kk:512 * (kk + 1)], ones1,
                          bpj_bf[0:1, 512 * kk:512 * (kk + 1)],
                          start=False, stop=True),
                 inc=("t", 1) if kk == 1 else None)
        gqt_read_t[j % 2] = p.cnt["t"]
        return p.cnt["t"]

    def postln(j, preg, tpj):
        b4 = j % 4
        p.wait("vector", "t", tpj)
        sv = ln_stats(preg[:, 0:512], preg[:, 512:1024])
        p.wait("scalar", "v", sv)
        p.wait("scalar", "t", w1_t_free[0])
        ns = normalize(tmpn, preg[:, 0:1024])
        p.wait("vector", "s", p.cnt["s"])
        vop(lambda E, b4=b4: E.tensor_tensor(out=x2tS[b4], in0=tmpn,
                                             in1=x2tS[b4], op=ALU.add),
            drain=False)
        vo = p.tick("vector", "v")
        p.wait("sync", "v", vo)
        x2t_free[b4] = dma(out_h[128 * j:128 * j + 128, :], x2tS[b4],
                           "ou%d" % b4)
        return ns

    uv = u2path(0)
    transp_u2(0, uv, pA, 0)
    uv = u2path(1)
    transp_u2(1, uv, pB, 0)

    for X in range(0, NT, 2):
        Y = X + 1
        tfaX = fc_half(X, 0, [u2qt_sfree[0], u2qt_sfree[1]])
        tfaY = fc_half(Y, 0, [normY_s[0]])
        gelu_amax(X, tfaX)
        quant_scales(64, 0)
        vqAX = chain_half(X, w1, w1_t_free)
        gelu_amax(Y, tfaY)
        quant_scales(64, 64)
        vqAY = chain_half(Y, w2, w2_t_free)
        eAX = tr_evac(X, 0, w1, w1_t_free, vqAX)
        tfbX = fc_half(X, 1, [eAX])
        eAY = tr_evac(Y, 0, w2, w2_t_free, vqAY)
        tfbY = fc_half(Y, 1, [eAY])
        gelu_amax(X, tfbX)
        quant_scales(64, 0)
        vqBX = chain_half(X, w1, w1_t_free)
        gelu_amax(Y, tfbY)
        quant_scales(64, 64)
        vqBY = chain_half(Y, w2, w2_t_free)
        eBX = tr_evac(X, 1, w1, w1_t_free, vqBX)
        if X + 2 < NT:
            uv2 = u2path(X + 2)
        tpjX = proj(X, pA, [eBX])
        eBY = tr_evac(Y, 1, w2, w2_t_free, vqBY)
        if X + 2 < NT:
            transp_u2(X + 2, uv2, pB, eBY)
        nsX = postln(X, pA, tpjX)
        if X + 3 < NT:
            uv3 = u2path(X + 3)
        tpjY = proj(Y, pC, [eBY])
        if X + 3 < NT:
            transp_u2(X + 3, uv3, pA, nsX)
        normY_s[0] = postln(Y, pC, tpjY)

    for b4 in range(4):
        p.wait("sync", "ou%d" % b4, p.cnt["ou%d" % b4])

    with nc.Block() as blk:
        @blk.sync
        def _(E):
            for f in p.steps["sync"]:
                f(E)

        @blk.vector
        def _(E):
            for f in p.steps["vector"]:
                f(E)

        @blk.scalar
        def _(E):
            for f in p.steps["scalar"]:
                f(E)

        @blk.tensor
        def _(E):
            for f in p.steps["tensor"]:
                f(E)

        @blk.gpsimd
        def _(E):
            for f in p.steps["gpsimd"]:
                f(E)

    es.close()
    return nc


_NC_CACHE = None


def kernel(L0_attn_x, L0_attn_Wq, L0_attn_Wk, L0_attn_Wv, L0_attn_Wo,
           L0_attn_bo, L0_ffn_w_fc, L0_ffn_b_fc, L0_ffn_w_proj, L0_ffn_b_proj):
    global _NC_CACHE
    x = np.ascontiguousarray(np.asarray(L0_attn_x, dtype=np.float32))
    wq = np.ascontiguousarray(np.asarray(L0_attn_Wq, dtype=np.float32))
    wk = np.ascontiguousarray(np.asarray(L0_attn_Wk, dtype=np.float32))
    wv = np.ascontiguousarray(np.asarray(L0_attn_Wv, dtype=np.float32))
    wo = np.asarray(L0_attn_Wo, dtype=np.float32)
    bo = np.ascontiguousarray(
        np.asarray(L0_attn_bo, dtype=np.float32).reshape(1, D))
    wfc = np.ascontiguousarray(np.asarray(L0_ffn_w_fc, dtype=np.float32))
    bfc = np.ascontiguousarray(
        np.asarray(L0_ffn_b_fc, dtype=np.float32).reshape(1, J))
    wpj = np.ascontiguousarray(np.asarray(L0_ffn_w_proj, dtype=np.float32))
    bpj = np.ascontiguousarray(
        np.asarray(L0_ffn_b_proj, dtype=np.float32).reshape(1, D))
    wo8 = np.ascontiguousarray(
        wo.reshape(H, DH, D).transpose(1, 0, 2).reshape(DH, H * D))
    ident = np.eye(128, dtype=np.float32)
    consts = np.ascontiguousarray(np.tile(np.array(
        [LN2, -3.0 * LN2, -LN2, 3.0 * LN2, LN_EPS, -0.5, IL2_LO, FLOOR_DELTA],
        dtype=np.float32), (128, 1)))

    if _NC_CACHE is None:
        _NC_CACHE = build_nc()
    nc = _NC_CACHE

    in_maps = []
    for i in range(B):
        in_maps.append(dict(
            x=np.ascontiguousarray(x[i]), wq=wq, wk=wk, wv=wv, wo8=wo8, bo=bo,
            wfc=wfc, bfc=bfc, wpj=wpj, bpj=bpj, ident=ident, consts=consts))
    res = run_bass_kernel_spmd(nc, in_maps, core_ids=list(range(B)))
    out = np.stack([np.asarray(res.results[i]["out"]) for i in range(B)],
                   axis=0)
    return out
